# version 27
# baseline (speedup 1.0000x reference)
import numpy as np
import ml_dtypes
from contextlib import ExitStack

import concourse.bass as bass
import concourse.mybir as mybir
from concourse.bass_utils import run_bass_kernel_spmd

F32 = mybir.dt.float32
BF16 = mybir.dt.bfloat16
AF = mybir.ActivationFunctionType
ALU = mybir.AluOpType
BF = ml_dtypes.bfloat16

S = 2048
D = 1024
NTC = 16
NFC = 8
NHP = 8
NQT = 4
QTW = 512
NKT = 16
GRPS = [(i, 1) for i in range(16)]
NSEQ = 2
EPS = 1e-5

ENGINES = ("tensor", "vector", "scalar", "sync", "gpsimd")


def build_program(sq, sk, sv, so):
    nc = bass.Bass("TRN2", target_bir_lowering=False)

    x_h = nc.dram_tensor("x", [NSEQ, S, D], F32, kind="ExternalInput")
    y_h = nc.dram_tensor("y", [NSEQ, S, D], F32, kind="ExternalOutput")
    wq_h = nc.dram_tensor("wq", [NHP, 128, NFC, 128], F32, kind="ExternalInput")
    wk_h = nc.dram_tensor("wk", [NHP, 128, NFC, 128], F32, kind="ExternalInput")
    wv_h = nc.dram_tensor("wv", [NHP, 128, NFC, 128], F32, kind="ExternalInput")
    wo_h = nc.dram_tensor("wo", [D, D], F32, kind="ExternalInput")
    bq_h = nc.dram_tensor("bq", [128, NHP], F32, kind="ExternalInput")
    bk_h = nc.dram_tensor("bk", [128, NHP], F32, kind="ExternalInput")
    bv_h = nc.dram_tensor("bv", [128, NHP], F32, kind="ExternalInput")
    bo_h = nc.dram_tensor("bo", [1, D], F32, kind="ExternalInput")
    win_h = nc.dram_tensor("win", [128, NFC, 64], F32, kind="ExternalInput")
    w2bd_h = nc.dram_tensor("w2bd", [128, NFC, 64], F32, kind="ExternalInput")
    e_h = nc.dram_tensor("em", [64, D], BF16, kind="ExternalInput")
    wout_h = nc.dram_tensor("wout", [64, D], F32, kind="ExternalInput")
    bout_h = nc.dram_tensor("bout", [1, D], F32, kind="ExternalInput")
    bin_h = nc.dram_tensor("binv", [64, 1], F32, kind="ExternalInput")
    b2_h = nc.dram_tensor("b2v", [64, 1], F32, kind="ExternalInput")
    w1il_h = nc.dram_tensor("w1il", [128, NFC], F32, kind="ExternalInput")
    b1il_h = nc.dram_tensor("b1il", [128, NFC], F32, kind="ExternalInput")
    ident_h = nc.dram_tensor("ident", [128, 128], BF16, kind="ExternalInput")
    ones1_h = nc.dram_tensor("ones1", [1, 128], BF16, kind="ExternalInput")
    recd_h = nc.dram_tensor("recd", [128, 256], BF16, kind="Internal")

    st = ExitStack()
    with st:
        sb = lambda n, sh, dt: st.enter_context(nc.sbuf_tensor(n, sh, dt))
        X = sb("X", [128, NTC * D], F32)
        fbA = sb("fbA", [128, NFC * S], BF16)
        fbB = sb("fbB", [128, NFC * S], BF16)
        UA = sb("UA", [128, 16448], BF16)
        xntm = sb("xntm", [128, 2 * D], BF16)
        wstage = sb("wstage", [128, 2 * D], F32)
        WB = sb("WB", [128, 8192], BF16)
        FU = sb("FU", [128, S], F32)
        FUb = FU.bitcast(BF16)
        CU = sb("CU", [128, S], BF16)
        sstage = CU.bitcast(F32)
        bnr = sb("bnr", [128, 192], F32)
        recb = bnr.bitcast(BF16)
        agt = sb("agt", [128, 96], F32)
        bqs = sb("bqs", [128, NHP], F32)
        bks = sb("bks", [128, NHP], F32)
        w1s = sb("w1s", [128, NFC], F32)
        b1s = sb("b1s", [128, NFC], F32)
        bins = sb("bins", [64, 1], F32)
        b2s = sb("b2s", [64, 1], F32)
        bvs = sb("bvs", [128, NHP], F32)
        bor = sb("bor", [1, D], BF16)
        identb = sb("identb", [128, 128], BF16)
        ones1b = sb("ones1b", [1, 128], BF16)

        psA = st.enter_context(nc.psum_tensor("psA", [128, 3584], F32))
        psT = st.enter_context(nc.psum_tensor("psT", [128, 512], F32))
        psT_bf = psT.bitcast(BF16)
        psA_bf = psA.bitcast(BF16)

        PAR = 6176
        QB = 0
        KB = 2048
        VHbase = 4096
        PTbase = 12352

        sem_names = ["vself", "aself", "gself",
                     "d_x", "d_w", "d_b", "d_y", "d_sum", "d_cb", "d_rc", "d_re",
                     "w0", "w1", "w2", "w3",
                     "x0", "x1", "x2", "x3", "y0", "y1", "y2", "y3",
                     "t_xnT", "t_qkv", "t_sc", "t_pvd", "t_wo", "t_ffn", "t_g",
                     "t_wout"]
        sems = {n: st.enter_context(nc.semaphore(n)) for n in sem_names}
        cnt = {n: 0 for n in sem_names}
        prog = {k: [] for k in ENGINES}

        def emit(eng, fn):
            prog[eng].append(fn)

        waited = {e: {} for e in ENGINES}

        def wait(eng, s, v):
            if v <= 0 or waited[eng].get(s, 0) >= v:
                return
            waited[eng][s] = v
            emit(eng, lambda e, _s=sems[s], _v=v: e.wait_ge(_s, _v))

        def op(eng, fn, inc=None):
            if inc is not None:
                cnt[inc] += 1

            def run(e, _fn=fn, _inc=inc):
                ins = _fn(e)
                if _inc is not None:
                    ins.then_inc(sems[_inc], 1)
            emit(eng, run)

        def dma(eng, fn, dsem):
            cnt[dsem] += 16

            def run(e, _fn=fn, _s=sems[dsem]):
                _fn(e).then_inc(_s, 16)
            emit(eng, run)

        def vop(fn):
            wait("vector", "vself", cnt["vself"])
            op("vector", fn, "vself")
            return cnt["vself"]

        def aop(fn):
            wait("scalar", "aself", cnt["aself"])
            op("scalar", fn, "aself")
            return cnt["aself"]

        wrun = [0]

        def wrun_next():
            wrun[0] += 1
            return "w%d" % (wrun[0] % 4)

        def gop(fn):
            wait("gpsimd", "gself", cnt["gself"])
            op("gpsimd", fn, "gself")
            return cnt["gself"]

        def vwait(s, v):
            wait("vector", s, v)

        def awaits(s, v):
            wait("scalar", s, v)

        def xcopy(tc, dst, srcp):
            if tc % 2 == 0:
                return ("aself", aop(lambda e: e.activation(dst, srcp,
                                                            AF.Identity)))
            return ("vself", vop(lambda e: e.tensor_copy(dst, srcp)))

        def xapply(tc, dst, srct, rcol, ncol):
            if tc % 2 == 0:
                return ("aself", aop(lambda e: e.activation(
                    dst, srct, AF.Identity, bias=ncol, scale=rcol)))
            return ("vself", vop(lambda e: e.tensor_scalar(
                dst, srct, rcol, ncol, ALU.mult, ALU.add)))

        def warm(n):
            for _ in range(n):
                op("tensor", lambda e: e.matmul(
                    psA[:, 2048:2176], identb[:], identb[:],
                    start=True, stop=True))

        def strided(t, off, dims):
            return bass.AP(t.ap().tensor, off, [t.ap().ap[0]] + dims)

        for dstt, srct in ((bqs, bq_h), (bks, bk_h), (bvs, bv_h),
                           (w1s, w1il_h),
                           (b1s, b1il_h), (bins, bin_h), (b2s, b2_h),
                           (identb, ident_h), (ones1b, ones1_h)):
            dma("sync", lambda e, a=dstt, b=srct: e.dma_start(a[:], b.ap()),
                "d_b")
        vsetup = 0
        for rsrc, rdst in ((bo_h, bor),):
            wait("sync", "vself", vsetup)
            dma("sync", lambda e, b=rsrc: e.dma_start(wstage[0:1, 0:D],
                                                      b.ap()), "d_b")
            vwait("d_b", cnt["d_b"])
            vsetup = vop(lambda e, d=rdst: e.tensor_copy(d[:],
                                                         wstage[0:1, 0:D]))
        vwait("d_b", cnt["d_b"])
        wait("tensor", "d_b", cnt["d_b"])

        Gg = [0]
        ctxa_pair = [0, 0]
        ctxe_pair = [0, 0]
        lastuse = [0, 0, 0, 0]

        def tb_view(b):
            if b == 3:
                return psT[:, 0:512], psT_bf[:, 0:1024]
            lo = 2048 + 512 * b
            return psA[:, lo:lo + 512], psA_bf[:, 2 * lo:2 * lo + 1024]

        def ctx_dst(cp, h):
            if cp == 0:
                c = 2048 + h * 512
                return psA[0:65, c:c + 512]
            return psA[0:65, 3072:3584] if h == 0 else psT[0:65, 0:512]

        def gen_stats_bn(src_tile, c, w=None):
            if w is not None:
                vwait(*w)
            vop(lambda e, c=c, t=src_tile: e.bn_stats(
                bnr[:, 12 * c:12 * c + 6], t[:, c * D:c * D + 512]))
            vop(lambda e, c=c, t=src_tile: e.bn_stats(
                bnr[:, 12 * c + 6:12 * c + 12],
                t[:, c * D + 512:c * D + 1024]))
            return vop(lambda e, c=c: e.bn_aggr(
                strided(agt, c, [[16, 2]]), bnr[:, 12 * c:12 * c + 12]))

        def gen_rstd_chunk(c, va):
            ve = vop(lambda e, c=c: e.tensor_scalar(
                agt[:, 32 + c:33 + c], agt[:, 16 + c:17 + c], EPS, None,
                ALU.add))
            awaits("vself", ve)
            aop(lambda e, c=c: e.activation(agt[:, 64 + c:65 + c],
                                            agt[:, 32 + c:33 + c],
                                            AF.Ln))
            ma = aop(lambda e, c=c: e.activation(agt[:, 64 + c:65 + c],
                                                 agt[:, 64 + c:65 + c],
                                                 AF.Exp, scale=-0.5))
            vwait("aself", ma)
            return vop(lambda e, c=c: e.tensor_scalar(
                agt[:, 80 + c:81 + c], agt[:, 64 + c:65 + c], -1.0,
                agt[:, c:c + 1], ALU.mult, ALU.mult))

        def gen_stats_agg():
            n = NTC
            MEAN, VARP = agt[:, 0:n], agt[:, 16:16 + n]
            mv = vop(lambda e: e.tensor_scalar(VARP, VARP, EPS, None,
                                               ALU.add))
            wait("scalar", "vself", mv)
            aop(lambda e: e.activation(agt[:, 64:64 + n], VARP, AF.Ln))
            ma = aop(lambda e: e.activation(agt[:, 64:64 + n],
                                            agt[:, 64:64 + n],
                                            AF.Exp, scale=-0.5))
            vwait("aself", ma)
            vop(lambda e: e.tensor_scalar(agt[:, 80:80 + n],
                                          agt[:, 64:64 + n],
                                          -1.0, None, ALU.mult))
            return vop(lambda e: e.tensor_tensor(agt[:, 80:80 + n], MEAN,
                                                 agt[:, 80:80 + n],
                                                 ALU.mult))

        def gen_stats(src_tile, chunk_waits=None):
            for c in range(NTC):
                gen_stats_bn(src_tile, c,
                             chunk_waits[c] if chunk_waits is not None
                             else None)
            return gen_stats_agg()

        def gen_ln_fb(fb, chunk_waits=None, rm3=None):
            wait("tensor", "aself", ctxa_pair[1])
            wait("tensor", "vself", ctxe_pair[1])
            tposted = []
            for tc in range(NTC):
                if rm3 is None:
                    va = gen_stats_bn(X, tc,
                                      chunk_waits[tc] if chunk_waits
                                      else None)
                    rm = gen_rstd_chunk(tc, va)
                else:
                    rm = rm3[tc]
                slot = (tc % 2) * D
                if chunk_waits is not None:
                    wait("scalar" if tc % 2 == 0 else "vector",
                         *chunk_waits[tc])
                if tc >= 2:
                    eng = "scalar" if tc % 2 == 0 else "vector"
                    wait(eng, "t_xnT", cnt["t_xnT"] - 1)
                if tc % 2 == 0:
                    awaits("vself", rm)
                mks, mk = xapply(tc, xntm[:, slot:slot + D],
                                 X[:, tc * D:tc * D + D],
                                 agt[:, 64 + tc:65 + tc],
                                 agt[:, 80 + tc:81 + tc])
                warm(3)
                wait("tensor", mks, mk)
                if tc >= 2:
                    wait("tensor", *tposted[tc - 2])
                pdst = psT_bf[:, 0:1024] if tc % 2 == 0 else psA_bf[:, 6144:7168]
                for fc in range(NFC):
                    op("tensor", lambda e, fc=fc, sl=slot, pd=pdst:
                       e.transpose(pd[:, fc * 128:fc * 128 + 128],
                                   xntm[:, sl + fc * 128:sl + fc * 128 + 128],
                                   identb[:]),
                       "t_xnT" if fc == NFC - 1 else None)
                wait("scalar" if tc % 2 == 0 else "vector",
                     "t_xnT", cnt["t_xnT"])
                tposted.append(xcopy(
                    tc, strided(fb, tc * 128, [[S, NFC], [1, 128]]),
                    pdst[:, 0:1024]))
            return (tposted[-2], tposted[-1])

        ysent = 0
        vstg_l = [0]
        wslot = [0]
        wfree = [None, None, None]
        for s in range(NSEQ):
            xr = x_h.rearrange("s (tc p) d -> s p tc d", p=128)
            for q in range(4):
                if s > 0:
                    wait("sync", "y%d" % q, 64 * s)
                dma("sync", lambda e, s=s, q=q: e.dma_start(
                    X[:, q * 4096:(q + 1) * 4096]
                    .rearrange("p (t d) -> p t d", d=D),
                    xr[s, :, q * 4:(q + 1) * 4, :]), "x%d" % q)

            xw = [("x%d" % (c // 4), 16 * (s + 1)) for c in range(NTC)]
            fbe_mark = gen_ln_fb(fbA, xw)

            fstate = {"k": 0, "marks": {}, "bank": {}}

            def stream_qkv_weights(hp, tq_need):
                if wfree[0] is None:
                    wfree[0] = wfree[1] = wfree[2] = vsetup
                wb = (hp % 2) * 3072
                rs = wrun_next()
                slots = [wstage[:, 0:D], wstage[:, D:2 * D], FU[:, 1024:2048]]
                wait("sync", "t_wout", cnt["t_wout"])
                for j, wh in enumerate((wq_h, wk_h, wv_h)):
                    wait("sync", "vself", wfree[j])
                    dma("sync", lambda e, wh=wh, hp=hp, st=slots[j]:
                        e.dma_start(st.rearrange("p (i o) -> p i o", o=128),
                                    wh.ap()[hp]), rs)
                vwait(rs, cnt[rs])
                vwait("t_qkv", tq_need)
                wcast = []
                for j, (wof, scale) in enumerate(((0, sq), (1024, sk),
                                                  (2048, sv))):
                    vstg_l[0] = vop(lambda e, c=wb + wof, sc=scale,
                                    st=slots[j]:
                                    e.tensor_scalar(WB[:, c:c + 1024], st,
                                                    sc, None, ALU.mult))
                    wfree[j] = vstg_l[0]
                    wcast.append(vstg_l[0])
                return wcast

            def make_qkv_fillers(hp, wcast, xnm):
                par = (hp % 2) * PAR
                wb = (hp % 2) * 3072
                groups = []
                vev = [0]
                for j, (wof, bvec, dcol) in enumerate(
                        ((0, bqs, QB), (1024, bks, KB))):
                    for q4 in range(4):
                        def emitA(tb, tbb, j=j, q4=q4, wof=wof, wb=wb,
                                  wc=wcast[j], xnm=xnm):
                            wait("tensor", "vself", wc)
                            for w_ in xnm:
                                wait("tensor", *w_)
                            for ic in range(4):
                                op("tensor",
                                   lambda e, ic=ic, q4=q4, tb=tb, c=wb + wof:
                                   e.matmul(tb[:, 0:512],
                                            WB[:, c + ic * 128:
                                               c + ic * 128 + 128],
                                            fbA[:, ic * S + q4 * 512:
                                                ic * S + q4 * 512 + 512],
                                            start=(ic == 0), stop=False))
                            return None
                        def emitB(tb, tbb, j=j, q4=q4, wof=wof, bvec=bvec,
                                  dcol=dcol, hp=hp, par=par, wb=wb):
                            for ic in range(4, NFC):
                                op("tensor",
                                   lambda e, ic=ic, q4=q4, tb=tb, c=wb + wof:
                                   e.matmul(tb[:, 0:512],
                                            WB[:, c + ic * 128:
                                               c + ic * 128 + 128],
                                            fbA[:, ic * S + q4 * 512:
                                                ic * S + q4 * 512 + 512],
                                            start=False, stop=(ic == NFC - 1)),
                                   "t_qkv" if ic == NFC - 1 else None)
                            vwait("t_qkv", cnt["t_qkv"])
                            return vop(
                                lambda e, q4=q4, tb=tb, bv=bvec, hp=hp,
                                dc=par + dcol:
                                e.tensor_scalar(
                                    UA[:, dc + q4 * 512:dc + q4 * 512 + 512],
                                    tb[:, 0:512], 1.0,
                                    bv[:, hp:hp + 1], ALU.mult, ALU.add))
                        groups.append((emitA, True))
                        groups.append((emitB, False))
                for q4 in range(4):
                    def emitA(tb, tbb, q4=q4, wb=wb, wc=wcast[2], xnm=xnm):
                        wait("tensor", "vself", wc)
                        for w_ in xnm:
                            wait("tensor", *w_)
                        for ic in range(4):
                            op("tensor", lambda e, ic=ic, q4=q4, tb=tb,
                               c=wb + 2048:
                               e.matmul(tb[:, 0:512],
                                        WB[:, c + ic * 128:c + ic * 128 + 128],
                                        fbA[:, ic * S + q4 * 512:
                                            ic * S + q4 * 512 + 512],
                                        start=(ic == 0), stop=False))
                        return None
                    def emitB(tb, tbb, q4=q4, hp=hp, wb=wb, vev=vev):
                        for ic in range(4, NFC):
                            op("tensor", lambda e, ic=ic, q4=q4, tb=tb,
                               c=wb + 2048:
                               e.matmul(tb[:, 0:512],
                                        WB[:, c + ic * 128:c + ic * 128 + 128],
                                        fbA[:, ic * S + q4 * 512:
                                            ic * S + q4 * 512 + 512],
                                        start=False, stop=(ic == NFC - 1)),
                               "t_qkv" if ic == NFC - 1 else None)
                        vwait("t_qkv", cnt["t_qkv"])
                        m = vop(lambda e, q4=q4, tb=tb, hp=hp, par=par:
                                e.tensor_scalar(
                                    UA[:, par + VHbase + q4 * 512:
                                       par + VHbase + q4 * 512 + 512],
                                    tb[:, 0:512], 1.0,
                                    bvs[:, hp:hp + 1], ALU.mult, ALU.add))
                        vev.append(m)
                        return m
                    groups.append((emitA, True))
                    groups.append((emitB, False))
                for th in (1, 0):
                    def emitT(tb, tbb, th=th, par=par, vev=vev):
                        wait("tensor", "vself", vev[-1])
                        for tl in range(8):
                            tc = th * 8 + tl
                            op("tensor", lambda e, tc=tc, tl=tl, tbb=tbb,
                               par=par:
                               e.transpose(tbb[:, tl * 128:tl * 128 + 128],
                                           UA[:, par + VHbase + tc * 128:
                                              par + VHbase + tc * 128 + 128],
                                           identb[:]),
                               "t_qkv" if tl == 7 else None)
                        vwait("t_qkv", cnt["t_qkv"])
                        m = vop(lambda e, th=th, tbb=tbb, par=par:
                                e.tensor_copy(
                                    strided(UA, par + VHbase + th * 8 * 130,
                                            [[130, 8], [65, 2], [1, 64]]),
                                    tbb[:, 0:1024]))
                        if th == 0:
                            m = vop(lambda e, par=par: e.memset(
                                strided(UA, par + VHbase + 64,
                                        [[130, 16], [65, 2], [1, 1]]), 1.0))
                        return m
                    groups.append((emitT, True))
                return groups

            def emit_filler(pend, fpair):
                if not pend:
                    return
                emitf, newb = pend.pop(0)
                if newb:
                    k = fstate["k"]
                    fstate["k"] += 1
                    if k >= 2 and (k - 2) in fstate["marks"]:
                        wait("tensor", "vself", fstate["marks"][k - 2])
                    b = 2 * fpair + (k % 2)
                    fstate["bank"][k] = b
                    wait("tensor", "aself", ctxa_pair[fpair])
                    wait("tensor", "vself", ctxe_pair[fpair])
                k = fstate["k"] - 1
                tb, tbb = tb_view(fstate["bank"][k])
                m = emitf(tb, tbb)
                if m is not None:
                    fstate["marks"][k] = m
                    fstate["last"] = m
                    lastuse[fstate["bank"][k]] = m

            vnorm_l = [0]
            vnorms = []
            ctxe_l = [0]
            ctxa_l = [0]
            fstate["last"] = 0
            eas = {}
            tq_hist = {}
            wcast0 = stream_qkv_weights(0, 0)
            pend0 = make_qkv_fillers(0, wcast0, fbe_mark)
            while pend0:
                emit_filler(pend0, 1)
            tq_hist[0] = cnt["t_qkv"]
            qkv_done = fstate["last"]
            wcast = stream_qkv_weights(1, 0)
            pend = make_qkv_fillers(1, wcast, fbe_mark)

            for hp in range(NHP):
                par = (hp % 2) * PAR

                wait("tensor", "vself", qkv_done)
                pendq = []

                def emit_pv(payload, par=par):
                    gq, k0, klen, pt, cp = payload
                    wait("tensor", "aself", eas[gq])
                    if k0 == 0:
                        wait("tensor", "aself", ctxa_pair[cp])
                        wait("tensor", "vself", ctxe_pair[cp])
                        wait("tensor", "vself", lastuse[2 * cp])
                        wait("tensor", "vself", lastuse[2 * cp + 1])
                    kt = k0
                    for h in range(2):
                        dst = ctx_dst(cp, h)
                        ph = pt + h * 512
                        vh0 = par + VHbase + kt * 130 + h * 65
                        op("tensor", lambda e, d=dst, ph=ph,
                           v0=vh0, kt=kt:
                           e.matmul(d,
                                    UA[:, v0:v0 + 65],
                                    UA[:, ph:ph + 512],
                                    start=(kt == 0), stop=(kt == NKT - 1)),
                           "t_pvd" if h == 1 else None)

                pend_evac = [None]

                def emit_evac():
                    ehp, eqb, first_of_hp = pend_evac[0]
                    pend_evac[0] = None
                    eqt = eqb // QTW
                    cp = eqt % 2
                    awaits("t_pvd", cnt["t_pvd"])
                    srcA = (psA[0:64, 2048:2560] if cp == 0
                            else psA[0:64, 3072:3584])
                    ctxa_pair[cp] = ctxa_l[0] = aop(
                        lambda e, hp=ehp, qb=eqb, s=srcA: e.activation(
                            fbB[0:64, hp * S + qb:hp * S + qb + QTW],
                            s, AF.Identity))
                    vwait("t_pvd", cnt["t_pvd"])
                    if first_of_hp:
                        vwait("d_cb", cnt["d_cb"])
                    srcB = (psA[0:64, 2560:3072] if cp == 0
                            else psT[0:64, 0:512])
                    vop(lambda e, qb=eqb, s=srcB: e.tensor_copy(
                        CU[0:64, qb:qb + QTW], s))
                    vwait("d_sum", cnt["d_sum"])
                    if cp == 0:
                        ctxe_l[0] = vop(lambda e: e.tensor_copy(
                            sstage[64:65, 0:1024], psA[64:65, 2048:3072]))
                    else:
                        vop(lambda e: e.tensor_copy(
                            sstage[64:65, 0:512], psA[64:65, 3072:3584]))
                        ctxe_l[0] = vop(lambda e: e.tensor_copy(
                            sstage[64:65, 512:1024], psT[64:65, 0:512]))
                    ctxe_pair[cp] = ctxe_l[0]
                    wait("gpsimd", "vself", ctxe_l[0])
                    for half in range(2):
                        pa = 2 * eqt + half
                        pb = 8 + 2 * eqt + half
                        dma("gpsimd", lambda e, p=pa, o=half * 256:
                            e.dma_start(FU[p:p + 1, 0:256],
                                        sstage[64:65, o:o + 256]), "d_sum")
                        dma("gpsimd", lambda e, p=pb, o=512 + half * 256:
                            e.dma_start(FU[p:p + 1, 0:256],
                                        sstage[64:65, o:o + 256]), "d_sum")

                for qt in range(NQT):
                    qb = qt * QTW
                    for (k0, klen) in GRPS:
                        g = Gg[0]
                        Gg[0] += 1
                        p = g % 2
                        wait("tensor", "aself", eas.get(g - 2, 0))
                        kt = k0
                        for h in range(2):
                            base = h * 64
                            sc = p * 1024 + h * 512
                            op("tensor", lambda e, kt=kt, b=base, sc=sc,
                               qb=qb, par=par:
                               e.matmul(psA[:, sc:sc + 512],
                                        UA[b:b + 64, par + KB + kt * 128:
                                           par + KB + kt * 128 + 128],
                                        UA[b:b + 64, par + QB + qb:
                                           par + QB + qb + QTW],
                                        start=True, stop=True),
                               "t_sc" if h == 1 else None)
                        if k0 == 2 and pend_evac[0] is not None:
                            emit_evac()
                        if len(pendq) >= 2:
                            emit_pv(pendq.pop(0))
                        if (qt * len(GRPS) + k0 >= 12
                                and (qt == 0 or k0 >= 3)):
                            emit_filler(pend, 1 - qt % 2)
                        wait("scalar", "t_sc", cnt["t_sc"])
                        if g >= 4:
                            wait("scalar", "t_pvd", g - 3)
                        pt = PTbase + (g % 4) * 1024
                        eas[g] = aop(lambda e, pt=pt, p=p: e.activation(
                            UA[:, pt:pt + 1024], psA[:, p * 1024:p * 1024 + 1024],
                            AF.Exp, scale=0.125))
                        pendq.append((g, k0, klen, pt, qt % 2))
                    if pend and not pend[0][1]:
                        emit_filler(pend, 1 - qt % 2)
                    if qt < NQT - 1:
                        pend_evac[0] = (hp, qb, hp > 0 and qt == 0)
                    else:
                        wait("tensor", "aself", cnt["aself"])
                        while pendq:
                            emit_pv(pendq.pop(0))
                        pend_evac[0] = (hp, qb, False)
                        emit_evac()
                while pend:
                    emit_filler(pend, 1)
                if hp + 1 < NHP:
                    tq_hist[hp + 1] = cnt["t_qkv"]
                    qkv_done = fstate["last"]
                if hp + 2 < NHP:
                    wcast = stream_qkv_weights(hp + 2, tq_hist[hp])
                    pend = make_qkv_fillers(hp + 2, wcast, fbe_mark)
                else:
                    pend = []
                if hp == 6:
                    tq_all = cnt["t_qkv"]
                    for icp in range(4):
                        rs = wrun_next()
                        for sl in range(2):
                            ic = icp * 2 + sl
                            wait("sync", "vself", wfree[sl])
                            dma("sync", lambda e, ic=ic, sl=sl: e.dma_start(
                                wstage[:, sl * D:sl * D + D],
                                wo_h.ap()[ic * 128:ic * 128 + 128, :]), rs)
                        vwait(rs, cnt[rs])
                        vwait("t_qkv", tq_all)
                        for sl in range(2):
                            ic = icp * 2 + sl
                            vstg_l[0] = vop(
                                lambda e, ic=ic, sl=sl: e.tensor_scalar(
                                    WB[:, ic * 1024:ic * 1024 + 1024],
                                    wstage[:, sl * D:sl * D + D], so, None,
                                    ALU.mult))
                            wfree[sl] = vstg_l[0]


                wait("gpsimd", "vself", ctxe_l[0])
                dma("gpsimd", lambda e, hp=hp: e.dma_start(
                    fbB[64:128, hp * S:hp * S + S], CU[0:64, :]), "d_cb")

                vwait("d_sum", cnt["d_sum"])

                def _recip(e):
                    with nc.allow_low_precision(
                            reason="softmax 1/sum consumed as bf16 scale"):
                        return e.reciprocal(recb[0:16, 0:256], FU[0:16, 0:256])
                vrec = vop(_recip)
                wait("gpsimd", "vself", vrec)
                dma("gpsimd", lambda e, hp=hp: e.dma_start(
                    recd_h.ap()[16 * hp:16 * hp + 16, :],
                    recb[0:16, 0:256]), "d_rc")
                wait("gpsimd", "d_rc", cnt["d_rc"])
                if hp > 0:
                    wait("gpsimd", "vself", vnorm_l[0])
                for h in range(2):
                    rsrc = bass.AP(recd_h.ap().tensor,
                                   (16 * hp + 8 * h) * 256,
                                   [[0, 64], [256, 8], [1, 256]])
                    dma("gpsimd", lambda e, h=h, rsrc=rsrc: e.dma_start(
                        xntm[h * 64:h * 64 + 64, :]
                        .rearrange("p (b j) -> p b j", j=256), rsrc), "d_re")
                vwait("d_re", cnt["d_re"])
                vwait("d_cb", cnt["d_cb"])
                vwait("aself", ctxa_l[0])
                vnorm_l[0] = vop(lambda e, hp=hp: e.tensor_tensor(
                    fbB[:, hp * S:hp * S + S], fbB[:, hp * S:hp * S + S],
                    xntm[:, :], ALU.mult))
                vnorms.append(vnorm_l[0])
            vnorm = vnorm_l[0]

            wait("tensor", "vself", vstg_l[0])
            vwait("d_rc", cnt["d_rc"])
            woe, adds, rm3 = [], [], {}
            for tc in range(NTC):
                bank = (tc % 2) * 1024
                if tc >= 2:
                    wait("tensor", *woe[tc - 2])
                for half in range(2):
                    for ic in range(NFC):
                        if tc == 0 and half == 0:
                            wait("tensor", "vself", vnorms[ic])
                        op("tensor", lambda e, tc=tc, ic=ic, h=half, b=bank:
                           e.matmul(psA[:, b + h * 512:b + h * 512 + 512],
                                    fbB[:, ic * S + tc * 128:
                                        ic * S + tc * 128 + 128],
                                    WB[:, ic * 1024 + h * 512:
                                       ic * 1024 + h * 512 + 512],
                                    start=(ic == 0), stop=False))
                    op("tensor", lambda e, h=half, b=bank:
                       e.matmul(psA[:, b + h * 512:b + h * 512 + 512],
                                ones1b[:], bor[0:1, h * 512:h * 512 + 512],
                                start=False, stop=True),
                       "t_wo" if half == 1 else None)
                wait("scalar" if tc % 2 == 0 else "vector",
                     "t_wo", cnt["t_wo"])
                woe.append(xcopy(tc, fbA[:, tc * 1024:tc * 1024 + 1024],
                                 psA[:, bank:bank + 1024]))
                va = gen_stats_bn(fbA, tc, woe[tc])
                rm = gen_rstd_chunk(tc, va)
                sl = (tc % 2) * D
                warm(3)
                if tc >= 2:
                    eng = "scalar" if tc % 2 == 0 else "vector"
                    wait(eng, "vself", adds[tc - 2])
                if tc % 2 == 0:
                    awaits("vself", rm)
                mks, mk = xapply(tc, xntm[:, sl:sl + D],
                                 fbA[:, tc * D:tc * D + D],
                                 agt[:, 64 + tc:65 + tc],
                                 agt[:, 80 + tc:81 + tc])
                vwait(mks, mk)
                adds.append(vop(lambda e, tc=tc, sl=sl: e.tensor_tensor(
                    X[:, tc * D:tc * D + D], X[:, tc * D:tc * D + D],
                    xntm[:, sl:sl + D], ALU.add)))
                va3 = gen_stats_bn(X, tc)
                rm3[tc] = gen_rstd_chunk(tc, va3)
            wait("sync", "t_wo", cnt["t_wo"])
            rs_e = wrun_next()
            dma("sync", lambda e: e.dma_start(WB[0:64, 1024:2048], e_h.ap()),
                rs_e)
            tw_all = cnt["t_wo"]
            ffw = [(win_h, slice(0, 512), 64, 512, None),
                   (w2bd_h, slice(512, 1024), 64, 512, None),
                   (wout_h, None, None, None, "wout"),
                   (bout_h, None, None, None, "bout")]
            for srct, cols, oo, ww, special in ffw:
                sl = wslot[0] % 2
                wslot[0] += 1
                rs = wrun_next()
                wait("sync", "vself", wfree[sl])
                if special is None:
                    dma("sync", lambda e, srct=srct, sl=sl: e.dma_start(
                        wstage[:, sl * D:sl * D + 512]
                        .rearrange("p (i o) -> p i o", o=64),
                        srct.ap()), rs)
                elif special == "wout":
                    dma("sync", lambda e, sl=sl: e.dma_start(
                        wstage[0:64, sl * D:sl * D + D], wout_h.ap()), rs)
                else:
                    dma("sync", lambda e, sl=sl: e.dma_start(
                        wstage[0:1, sl * D:sl * D + D], bout_h.ap()), rs)
                vwait(rs, cnt[rs])
                vwait("t_wo", tw_all)
                if special is None:
                    vstg_l[0] = vop(lambda e, c=cols, sl=sl: e.tensor_copy(
                        WB[:, c], wstage[:, sl * D:sl * D + 512]))
                elif special == "wout":
                    vstg_l[0] = vop(lambda e, sl=sl: e.tensor_copy(
                        WB[0:64, 2048:3072], wstage[0:64, sl * D:sl * D + D]))
                else:
                    vstg_l[0] = vop(lambda e, sl=sl: e.tensor_copy(
                        WB[64:65, 2048:3072], wstage[0:1, sl * D:sl * D + D]))
                wfree[sl] = vstg_l[0]

            fbe_mark = gen_ln_fb(fbB, rm3=rm3)

            wait("sync", "t_wo", cnt["t_wo"])
            wait("tensor", "vself", vstg_l[0])
            for w_ in fbe_mark:
                wait("tensor", *w_)
            wait("tensor", rs_e, cnt[rs_e])
            sfbe = []
            for q4 in range(4):
                if q4 >= 1:
                    wait("tensor", "vself", sfbe[-1])
                for ic in range(NFC):
                    op("tensor", lambda e, ic=ic, q4=q4:
                       e.matmul(psT[0:64, 0:512],
                                WB[:, ic * 64:ic * 64 + 64],
                                fbB[:, ic * S + q4 * 512:
                                    ic * S + q4 * 512 + 512],
                                start=(ic == 0), stop=(ic == NFC - 1)),
                       "t_ffn" if ic == NFC - 1 else None)
                vwait("t_ffn", cnt["t_ffn"])
                sfbe.append(vop(lambda e, q4=q4: e.tensor_scalar(
                    FUb[0:64, q4 * 512:q4 * 512 + 512], psT[0:64, 0:512],
                    1.0, bins[0:64, 0:1], ALU.mult, ALU.add)))
            wait("tensor", "vself", sfbe[-1])
            for fc in range(NFC):
                if fc >= 1:
                    wait("tensor", "aself", cnt["aself"])
                for q4 in range(4):
                    op("tensor", lambda e, fc=fc, q4=q4:
                       e.matmul(psA[:, q4 * 512:q4 * 512 + 512],
                                WB[0:64, 1024 + fc * 128:1024 + fc * 128 + 128],
                                FUb[0:64, q4 * 512:q4 * 512 + 512],
                                start=True, stop=True),
                       "t_ffn" if q4 == 3 else None)
                wait("scalar", "t_ffn", cnt["t_ffn"])
                aop(lambda e, fc=fc: e.activation(
                    fbA[:, fc * S:fc * S + S], psA[:, 0:2048], AF.Relu,
                    bias=b1s[:, fc:fc + 1], scale=w1s[:, fc:fc + 1]))
            wait("tensor", "aself", cnt["aself"])
            gfbe = []
            for q4 in range(4):
                if q4 >= 1:
                    wait("tensor", "vself", gfbe[-1])
                for fc in range(NFC):
                    op("tensor", lambda e, fc=fc, q4=q4:
                       e.matmul(psT[0:64, 0:512],
                                WB[:, 512 + fc * 64:512 + fc * 64 + 64],
                                fbA[:, fc * S + q4 * 512:
                                    fc * S + q4 * 512 + 512],
                                start=(fc == 0), stop=(fc == NFC - 1)),
                       "t_g" if fc == NFC - 1 else None)
                vwait("t_g", cnt["t_g"])
                gfbe.append(vop(lambda e, q4=q4: e.tensor_scalar(
                    FUb[0:64, S + q4 * 512:S + q4 * 512 + 512],
                    psT[0:64, 0:512],
                    1.0, b2s[0:64, 0:1], ALU.mult, ALU.add)))
            gfbe.append(vop(lambda e: e.memset(FUb[64:65, S:2 * S], 1.0)))
            wait("tensor", "vself", gfbe[-1])
            woe, resid = [], []
            for tc in range(NTC):
                bank = (tc % 2) * 1024
                if tc >= 2:
                    wait("tensor", *woe[tc - 2])
                for half in range(2):
                    op("tensor", lambda e, tc=tc, h=half, b=bank:
                       e.matmul(psA[:, b + h * 512:b + h * 512 + 512],
                                FUb[0:65, S + tc * 128:S + tc * 128 + 128],
                                WB[0:65, 2048 + h * 512:2048 + h * 512 + 512],
                                start=True, stop=True),
                       "t_wout" if half == 1 else None)
                wait("scalar" if tc % 2 == 0 else "vector",
                     "t_wout", cnt["t_wout"])
                woe.append(xcopy(tc, UA[:, tc * 1024:tc * 1024 + 1024],
                                 psA[:, bank:bank + 1024]))
                va = gen_stats_bn(UA, tc, woe[tc])
                rm = gen_rstd_chunk(tc, va)
                sl = (tc % 2) * D
                warm(3)
                if tc >= 2:
                    eng = "scalar" if tc % 2 == 0 else "vector"
                    wait(eng, "vself", resid[tc - 2])
                if tc % 2 == 0:
                    awaits("vself", rm)
                mks, mk = xapply(tc, xntm[:, sl:sl + D],
                                 UA[:, tc * D:tc * D + D],
                                 agt[:, 64 + tc:65 + tc],
                                 agt[:, 80 + tc:81 + tc])
                vwait(mks, mk)
                resid.append(vop(lambda e, tc=tc, sl=sl: e.tensor_tensor(
                    X[:, tc * D:tc * D + D], X[:, tc * D:tc * D + D],
                    xntm[:, sl:sl + D], ALU.add)))

            for tc in range(NTC):
                wait("sync", "vself", resid[tc])
                dma("sync", lambda e, s=s, tc=tc: e.dma_start(
                    y_h.ap()[s, tc * 128:tc * 128 + 128, :],
                    X[:, tc * D:tc * D + D]), "y%d" % (tc // 4))
            ysent += NTC

        for q in range(4):
            wait("sync", "y%d" % q, cnt["y%d" % q])

        blk = st.enter_context(nc.Block())

        @blk.tensor
        def _(e):
            for f in prog["tensor"]:
                f(e)

        @blk.vector
        def _(e):
            for f in prog["vector"]:
                f(e)

        @blk.scalar
        def _(e):
            for f in prog["scalar"]:
                f(e)

        @blk.sync
        def _(e):
            for f in prog["sync"]:
                f(e)

        @blk.gpsimd
        def _(e):
            for f in prog["gpsimd"]:
                f(e)

    return nc


def kernel(**inputs):
    g = lambda k: np.asarray(inputs[k])
    x = g("L0_attn_x")
    sq = float(np.asarray(inputs["L0_attn_s_q"]))
    sk = float(np.asarray(inputs["L0_attn_s_k"]))
    sv = float(np.asarray(inputs["L0_attn_s_v"]))
    so = float(np.asarray(inputs["L0_attn_s_o"]))

    def wlayout(w):
        wT = np.ascontiguousarray(w.T)
        return np.ascontiguousarray(
            wT.reshape(NFC, 128, NHP, 128).transpose(2, 1, 0, 3))

    wq = wlayout(g("L0_attn_w_q"))
    wk = wlayout(g("L0_attn_w_k"))
    wv = wlayout(g("L0_attn_w_v"))
    wo = np.ascontiguousarray(g("L0_attn_w_o").T)
    bq = np.ascontiguousarray(g("L0_attn_b_q").reshape(NHP, 128).T)
    bk = np.ascontiguousarray(g("L0_attn_b_k").reshape(NHP, 128).T)
    bv = np.ascontiguousarray(g("L0_attn_b_v").reshape(NHP, 128).T)
    bo = g("L0_attn_b_o").reshape(1, D)
    Win = g("L0_ffn_Win")
    win = np.ascontiguousarray(Win.reshape(NFC, 128, 64).transpose(1, 0, 2))
    W2 = g("L0_ffn_W2")
    w2bd_full = np.zeros((D, 64), np.float32)
    for m in range(64):
        w2bd_full[m * 16:(m + 1) * 16, m] = W2[m]
    w2bd = np.ascontiguousarray(
        w2bd_full.reshape(NFC, 128, 64).transpose(1, 0, 2))
    em = np.zeros((64, D), np.float32)
    for m in range(64):
        em[m, m * 16:(m + 1) * 16] = 1.0
    em = em.astype(BF)
    wout = np.ascontiguousarray(g("L0_ffn_Wout"))
    bout = g("L0_ffn_bout").reshape(1, D)
    binv = g("L0_ffn_bin_").reshape(64, 1)
    b2v = g("L0_ffn_b2").reshape(64, 1)
    w1il = np.ascontiguousarray(g("L0_ffn_W1").reshape(NFC, 128).T)
    b1il = np.ascontiguousarray(g("L0_ffn_b1").reshape(NFC, 128).T)
    ident = np.eye(128, dtype=np.float32).astype(BF)
    ones1 = np.ones((1, 128), np.float32).astype(BF)

    nc = build_program(sq, sk, sv, so)
    shared = dict(wq=wq, wk=wk, wv=wv, wo=wo, bq=bq, bk=bk, bv=bv, bo=bo,
                  win=win, w2bd=w2bd, em=em, wout=wout, bout=bout, binv=binv,
                  b2v=b2v, w1il=w1il, b1il=b1il, ident=ident, ones1=ones1)
    in_maps = []
    for c in range(8):
        m = dict(shared)
        m["x"] = np.ascontiguousarray(x[2 * c:2 * c + 2])
        in_maps.append(m)
    res = run_bass_kernel_spmd(nc, in_maps, core_ids=list(range(8)))
    out = np.empty((16, S, D), np.float32)
    for c in range(8):
        out[2 * c:2 * c + 2] = res.results[c]["y"]
    return out


# version 28
# speedup vs baseline: 1.0620x; 1.0620x over previous
import numpy as np
import ml_dtypes
from contextlib import ExitStack

import concourse.bass as bass
import concourse.mybir as mybir
from concourse.bass_utils import run_bass_kernel_spmd

F32 = mybir.dt.float32
BF16 = mybir.dt.bfloat16
AF = mybir.ActivationFunctionType
ALU = mybir.AluOpType
BF = ml_dtypes.bfloat16

S = 2048
D = 1024
NTC = 16
NFC = 8
NHP = 8
NQT = 4
QTW = 512
NKT = 16
GRPS = [(i, 1) for i in range(16)]
NSEQ = 2
EPS = 1e-5

ENGINES = ("tensor", "vector", "scalar", "sync", "gpsimd")


def build_program(sq, sk, sv, so):
    nc = bass.Bass("TRN2", target_bir_lowering=False)

    x_h = nc.dram_tensor("x", [NSEQ, S, D], F32, kind="ExternalInput")
    y_h = nc.dram_tensor("y", [NSEQ, S, D], F32, kind="ExternalOutput")
    wq_h = nc.dram_tensor("wq", [NHP, 128, NFC, 128], F32, kind="ExternalInput")
    wk_h = nc.dram_tensor("wk", [NHP, 128, NFC, 128], F32, kind="ExternalInput")
    wv_h = nc.dram_tensor("wv", [NHP, 128, NFC, 128], F32, kind="ExternalInput")
    wo_h = nc.dram_tensor("wo", [D, D], F32, kind="ExternalInput")
    bq_h = nc.dram_tensor("bq", [128, NHP], F32, kind="ExternalInput")
    bk_h = nc.dram_tensor("bk", [128, NHP], F32, kind="ExternalInput")
    bv_h = nc.dram_tensor("bv", [128, NHP], F32, kind="ExternalInput")
    bo_h = nc.dram_tensor("bo", [1, D], F32, kind="ExternalInput")
    win_h = nc.dram_tensor("win", [128, NFC, 64], F32, kind="ExternalInput")
    w2bd_h = nc.dram_tensor("w2bd", [128, NFC, 64], F32, kind="ExternalInput")
    e_h = nc.dram_tensor("em", [64, D], BF16, kind="ExternalInput")
    wout_h = nc.dram_tensor("wout", [64, D], F32, kind="ExternalInput")
    bout_h = nc.dram_tensor("bout", [1, D], F32, kind="ExternalInput")
    bin_h = nc.dram_tensor("binv", [64, 1], F32, kind="ExternalInput")
    b2_h = nc.dram_tensor("b2v", [64, 1], F32, kind="ExternalInput")
    w1il_h = nc.dram_tensor("w1il", [128, NFC], F32, kind="ExternalInput")
    b1il_h = nc.dram_tensor("b1il", [128, NFC], F32, kind="ExternalInput")
    ident_h = nc.dram_tensor("ident", [128, 128], BF16, kind="ExternalInput")
    ones1_h = nc.dram_tensor("ones1", [1, 128], BF16, kind="ExternalInput")
    recd_h = nc.dram_tensor("recd", [128, 256], BF16, kind="Internal")

    st = ExitStack()
    with st:
        sb = lambda n, sh, dt: st.enter_context(nc.sbuf_tensor(n, sh, dt))
        X = sb("X", [128, NTC * D], F32)
        fbA = sb("fbA", [128, NFC * S], BF16)
        fbB = sb("fbB", [128, NFC * S], BF16)
        UA = sb("UA", [128, 16448], BF16)
        xntm = sb("xntm", [128, 2 * D], BF16)
        wstage = sb("wstage", [128, 2 * D], F32)
        WB = sb("WB", [128, 8192], BF16)
        FU = sb("FU", [128, S], F32)
        FUb = FU.bitcast(BF16)
        CU = sb("CU", [128, S], BF16)
        sstage = CU.bitcast(F32)
        bnr = sb("bnr", [128, 192], F32)
        recb = bnr.bitcast(BF16)
        agt = sb("agt", [128, 96], F32)
        bqs = sb("bqs", [128, NHP], F32)
        bks = sb("bks", [128, NHP], F32)
        w1s = sb("w1s", [128, NFC], F32)
        b1s = sb("b1s", [128, NFC], F32)
        bins = sb("bins", [64, 1], F32)
        b2s = sb("b2s", [64, 1], F32)
        bvs = sb("bvs", [128, NHP], F32)
        bor = sb("bor", [1, D], BF16)
        identb = sb("identb", [128, 128], BF16)
        ones1b = sb("ones1b", [1, 128], BF16)

        psA = st.enter_context(nc.psum_tensor("psA", [128, 3584], F32))
        psT = st.enter_context(nc.psum_tensor("psT", [128, 512], F32))
        psT_bf = psT.bitcast(BF16)
        psA_bf = psA.bitcast(BF16)

        PAR = 6176
        QB = 0
        KB = 2048
        VHbase = 4096
        PTbase = 12352

        sem_names = ["vself", "aself", "gself",
                     "d_x", "d_w", "d_b", "d_y", "d_sum", "d_cb", "d_rc", "d_re",
                     "w0", "w1", "w2", "w3",
                     "x0", "x1", "x2", "x3", "y0", "y1", "y2", "y3",
                     "t_xnT", "t_qkv", "t_sc", "t_pvd", "t_wo", "t_ffn", "t_g",
                     "t_wout"]
        sems = {n: st.enter_context(nc.semaphore(n)) for n in sem_names}
        cnt = {n: 0 for n in sem_names}
        prog = {k: [] for k in ENGINES}

        def emit(eng, fn):
            prog[eng].append(fn)

        waited = {e: {} for e in ENGINES}

        def wait(eng, s, v):
            if v <= 0 or waited[eng].get(s, 0) >= v:
                return
            waited[eng][s] = v
            emit(eng, lambda e, _s=sems[s], _v=v: e.wait_ge(_s, _v))

        def op(eng, fn, inc=None):
            if inc is not None:
                cnt[inc] += 1

            def run(e, _fn=fn, _inc=inc):
                ins = _fn(e)
                if _inc is not None:
                    ins.then_inc(sems[_inc], 1)
            emit(eng, run)

        def dma(eng, fn, dsem):
            cnt[dsem] += 16

            def run(e, _fn=fn, _s=sems[dsem]):
                _fn(e).then_inc(_s, 16)
            emit(eng, run)

        def vop(fn):
            wait("vector", "vself", cnt["vself"])
            op("vector", fn, "vself")
            return cnt["vself"]

        def aop(fn):
            wait("scalar", "aself", cnt["aself"])
            op("scalar", fn, "aself")
            return cnt["aself"]

        wrun = [0]

        def wrun_next():
            wrun[0] += 1
            return "w%d" % (wrun[0] % 4)

        def gop(fn):
            wait("gpsimd", "gself", cnt["gself"])
            op("gpsimd", fn, "gself")
            return cnt["gself"]

        def vwait(s, v):
            wait("vector", s, v)

        def awaits(s, v):
            wait("scalar", s, v)

        def xcopy(tc, dst, srcp):
            if tc % 2 == 0:
                return ("aself", aop(lambda e: e.activation(dst, srcp,
                                                            AF.Identity)))
            return ("vself", vop(lambda e: e.tensor_copy(dst, srcp)))

        def xapply(tc, dst, srct, rcol, ncol):
            if tc % 2 == 0:
                return ("aself", aop(lambda e: e.activation(
                    dst, srct, AF.Identity, bias=ncol, scale=rcol)))
            return ("vself", vop(lambda e: e.tensor_scalar(
                dst, srct, rcol, ncol, ALU.mult, ALU.add)))

        def warm(n):
            for _ in range(n):
                op("tensor", lambda e: e.matmul(
                    psA[:, 2048:2176], identb[:], identb[:],
                    start=True, stop=True))

        def strided(t, off, dims):
            return bass.AP(t.ap().tensor, off, [t.ap().ap[0]] + dims)

        for dstt, srct in ((bqs, bq_h), (bks, bk_h), (bvs, bv_h),
                           (w1s, w1il_h),
                           (b1s, b1il_h), (bins, bin_h), (b2s, b2_h),
                           (identb, ident_h), (ones1b, ones1_h)):
            dma("sync", lambda e, a=dstt, b=srct: e.dma_start(a[:], b.ap()),
                "d_b")
        vsetup = 0
        for rsrc, rdst in ((bo_h, bor),):
            wait("sync", "vself", vsetup)
            dma("sync", lambda e, b=rsrc: e.dma_start(wstage[0:1, 0:D],
                                                      b.ap()), "d_b")
            vwait("d_b", cnt["d_b"])
            vsetup = vop(lambda e, d=rdst: e.tensor_copy(d[:],
                                                         wstage[0:1, 0:D]))
        vwait("d_b", cnt["d_b"])
        wait("tensor", "d_b", cnt["d_b"])

        Gg = [0]
        ctxa_pair = [0, 0]
        ctxe_pair = [0, 0]
        lastuse = [0, 0, 0, 0]

        def tb_view(b):
            if b == 3:
                return psT[:, 0:512], psT_bf[:, 0:1024]
            lo = 2048 + 512 * b
            return psA[:, lo:lo + 512], psA_bf[:, 2 * lo:2 * lo + 1024]

        def ctx_dst(cp, h):
            if cp == 0:
                c = 2048 + h * 512
                return psA[0:65, c:c + 512]
            return psA[0:65, 3072:3584] if h == 0 else psT[0:65, 0:512]

        def gen_stats_bn(src_tile, c, w=None):
            if w is not None:
                vwait(*w)
            vop(lambda e, c=c, t=src_tile: e.bn_stats(
                bnr[:, 12 * c:12 * c + 6], t[:, c * D:c * D + 512]))
            vop(lambda e, c=c, t=src_tile: e.bn_stats(
                bnr[:, 12 * c + 6:12 * c + 12],
                t[:, c * D + 512:c * D + 1024]))
            return vop(lambda e, c=c: e.bn_aggr(
                strided(agt, c, [[16, 2]]), bnr[:, 12 * c:12 * c + 12]))

        def gen_rstd_batch(c0, n=4):
            ve = vop(lambda e, c=c0, n=n: e.tensor_scalar(
                agt[:, 32 + c:32 + c + n], agt[:, 16 + c:16 + c + n], EPS,
                None, ALU.add))
            awaits("vself", ve)
            aop(lambda e, c=c0, n=n: e.activation(
                agt[:, 64 + c:64 + c + n], agt[:, 32 + c:32 + c + n],
                AF.Ln))
            ma = aop(lambda e, c=c0, n=n: e.activation(
                agt[:, 64 + c:64 + c + n], agt[:, 64 + c:64 + c + n],
                AF.Exp, scale=-0.5))
            vwait("aself", ma)
            vop(lambda e, c=c0, n=n: e.tensor_scalar(
                agt[:, 80 + c:80 + c + n], agt[:, 64 + c:64 + c + n], -1.0,
                None, ALU.mult))
            return vop(lambda e, c=c0, n=n: e.tensor_tensor(
                agt[:, 80 + c:80 + c + n], agt[:, c:c + n],
                agt[:, 80 + c:80 + c + n], ALU.mult))

        def gen_stats_agg():
            n = NTC
            MEAN, VARP = agt[:, 0:n], agt[:, 16:16 + n]
            mv = vop(lambda e: e.tensor_scalar(VARP, VARP, EPS, None,
                                               ALU.add))
            wait("scalar", "vself", mv)
            aop(lambda e: e.activation(agt[:, 64:64 + n], VARP, AF.Ln))
            ma = aop(lambda e: e.activation(agt[:, 64:64 + n],
                                            agt[:, 64:64 + n],
                                            AF.Exp, scale=-0.5))
            vwait("aself", ma)
            vop(lambda e: e.tensor_scalar(agt[:, 80:80 + n],
                                          agt[:, 64:64 + n],
                                          -1.0, None, ALU.mult))
            return vop(lambda e: e.tensor_tensor(agt[:, 80:80 + n], MEAN,
                                                 agt[:, 80:80 + n],
                                                 ALU.mult))

        def gen_stats(src_tile, chunk_waits=None):
            for c in range(NTC):
                gen_stats_bn(src_tile, c,
                             chunk_waits[c] if chunk_waits is not None
                             else None)
            return gen_stats_agg()

        def gen_ln_fb(fb, chunk_waits=None, rm3=None):
            wait("tensor", "aself", ctxa_pair[1])
            wait("tensor", "vself", ctxe_pair[1])
            tposted = []
            rmq = [0]
            for tc in range(NTC):
                if rm3 is None:
                    if tc % 4 == 0:
                        for dc in range(tc, tc + 4):
                            gen_stats_bn(X, dc,
                                         chunk_waits[dc] if chunk_waits
                                         else None)
                        rmq[0] = gen_rstd_batch(tc)
                    rm = rmq[0]
                else:
                    rm = rm3[(tc // 4) * 4]
                slot = (tc % 2) * D
                if chunk_waits is not None:
                    wait("scalar" if tc % 2 == 0 else "vector",
                         *chunk_waits[tc])
                if tc >= 2:
                    eng = "scalar" if tc % 2 == 0 else "vector"
                    wait(eng, "t_xnT", cnt["t_xnT"] - 1)
                if tc % 2 == 0:
                    awaits("vself", rm)
                mks, mk = xapply(tc, xntm[:, slot:slot + D],
                                 X[:, tc * D:tc * D + D],
                                 agt[:, 64 + tc:65 + tc],
                                 agt[:, 80 + tc:81 + tc])
                warm(3)
                wait("tensor", mks, mk)
                if tc >= 2:
                    wait("tensor", *tposted[tc - 2])
                pdst = psT_bf[:, 0:1024] if tc % 2 == 0 else psA_bf[:, 6144:7168]
                for fc in range(NFC):
                    op("tensor", lambda e, fc=fc, sl=slot, pd=pdst:
                       e.transpose(pd[:, fc * 128:fc * 128 + 128],
                                   xntm[:, sl + fc * 128:sl + fc * 128 + 128],
                                   identb[:]),
                       "t_xnT" if fc == NFC - 1 else None)
                wait("scalar" if tc % 2 == 0 else "vector",
                     "t_xnT", cnt["t_xnT"])
                tposted.append(xcopy(
                    tc, strided(fb, tc * 128, [[S, NFC], [1, 128]]),
                    pdst[:, 0:1024]))
            return (tposted[-2], tposted[-1])

        ysent = 0
        vstg_l = [0]
        wslot = [0]
        wfree = [None, None, None]
        for s in range(NSEQ):
            xr = x_h.rearrange("s (tc p) d -> s p tc d", p=128)
            for q in range(4):
                if s > 0:
                    wait("sync", "y%d" % q, 64 * s)
                dma("sync", lambda e, s=s, q=q: e.dma_start(
                    X[:, q * 4096:(q + 1) * 4096]
                    .rearrange("p (t d) -> p t d", d=D),
                    xr[s, :, q * 4:(q + 1) * 4, :]), "x%d" % q)

            xw = [("x%d" % (c // 4), 16 * (s + 1)) for c in range(NTC)]
            fbe_mark = gen_ln_fb(fbA, xw)

            fstate = {"k": 0, "marks": {}, "bank": {}}

            def stream_qkv_weights(hp, tq_need):
                if wfree[0] is None:
                    wfree[0] = wfree[1] = wfree[2] = vsetup
                wb = (hp % 2) * 3072
                rs = wrun_next()
                slots = [wstage[:, 0:D], wstage[:, D:2 * D], FU[:, 1024:2048]]
                wait("sync", "t_wout", cnt["t_wout"])
                for j, wh in enumerate((wq_h, wk_h, wv_h)):
                    wait("sync", "vself", wfree[j])
                    dma("sync", lambda e, wh=wh, hp=hp, st=slots[j]:
                        e.dma_start(st.rearrange("p (i o) -> p i o", o=128),
                                    wh.ap()[hp]), rs)
                vwait(rs, cnt[rs])
                vwait("t_qkv", tq_need)
                wcast = []
                for j, (wof, scale) in enumerate(((0, sq), (1024, sk),
                                                  (2048, sv))):
                    vstg_l[0] = vop(lambda e, c=wb + wof, sc=scale,
                                    st=slots[j]:
                                    e.tensor_scalar(WB[:, c:c + 1024], st,
                                                    sc, None, ALU.mult))
                    wfree[j] = vstg_l[0]
                    wcast.append(vstg_l[0])
                return wcast

            def make_qkv_fillers(hp, wcast, xnm):
                par = (hp % 2) * PAR
                wb = (hp % 2) * 3072
                groups = []
                vev = [0]
                for j, (wof, bvec, dcol) in enumerate(
                        ((0, bqs, QB), (1024, bks, KB))):
                    for q4 in range(4):
                        def emitA(tb, tbb, j=j, q4=q4, wof=wof, wb=wb,
                                  wc=wcast[j], xnm=xnm):
                            wait("tensor", "vself", wc)
                            for w_ in xnm:
                                wait("tensor", *w_)
                            for ic in range(4):
                                op("tensor",
                                   lambda e, ic=ic, q4=q4, tb=tb, c=wb + wof:
                                   e.matmul(tb[:, 0:512],
                                            WB[:, c + ic * 128:
                                               c + ic * 128 + 128],
                                            fbA[:, ic * S + q4 * 512:
                                                ic * S + q4 * 512 + 512],
                                            start=(ic == 0), stop=False))
                            return None
                        def emitB(tb, tbb, j=j, q4=q4, wof=wof, bvec=bvec,
                                  dcol=dcol, hp=hp, par=par, wb=wb):
                            for ic in range(4, NFC):
                                op("tensor",
                                   lambda e, ic=ic, q4=q4, tb=tb, c=wb + wof:
                                   e.matmul(tb[:, 0:512],
                                            WB[:, c + ic * 128:
                                               c + ic * 128 + 128],
                                            fbA[:, ic * S + q4 * 512:
                                                ic * S + q4 * 512 + 512],
                                            start=False, stop=(ic == NFC - 1)),
                                   "t_qkv" if ic == NFC - 1 else None)
                            vwait("t_qkv", cnt["t_qkv"])
                            return vop(
                                lambda e, q4=q4, tb=tb, bv=bvec, hp=hp,
                                dc=par + dcol:
                                e.tensor_scalar(
                                    UA[:, dc + q4 * 512:dc + q4 * 512 + 512],
                                    tb[:, 0:512], 1.0,
                                    bv[:, hp:hp + 1], ALU.mult, ALU.add))
                        groups.append((emitA, True))
                        groups.append((emitB, False))
                for q4 in range(4):
                    def emitA(tb, tbb, q4=q4, wb=wb, wc=wcast[2], xnm=xnm):
                        wait("tensor", "vself", wc)
                        for w_ in xnm:
                            wait("tensor", *w_)
                        for ic in range(4):
                            op("tensor", lambda e, ic=ic, q4=q4, tb=tb,
                               c=wb + 2048:
                               e.matmul(tb[:, 0:512],
                                        WB[:, c + ic * 128:c + ic * 128 + 128],
                                        fbA[:, ic * S + q4 * 512:
                                            ic * S + q4 * 512 + 512],
                                        start=(ic == 0), stop=False))
                        return None
                    def emitB(tb, tbb, q4=q4, hp=hp, wb=wb, vev=vev):
                        for ic in range(4, NFC):
                            op("tensor", lambda e, ic=ic, q4=q4, tb=tb,
                               c=wb + 2048:
                               e.matmul(tb[:, 0:512],
                                        WB[:, c + ic * 128:c + ic * 128 + 128],
                                        fbA[:, ic * S + q4 * 512:
                                            ic * S + q4 * 512 + 512],
                                        start=False, stop=(ic == NFC - 1)),
                               "t_qkv" if ic == NFC - 1 else None)
                        vwait("t_qkv", cnt["t_qkv"])
                        m = vop(lambda e, q4=q4, tb=tb, hp=hp, par=par:
                                e.tensor_scalar(
                                    UA[:, par + VHbase + q4 * 512:
                                       par + VHbase + q4 * 512 + 512],
                                    tb[:, 0:512], 1.0,
                                    bvs[:, hp:hp + 1], ALU.mult, ALU.add))
                        vev.append(m)
                        return m
                    groups.append((emitA, True))
                    groups.append((emitB, False))
                for th in (1, 0):
                    def emitT(tb, tbb, th=th, par=par, vev=vev):
                        wait("tensor", "vself", vev[-1])
                        for tl in range(8):
                            tc = th * 8 + tl
                            op("tensor", lambda e, tc=tc, tl=tl, tbb=tbb,
                               par=par:
                               e.transpose(tbb[:, tl * 128:tl * 128 + 128],
                                           UA[:, par + VHbase + tc * 128:
                                              par + VHbase + tc * 128 + 128],
                                           identb[:]),
                               "t_qkv" if tl == 7 else None)
                        vwait("t_qkv", cnt["t_qkv"])
                        m = vop(lambda e, th=th, tbb=tbb, par=par:
                                e.tensor_copy(
                                    strided(UA, par + VHbase + th * 8 * 130,
                                            [[130, 8], [65, 2], [1, 64]]),
                                    tbb[:, 0:1024]))
                        if th == 0:
                            m = vop(lambda e, par=par: e.memset(
                                strided(UA, par + VHbase + 64,
                                        [[130, 16], [65, 2], [1, 1]]), 1.0))
                        return m
                    groups.append((emitT, True))
                return groups

            def emit_filler(pend, fpair):
                if not pend:
                    return
                emitf, newb = pend.pop(0)
                if newb:
                    k = fstate["k"]
                    fstate["k"] += 1
                    if k >= 2 and (k - 2) in fstate["marks"]:
                        wait("tensor", "vself", fstate["marks"][k - 2])
                    b = 2 * fpair + (k % 2)
                    fstate["bank"][k] = b
                    wait("tensor", "aself", ctxa_pair[fpair])
                    wait("tensor", "vself", ctxe_pair[fpair])
                k = fstate["k"] - 1
                tb, tbb = tb_view(fstate["bank"][k])
                m = emitf(tb, tbb)
                if m is not None:
                    fstate["marks"][k] = m
                    fstate["last"] = m
                    lastuse[fstate["bank"][k]] = m

            vnorm_l = [0]
            vnorms = []
            ctxe_l = [0]
            ctxa_l = [0]
            fstate["last"] = 0
            eas = {}
            tq_hist = {}
            wcast0 = stream_qkv_weights(0, 0)
            pend0 = make_qkv_fillers(0, wcast0, fbe_mark)
            while pend0:
                emit_filler(pend0, 1)
            tq_hist[0] = cnt["t_qkv"]
            qkv_done = fstate["last"]
            wcast = stream_qkv_weights(1, 0)
            pend = make_qkv_fillers(1, wcast, fbe_mark)

            for hp in range(NHP):
                par = (hp % 2) * PAR

                wait("tensor", "vself", qkv_done)
                pendq = []

                def emit_pv(payload, par=par):
                    gq, k0, klen, pt, cp = payload
                    wait("tensor", "aself", eas[gq])
                    if k0 == 0:
                        wait("tensor", "aself", ctxa_pair[cp])
                        wait("tensor", "vself", ctxe_pair[cp])
                        wait("tensor", "vself", lastuse[2 * cp])
                        wait("tensor", "vself", lastuse[2 * cp + 1])
                    kt = k0
                    for h in range(2):
                        dst = ctx_dst(cp, h)
                        ph = pt + h * 512
                        vh0 = par + VHbase + kt * 130 + h * 65
                        op("tensor", lambda e, d=dst, ph=ph,
                           v0=vh0, kt=kt:
                           e.matmul(d,
                                    UA[:, v0:v0 + 65],
                                    UA[:, ph:ph + 512],
                                    start=(kt == 0), stop=(kt == NKT - 1)),
                           "t_pvd" if h == 1 else None)

                pend_evac = [None]

                def emit_evac():
                    ehp, eqb, first_of_hp = pend_evac[0]
                    pend_evac[0] = None
                    eqt = eqb // QTW
                    cp = eqt % 2
                    awaits("t_pvd", cnt["t_pvd"])
                    srcA = (psA[0:64, 2048:2560] if cp == 0
                            else psA[0:64, 3072:3584])
                    ctxa_pair[cp] = ctxa_l[0] = aop(
                        lambda e, hp=ehp, qb=eqb, s=srcA: e.activation(
                            fbB[0:64, hp * S + qb:hp * S + qb + QTW],
                            s, AF.Identity))
                    vwait("t_pvd", cnt["t_pvd"])
                    if first_of_hp:
                        vwait("d_cb", cnt["d_cb"])
                    srcB = (psA[0:64, 2560:3072] if cp == 0
                            else psT[0:64, 0:512])
                    vop(lambda e, qb=eqb, s=srcB: e.tensor_copy(
                        CU[0:64, qb:qb + QTW], s))
                    vwait("d_sum", cnt["d_sum"])
                    if cp == 0:
                        ctxe_l[0] = vop(lambda e: e.tensor_copy(
                            sstage[64:65, 0:1024], psA[64:65, 2048:3072]))
                    else:
                        vop(lambda e: e.tensor_copy(
                            sstage[64:65, 0:512], psA[64:65, 3072:3584]))
                        ctxe_l[0] = vop(lambda e: e.tensor_copy(
                            sstage[64:65, 512:1024], psT[64:65, 0:512]))
                    ctxe_pair[cp] = ctxe_l[0]
                    wait("gpsimd", "vself", ctxe_l[0])
                    for half in range(2):
                        pa = 2 * eqt + half
                        pb = 8 + 2 * eqt + half
                        dma("gpsimd", lambda e, p=pa, o=half * 256:
                            e.dma_start(FU[p:p + 1, 0:256],
                                        sstage[64:65, o:o + 256]), "d_sum")
                        dma("gpsimd", lambda e, p=pb, o=512 + half * 256:
                            e.dma_start(FU[p:p + 1, 0:256],
                                        sstage[64:65, o:o + 256]), "d_sum")

                for qt in range(NQT):
                    qb = qt * QTW
                    for (k0, klen) in GRPS:
                        g = Gg[0]
                        Gg[0] += 1
                        p = g % 2
                        wait("tensor", "aself", eas.get(g - 2, 0))
                        kt = k0
                        for h in range(2):
                            base = h * 64
                            sc = p * 1024 + h * 512
                            op("tensor", lambda e, kt=kt, b=base, sc=sc,
                               qb=qb, par=par:
                               e.matmul(psA[:, sc:sc + 512],
                                        UA[b:b + 64, par + KB + kt * 128:
                                           par + KB + kt * 128 + 128],
                                        UA[b:b + 64, par + QB + qb:
                                           par + QB + qb + QTW],
                                        start=True, stop=True),
                               "t_sc" if h == 1 else None)
                        if k0 == 2 and pend_evac[0] is not None:
                            emit_evac()
                        if len(pendq) >= 2:
                            emit_pv(pendq.pop(0))
                        if (qt * len(GRPS) + k0 >= 12
                                and (qt == 0 or k0 >= 3)):
                            emit_filler(pend, 1 - qt % 2)
                        wait("scalar", "t_sc", cnt["t_sc"])
                        if g >= 4:
                            wait("scalar", "t_pvd", g - 3)
                        pt = PTbase + (g % 4) * 1024
                        eas[g] = aop(lambda e, pt=pt, p=p: e.activation(
                            UA[:, pt:pt + 1024], psA[:, p * 1024:p * 1024 + 1024],
                            AF.Exp, scale=0.125))
                        pendq.append((g, k0, klen, pt, qt % 2))
                    if pend and not pend[0][1]:
                        emit_filler(pend, 1 - qt % 2)
                    if qt < NQT - 1:
                        pend_evac[0] = (hp, qb, hp > 0 and qt == 0)
                    else:
                        wait("tensor", "aself", cnt["aself"])
                        while pendq:
                            emit_pv(pendq.pop(0))
                        pend_evac[0] = (hp, qb, False)
                        emit_evac()
                while pend:
                    emit_filler(pend, 1)
                if hp + 1 < NHP:
                    tq_hist[hp + 1] = cnt["t_qkv"]
                    qkv_done = fstate["last"]
                if hp + 2 < NHP:
                    wcast = stream_qkv_weights(hp + 2, tq_hist[hp])
                    pend = make_qkv_fillers(hp + 2, wcast, fbe_mark)
                else:
                    pend = []
                if hp == 6:
                    tq_all = cnt["t_qkv"]
                    for icp in range(4):
                        rs = wrun_next()
                        for sl in range(2):
                            ic = icp * 2 + sl
                            wait("sync", "vself", wfree[sl])
                            dma("sync", lambda e, ic=ic, sl=sl: e.dma_start(
                                wstage[:, sl * D:sl * D + D],
                                wo_h.ap()[ic * 128:ic * 128 + 128, :]), rs)
                        vwait(rs, cnt[rs])
                        vwait("t_qkv", tq_all)
                        for sl in range(2):
                            ic = icp * 2 + sl
                            vstg_l[0] = vop(
                                lambda e, ic=ic, sl=sl: e.tensor_scalar(
                                    WB[:, ic * 1024:ic * 1024 + 1024],
                                    wstage[:, sl * D:sl * D + D], so, None,
                                    ALU.mult))
                            wfree[sl] = vstg_l[0]


                wait("gpsimd", "vself", ctxe_l[0])
                dma("gpsimd", lambda e, hp=hp: e.dma_start(
                    fbB[64:128, hp * S:hp * S + S], CU[0:64, :]), "d_cb")

                vwait("d_sum", cnt["d_sum"])

                def _recip(e):
                    with nc.allow_low_precision(
                            reason="softmax 1/sum consumed as bf16 scale"):
                        return e.reciprocal(recb[0:16, 0:256], FU[0:16, 0:256])
                vrec = vop(_recip)
                wait("gpsimd", "vself", vrec)
                dma("gpsimd", lambda e, hp=hp: e.dma_start(
                    recd_h.ap()[16 * hp:16 * hp + 16, :],
                    recb[0:16, 0:256]), "d_rc")
                wait("gpsimd", "d_rc", cnt["d_rc"])
                if hp > 0:
                    wait("gpsimd", "vself", vnorm_l[0])
                for h in range(2):
                    rsrc = bass.AP(recd_h.ap().tensor,
                                   (16 * hp + 8 * h) * 256,
                                   [[0, 64], [256, 8], [1, 256]])
                    dma("gpsimd", lambda e, h=h, rsrc=rsrc: e.dma_start(
                        xntm[h * 64:h * 64 + 64, :]
                        .rearrange("p (b j) -> p b j", j=256), rsrc), "d_re")
                vwait("d_re", cnt["d_re"])
                vwait("d_cb", cnt["d_cb"])
                vwait("aself", ctxa_l[0])
                vnorm_l[0] = vop(lambda e, hp=hp: e.tensor_tensor(
                    fbB[:, hp * S:hp * S + S], fbB[:, hp * S:hp * S + S],
                    xntm[:, :], ALU.mult))
                vnorms.append(vnorm_l[0])
            vnorm = vnorm_l[0]

            wait("tensor", "vself", vstg_l[0])
            vwait("d_rc", cnt["d_rc"])
            woe, adds, rm3 = [], [], {}
            for tc in range(NTC):
                bank = (tc % 2) * 1024
                if tc >= 2:
                    wait("tensor", *woe[tc - 2])
                for half in range(2):
                    for ic in range(NFC):
                        if tc == 0 and half == 0:
                            wait("tensor", "vself", vnorms[ic])
                        op("tensor", lambda e, tc=tc, ic=ic, h=half, b=bank:
                           e.matmul(psA[:, b + h * 512:b + h * 512 + 512],
                                    fbB[:, ic * S + tc * 128:
                                        ic * S + tc * 128 + 128],
                                    WB[:, ic * 1024 + h * 512:
                                       ic * 1024 + h * 512 + 512],
                                    start=(ic == 0), stop=False))
                    op("tensor", lambda e, h=half, b=bank:
                       e.matmul(psA[:, b + h * 512:b + h * 512 + 512],
                                ones1b[:], bor[0:1, h * 512:h * 512 + 512],
                                start=False, stop=True),
                       "t_wo" if half == 1 else None)
                wait("scalar" if tc % 2 == 0 else "vector",
                     "t_wo", cnt["t_wo"])
                woe.append(xcopy(tc, fbA[:, tc * 1024:tc * 1024 + 1024],
                                 psA[:, bank:bank + 1024]))
                gen_stats_bn(fbA, tc, woe[tc])
                if tc % 4 != 3:
                    continue
                rmb = gen_rstd_batch(tc - 3)
                for dc in range(tc - 3, tc + 1):
                    sl = (dc % 2) * D
                    warm(2)
                    if dc >= 2:
                        eng = "scalar" if dc % 2 == 0 else "vector"
                        wait(eng, "vself", adds[dc - 2])
                    if dc % 2 == 0:
                        awaits("vself", rmb)
                    mks, mk = xapply(dc, xntm[:, sl:sl + D],
                                     fbA[:, dc * D:dc * D + D],
                                     agt[:, 64 + dc:65 + dc],
                                     agt[:, 80 + dc:81 + dc])
                    vwait(mks, mk)
                    adds.append(vop(lambda e, dc=dc, sl=sl: e.tensor_tensor(
                        X[:, dc * D:dc * D + D], X[:, dc * D:dc * D + D],
                        xntm[:, sl:sl + D], ALU.add)))
                    gen_stats_bn(X, dc)
                rm3[tc - 3] = gen_rstd_batch(tc - 3)
            wait("sync", "t_wo", cnt["t_wo"])
            rs_e = wrun_next()
            dma("sync", lambda e: e.dma_start(WB[0:64, 1024:2048], e_h.ap()),
                rs_e)
            tw_all = cnt["t_wo"]
            ffw = [(win_h, slice(0, 512), 64, 512, None),
                   (w2bd_h, slice(512, 1024), 64, 512, None),
                   (wout_h, None, None, None, "wout"),
                   (bout_h, None, None, None, "bout")]
            for srct, cols, oo, ww, special in ffw:
                sl = wslot[0] % 2
                wslot[0] += 1
                rs = wrun_next()
                wait("sync", "vself", wfree[sl])
                if special is None:
                    dma("sync", lambda e, srct=srct, sl=sl: e.dma_start(
                        wstage[:, sl * D:sl * D + 512]
                        .rearrange("p (i o) -> p i o", o=64),
                        srct.ap()), rs)
                elif special == "wout":
                    dma("sync", lambda e, sl=sl: e.dma_start(
                        wstage[0:64, sl * D:sl * D + D], wout_h.ap()), rs)
                else:
                    dma("sync", lambda e, sl=sl: e.dma_start(
                        wstage[0:1, sl * D:sl * D + D], bout_h.ap()), rs)
                vwait(rs, cnt[rs])
                vwait("t_wo", tw_all)
                if special is None:
                    vstg_l[0] = vop(lambda e, c=cols, sl=sl: e.tensor_copy(
                        WB[:, c], wstage[:, sl * D:sl * D + 512]))
                elif special == "wout":
                    vstg_l[0] = vop(lambda e, sl=sl: e.tensor_copy(
                        WB[0:64, 2048:3072], wstage[0:64, sl * D:sl * D + D]))
                else:
                    vstg_l[0] = vop(lambda e, sl=sl: e.tensor_copy(
                        WB[64:65, 2048:3072], wstage[0:1, sl * D:sl * D + D]))
                wfree[sl] = vstg_l[0]

            fbe_mark = gen_ln_fb(fbB, rm3=rm3)

            wait("sync", "t_wo", cnt["t_wo"])
            wait("tensor", "vself", vstg_l[0])
            for w_ in fbe_mark:
                wait("tensor", *w_)
            wait("tensor", rs_e, cnt[rs_e])
            sfbe = []
            for q4 in range(4):
                if q4 >= 1:
                    wait("tensor", "vself", sfbe[-1])
                for ic in range(NFC):
                    op("tensor", lambda e, ic=ic, q4=q4:
                       e.matmul(psT[0:64, 0:512],
                                WB[:, ic * 64:ic * 64 + 64],
                                fbB[:, ic * S + q4 * 512:
                                    ic * S + q4 * 512 + 512],
                                start=(ic == 0), stop=(ic == NFC - 1)),
                       "t_ffn" if ic == NFC - 1 else None)
                vwait("t_ffn", cnt["t_ffn"])
                sfbe.append(vop(lambda e, q4=q4: e.tensor_scalar(
                    FUb[0:64, q4 * 512:q4 * 512 + 512], psT[0:64, 0:512],
                    1.0, bins[0:64, 0:1], ALU.mult, ALU.add)))
            wait("tensor", "vself", sfbe[-1])
            for fc in range(NFC):
                if fc >= 1:
                    wait("tensor", "aself", cnt["aself"])
                for q4 in range(4):
                    op("tensor", lambda e, fc=fc, q4=q4:
                       e.matmul(psA[:, q4 * 512:q4 * 512 + 512],
                                WB[0:64, 1024 + fc * 128:1024 + fc * 128 + 128],
                                FUb[0:64, q4 * 512:q4 * 512 + 512],
                                start=True, stop=True),
                       "t_ffn" if q4 == 3 else None)
                wait("scalar", "t_ffn", cnt["t_ffn"])
                aop(lambda e, fc=fc: e.activation(
                    fbA[:, fc * S:fc * S + S], psA[:, 0:2048], AF.Relu,
                    bias=b1s[:, fc:fc + 1], scale=w1s[:, fc:fc + 1]))
            wait("tensor", "aself", cnt["aself"])
            gfbe = []
            for q4 in range(4):
                if q4 >= 1:
                    wait("tensor", "vself", gfbe[-1])
                for fc in range(NFC):
                    op("tensor", lambda e, fc=fc, q4=q4:
                       e.matmul(psT[0:64, 0:512],
                                WB[:, 512 + fc * 64:512 + fc * 64 + 64],
                                fbA[:, fc * S + q4 * 512:
                                    fc * S + q4 * 512 + 512],
                                start=(fc == 0), stop=(fc == NFC - 1)),
                       "t_g" if fc == NFC - 1 else None)
                vwait("t_g", cnt["t_g"])
                gfbe.append(vop(lambda e, q4=q4: e.tensor_scalar(
                    FUb[0:64, S + q4 * 512:S + q4 * 512 + 512],
                    psT[0:64, 0:512],
                    1.0, b2s[0:64, 0:1], ALU.mult, ALU.add)))
            gfbe.append(vop(lambda e: e.memset(FUb[64:65, S:2 * S], 1.0)))
            wait("tensor", "vself", gfbe[-1])
            woe, resid = [], []
            for tc in range(NTC):
                bank = (tc % 2) * 1024
                if tc >= 2:
                    wait("tensor", *woe[tc - 2])
                for half in range(2):
                    op("tensor", lambda e, tc=tc, h=half, b=bank:
                       e.matmul(psA[:, b + h * 512:b + h * 512 + 512],
                                FUb[0:65, S + tc * 128:S + tc * 128 + 128],
                                WB[0:65, 2048 + h * 512:2048 + h * 512 + 512],
                                start=True, stop=True),
                       "t_wout" if half == 1 else None)
                wait("scalar" if tc % 2 == 0 else "vector",
                     "t_wout", cnt["t_wout"])
                woe.append(xcopy(tc, UA[:, tc * 1024:tc * 1024 + 1024],
                                 psA[:, bank:bank + 1024]))
                gen_stats_bn(UA, tc, woe[tc])
                if tc % 4 != 3:
                    continue
                rmb = gen_rstd_batch(tc - 3)
                for dc in range(tc - 3, tc + 1):
                    sl = (dc % 2) * D
                    warm(2)
                    if dc >= 2:
                        eng = "scalar" if dc % 2 == 0 else "vector"
                        wait(eng, "vself", resid[dc - 2])
                    if dc % 2 == 0:
                        awaits("vself", rmb)
                    mks, mk = xapply(dc, xntm[:, sl:sl + D],
                                     UA[:, dc * D:dc * D + D],
                                     agt[:, 64 + dc:65 + dc],
                                     agt[:, 80 + dc:81 + dc])
                    vwait(mks, mk)
                    resid.append(vop(lambda e, dc=dc, sl=sl: e.tensor_tensor(
                        X[:, dc * D:dc * D + D], X[:, dc * D:dc * D + D],
                        xntm[:, sl:sl + D], ALU.add)))

            for tc in range(NTC):
                wait("sync", "vself", resid[tc])
                dma("sync", lambda e, s=s, tc=tc: e.dma_start(
                    y_h.ap()[s, tc * 128:tc * 128 + 128, :],
                    X[:, tc * D:tc * D + D]), "y%d" % (tc // 4))
            ysent += NTC

        for q in range(4):
            wait("sync", "y%d" % q, cnt["y%d" % q])

        blk = st.enter_context(nc.Block())

        @blk.tensor
        def _(e):
            for f in prog["tensor"]:
                f(e)

        @blk.vector
        def _(e):
            for f in prog["vector"]:
                f(e)

        @blk.scalar
        def _(e):
            for f in prog["scalar"]:
                f(e)

        @blk.sync
        def _(e):
            for f in prog["sync"]:
                f(e)

        @blk.gpsimd
        def _(e):
            for f in prog["gpsimd"]:
                f(e)

    return nc


def kernel(**inputs):
    g = lambda k: np.asarray(inputs[k])
    x = g("L0_attn_x")
    sq = float(np.asarray(inputs["L0_attn_s_q"]))
    sk = float(np.asarray(inputs["L0_attn_s_k"]))
    sv = float(np.asarray(inputs["L0_attn_s_v"]))
    so = float(np.asarray(inputs["L0_attn_s_o"]))

    def wlayout(w):
        wT = np.ascontiguousarray(w.T)
        return np.ascontiguousarray(
            wT.reshape(NFC, 128, NHP, 128).transpose(2, 1, 0, 3))

    wq = wlayout(g("L0_attn_w_q"))
    wk = wlayout(g("L0_attn_w_k"))
    wv = wlayout(g("L0_attn_w_v"))
    wo = np.ascontiguousarray(g("L0_attn_w_o").T)
    bq = np.ascontiguousarray(g("L0_attn_b_q").reshape(NHP, 128).T)
    bk = np.ascontiguousarray(g("L0_attn_b_k").reshape(NHP, 128).T)
    bv = np.ascontiguousarray(g("L0_attn_b_v").reshape(NHP, 128).T)
    bo = g("L0_attn_b_o").reshape(1, D)
    Win = g("L0_ffn_Win")
    win = np.ascontiguousarray(Win.reshape(NFC, 128, 64).transpose(1, 0, 2))
    W2 = g("L0_ffn_W2")
    w2bd_full = np.zeros((D, 64), np.float32)
    for m in range(64):
        w2bd_full[m * 16:(m + 1) * 16, m] = W2[m]
    w2bd = np.ascontiguousarray(
        w2bd_full.reshape(NFC, 128, 64).transpose(1, 0, 2))
    em = np.zeros((64, D), np.float32)
    for m in range(64):
        em[m, m * 16:(m + 1) * 16] = 1.0
    em = em.astype(BF)
    wout = np.ascontiguousarray(g("L0_ffn_Wout"))
    bout = g("L0_ffn_bout").reshape(1, D)
    binv = g("L0_ffn_bin_").reshape(64, 1)
    b2v = g("L0_ffn_b2").reshape(64, 1)
    w1il = np.ascontiguousarray(g("L0_ffn_W1").reshape(NFC, 128).T)
    b1il = np.ascontiguousarray(g("L0_ffn_b1").reshape(NFC, 128).T)
    ident = np.eye(128, dtype=np.float32).astype(BF)
    ones1 = np.ones((1, 128), np.float32).astype(BF)

    nc = build_program(sq, sk, sv, so)
    shared = dict(wq=wq, wk=wk, wv=wv, wo=wo, bq=bq, bk=bk, bv=bv, bo=bo,
                  win=win, w2bd=w2bd, em=em, wout=wout, bout=bout, binv=binv,
                  b2v=b2v, w1il=w1il, b1il=b1il, ident=ident, ones1=ones1)
    in_maps = []
    for c in range(8):
        m = dict(shared)
        m["x"] = np.ascontiguousarray(x[2 * c:2 * c + 2])
        in_maps.append(m)
    res = run_bass_kernel_spmd(nc, in_maps, core_ids=list(range(8)))
    out = np.empty((16, S, D), np.float32)
    for c in range(8):
        out[2 * c:2 * c + 2] = res.results[c]["y"]
    return out


# version 29
# speedup vs baseline: 1.0707x; 1.0082x over previous
import numpy as np
import ml_dtypes
from contextlib import ExitStack

import concourse.bass as bass
import concourse.mybir as mybir
from concourse.bass_utils import run_bass_kernel_spmd

F32 = mybir.dt.float32
BF16 = mybir.dt.bfloat16
AF = mybir.ActivationFunctionType
ALU = mybir.AluOpType
BF = ml_dtypes.bfloat16

S = 2048
D = 1024
NTC = 16
NFC = 8
NHP = 8
NQT = 4
QTW = 512
NKT = 16
GRPS = [(i, 1) for i in range(16)]
NSEQ = 2
EPS = 1e-5

ENGINES = ("tensor", "vector", "scalar", "sync", "gpsimd")


def build_program(sq, sk, sv, so):
    nc = bass.Bass("TRN2", target_bir_lowering=False)

    x_h = nc.dram_tensor("x", [NSEQ, S, D], F32, kind="ExternalInput")
    y_h = nc.dram_tensor("y", [NSEQ, S, D], F32, kind="ExternalOutput")
    wq_h = nc.dram_tensor("wq", [NHP, 128, NFC, 128], F32, kind="ExternalInput")
    wk_h = nc.dram_tensor("wk", [NHP, 128, NFC, 128], F32, kind="ExternalInput")
    wv_h = nc.dram_tensor("wv", [NHP, 128, NFC, 128], F32, kind="ExternalInput")
    wo_h = nc.dram_tensor("wo", [D, D], F32, kind="ExternalInput")
    bq_h = nc.dram_tensor("bq", [128, NHP], F32, kind="ExternalInput")
    bk_h = nc.dram_tensor("bk", [128, NHP], F32, kind="ExternalInput")
    bv_h = nc.dram_tensor("bv", [128, NHP], F32, kind="ExternalInput")
    bo_h = nc.dram_tensor("bo", [1, D], F32, kind="ExternalInput")
    win_h = nc.dram_tensor("win", [128, NFC, 64], F32, kind="ExternalInput")
    w2bd_h = nc.dram_tensor("w2bd", [128, NFC, 64], F32, kind="ExternalInput")
    e_h = nc.dram_tensor("em", [64, D], BF16, kind="ExternalInput")
    wout_h = nc.dram_tensor("wout", [64, D], F32, kind="ExternalInput")
    bout_h = nc.dram_tensor("bout", [1, D], F32, kind="ExternalInput")
    bin_h = nc.dram_tensor("binv", [64, 1], F32, kind="ExternalInput")
    b2_h = nc.dram_tensor("b2v", [64, 1], F32, kind="ExternalInput")
    w1il_h = nc.dram_tensor("w1il", [128, NFC], F32, kind="ExternalInput")
    b1il_h = nc.dram_tensor("b1il", [128, NFC], F32, kind="ExternalInput")
    ident_h = nc.dram_tensor("ident", [128, 128], BF16, kind="ExternalInput")
    ones1_h = nc.dram_tensor("ones1", [1, 128], BF16, kind="ExternalInput")
    recd_h = nc.dram_tensor("recd", [128, 256], BF16, kind="Internal")

    st = ExitStack()
    with st:
        sb = lambda n, sh, dt: st.enter_context(nc.sbuf_tensor(n, sh, dt))
        X = sb("X", [128, NTC * D], F32)
        fbA = sb("fbA", [128, NFC * S], BF16)
        fbB = sb("fbB", [128, NFC * S], BF16)
        UA = sb("UA", [128, 16448], BF16)
        xntm = sb("xntm", [128, 2 * D], BF16)
        wstage = sb("wstage", [128, 2 * D], F32)
        WB = sb("WB", [128, 8192], BF16)
        FU = sb("FU", [128, S], F32)
        FUb = FU.bitcast(BF16)
        CU = sb("CU", [128, S], BF16)
        sstage = CU.bitcast(F32)
        bnr = sb("bnr", [128, 192], F32)
        recb = bnr.bitcast(BF16)
        agt = sb("agt", [128, 96], F32)
        bqs = sb("bqs", [128, NHP], F32)
        bks = sb("bks", [128, NHP], F32)
        w1s = sb("w1s", [128, NFC], F32)
        b1s = sb("b1s", [128, NFC], F32)
        bins = sb("bins", [64, 1], F32)
        b2s = sb("b2s", [64, 1], F32)
        bvs = sb("bvs", [128, NHP], F32)
        bor = sb("bor", [1, D], BF16)
        identb = sb("identb", [128, 128], BF16)
        ones1b = sb("ones1b", [1, 128], BF16)

        psA = st.enter_context(nc.psum_tensor("psA", [128, 3584], F32))
        psT = st.enter_context(nc.psum_tensor("psT", [128, 512], F32))
        psT_bf = psT.bitcast(BF16)
        psA_bf = psA.bitcast(BF16)

        PAR = 6176
        QB = 0
        KB = 2048
        VHbase = 4096
        PTbase = 12352

        sem_names = ["vself", "aself", "gself",
                     "d_x", "d_w", "d_b", "d_y", "d_sum", "d_cb", "d_rc", "d_re",
                     "w0", "w1", "w2", "w3",
                     "x0", "x1", "x2", "x3", "y0", "y1", "y2", "y3",
                     "t_xnT", "t_qkv", "t_sc", "t_pvd", "t_wo", "t_ffn", "t_g",
                     "t_wout"]
        sems = {n: st.enter_context(nc.semaphore(n)) for n in sem_names}
        cnt = {n: 0 for n in sem_names}
        prog = {k: [] for k in ENGINES}

        def emit(eng, fn):
            prog[eng].append(fn)

        waited = {e: {} for e in ENGINES}

        def wait(eng, s, v):
            if v <= 0 or waited[eng].get(s, 0) >= v:
                return
            waited[eng][s] = v
            emit(eng, lambda e, _s=sems[s], _v=v: e.wait_ge(_s, _v))

        def op(eng, fn, inc=None):
            if inc is not None:
                cnt[inc] += 1

            def run(e, _fn=fn, _inc=inc):
                ins = _fn(e)
                if _inc is not None:
                    ins.then_inc(sems[_inc], 1)
            emit(eng, run)

        def dma(eng, fn, dsem):
            cnt[dsem] += 16

            def run(e, _fn=fn, _s=sems[dsem]):
                _fn(e).then_inc(_s, 16)
            emit(eng, run)

        def vop(fn):
            wait("vector", "vself", cnt["vself"])
            op("vector", fn, "vself")
            return cnt["vself"]

        def aop(fn):
            wait("scalar", "aself", cnt["aself"])
            op("scalar", fn, "aself")
            return cnt["aself"]

        wrun = [0]

        def wrun_next():
            wrun[0] += 1
            return "w%d" % (wrun[0] % 4)

        def gop(fn):
            wait("gpsimd", "gself", cnt["gself"])
            op("gpsimd", fn, "gself")
            return cnt["gself"]

        def vwait(s, v):
            wait("vector", s, v)

        def awaits(s, v):
            wait("scalar", s, v)

        def xcopy(tc, dst, srcp):
            if tc % 2 == 0:
                return ("aself", aop(lambda e: e.activation(dst, srcp,
                                                            AF.Identity)))
            return ("vself", vop(lambda e: e.tensor_copy(dst, srcp)))

        def xapply(tc, dst, srct, rcol, ncol):
            if tc % 2 == 0:
                return ("aself", aop(lambda e: e.activation(
                    dst, srct, AF.Identity, bias=ncol, scale=rcol)))
            return ("vself", vop(lambda e: e.tensor_scalar(
                dst, srct, rcol, ncol, ALU.mult, ALU.add)))

        def warm(n, wb=2048):
            for _ in range(n):
                if wb == -1:
                    op("tensor", lambda e: e.matmul(
                        psT[:, 0:128], identb[:], identb[:],
                        start=True, stop=True))
                else:
                    op("tensor", lambda e, wb=wb: e.matmul(
                        psA[:, wb:wb + 128], identb[:], identb[:],
                        start=True, stop=True))

        def strided(t, off, dims):
            return bass.AP(t.ap().tensor, off, [t.ap().ap[0]] + dims)

        for dstt, srct in ((bqs, bq_h), (bks, bk_h), (bvs, bv_h),
                           (w1s, w1il_h),
                           (b1s, b1il_h), (bins, bin_h), (b2s, b2_h),
                           (identb, ident_h), (ones1b, ones1_h)):
            dma("sync", lambda e, a=dstt, b=srct: e.dma_start(a[:], b.ap()),
                "d_b")
        vsetup = 0
        for rsrc, rdst in ((bo_h, bor),):
            wait("sync", "vself", vsetup)
            dma("sync", lambda e, b=rsrc: e.dma_start(wstage[0:1, 0:D],
                                                      b.ap()), "d_b")
            vwait("d_b", cnt["d_b"])
            vsetup = vop(lambda e, d=rdst: e.tensor_copy(d[:],
                                                         wstage[0:1, 0:D]))
        vwait("d_b", cnt["d_b"])
        wait("tensor", "d_b", cnt["d_b"])

        Gg = [0]
        ctxa_pair = [0, 0]
        ctxe_pair = [0, 0]
        lastuse = [0, 0, 0, 0]

        def tb_view(b):
            if b == 3:
                return psT[:, 0:512], psT_bf[:, 0:1024]
            lo = 2048 + 512 * b
            return psA[:, lo:lo + 512], psA_bf[:, 2 * lo:2 * lo + 1024]

        def ctx_dst(cp, h):
            if cp == 0:
                c = 2048 + h * 512
                return psA[0:65, c:c + 512]
            return psA[0:65, 3072:3584] if h == 0 else psT[0:65, 0:512]

        def gen_stats_bn(src_tile, c, w=None):
            if w is not None:
                vwait(*w)
            vop(lambda e, c=c, t=src_tile: e.bn_stats(
                bnr[:, 12 * c:12 * c + 6], t[:, c * D:c * D + 512]))
            vop(lambda e, c=c, t=src_tile: e.bn_stats(
                bnr[:, 12 * c + 6:12 * c + 12],
                t[:, c * D + 512:c * D + 1024]))
            return vop(lambda e, c=c: e.bn_aggr(
                strided(agt, c, [[16, 2]]), bnr[:, 12 * c:12 * c + 12]))

        def gen_rstd_batch(c0, n=4):
            ve = vop(lambda e, c=c0, n=n: e.tensor_scalar(
                agt[:, 32 + c:32 + c + n], agt[:, 16 + c:16 + c + n], EPS,
                None, ALU.add))
            awaits("vself", ve)
            aop(lambda e, c=c0, n=n: e.activation(
                agt[:, 64 + c:64 + c + n], agt[:, 32 + c:32 + c + n],
                AF.Ln))
            ma = aop(lambda e, c=c0, n=n: e.activation(
                agt[:, 64 + c:64 + c + n], agt[:, 64 + c:64 + c + n],
                AF.Exp, scale=-0.5))
            vwait("aself", ma)
            vop(lambda e, c=c0, n=n: e.tensor_scalar(
                agt[:, 80 + c:80 + c + n], agt[:, 64 + c:64 + c + n], -1.0,
                None, ALU.mult))
            return vop(lambda e, c=c0, n=n: e.tensor_tensor(
                agt[:, 80 + c:80 + c + n], agt[:, c:c + n],
                agt[:, 80 + c:80 + c + n], ALU.mult))

        def gen_stats_agg():
            n = NTC
            MEAN, VARP = agt[:, 0:n], agt[:, 16:16 + n]
            mv = vop(lambda e: e.tensor_scalar(VARP, VARP, EPS, None,
                                               ALU.add))
            wait("scalar", "vself", mv)
            aop(lambda e: e.activation(agt[:, 64:64 + n], VARP, AF.Ln))
            ma = aop(lambda e: e.activation(agt[:, 64:64 + n],
                                            agt[:, 64:64 + n],
                                            AF.Exp, scale=-0.5))
            vwait("aself", ma)
            vop(lambda e: e.tensor_scalar(agt[:, 80:80 + n],
                                          agt[:, 64:64 + n],
                                          -1.0, None, ALU.mult))
            return vop(lambda e: e.tensor_tensor(agt[:, 80:80 + n], MEAN,
                                                 agt[:, 80:80 + n],
                                                 ALU.mult))

        def gen_stats(src_tile, chunk_waits=None):
            for c in range(NTC):
                gen_stats_bn(src_tile, c,
                             chunk_waits[c] if chunk_waits is not None
                             else None)
            return gen_stats_agg()

        def gen_ln_fb(fb, chunk_waits=None, rm3=None):
            wait("tensor", "aself", ctxa_pair[1])
            wait("tensor", "vself", ctxe_pair[1])
            tposted = []
            rmq = [0]
            for tc in range(NTC):
                if rm3 is None:
                    if tc % 4 == 0:
                        for dc in range(tc, tc + 4):
                            gen_stats_bn(X, dc,
                                         chunk_waits[dc] if chunk_waits
                                         else None)
                        rmq[0] = gen_rstd_batch(tc)
                    rm = rmq[0]
                else:
                    rm = rm3[(tc // 4) * 4]
                slot = (tc % 2) * D
                if chunk_waits is not None:
                    wait("scalar" if tc % 2 == 0 else "vector",
                         *chunk_waits[tc])
                if tc >= 2:
                    eng = "scalar" if tc % 2 == 0 else "vector"
                    wait(eng, "t_xnT", cnt["t_xnT"] - 1)
                if tc % 2 == 0:
                    awaits("vself", rm)
                mks, mk = xapply(tc, xntm[:, slot:slot + D],
                                 X[:, tc * D:tc * D + D],
                                 agt[:, 64 + tc:65 + tc],
                                 agt[:, 80 + tc:81 + tc])
                wait("tensor", mks, mk)
                if tc >= 2:
                    wait("tensor", *tposted[tc - 2])
                warm(3, -1 if tc % 2 == 0 else 3072)
                pdst = psT_bf[:, 0:1024] if tc % 2 == 0 else psA_bf[:, 6144:7168]
                for fc in range(NFC):
                    op("tensor", lambda e, fc=fc, sl=slot, pd=pdst:
                       e.transpose(pd[:, fc * 128:fc * 128 + 128],
                                   xntm[:, sl + fc * 128:sl + fc * 128 + 128],
                                   identb[:]),
                       "t_xnT" if fc == NFC - 1 else None)
                wait("scalar" if tc % 2 == 0 else "vector",
                     "t_xnT", cnt["t_xnT"])
                tposted.append(xcopy(
                    tc, strided(fb, tc * 128, [[S, NFC], [1, 128]]),
                    pdst[:, 0:1024]))
            return (tposted[-2], tposted[-1])

        ysent = 0
        vstg_l = [0]
        wslot = [0]
        wfree = [None, None, None]
        for s in range(NSEQ):
            xr = x_h.rearrange("s (tc p) d -> s p tc d", p=128)
            for q in range(4):
                if s > 0:
                    wait("sync", "y%d" % q, 64 * s)
                dma("sync", lambda e, s=s, q=q: e.dma_start(
                    X[:, q * 4096:(q + 1) * 4096]
                    .rearrange("p (t d) -> p t d", d=D),
                    xr[s, :, q * 4:(q + 1) * 4, :]), "x%d" % q)

            xw = [("x%d" % (c // 4), 16 * (s + 1)) for c in range(NTC)]
            fbe_mark = gen_ln_fb(fbA, xw)

            fstate = {"k": 0, "marks": {}, "bank": {}}

            def stream_qkv_weights(hp, tq_need):
                if wfree[0] is None:
                    wfree[0] = wfree[1] = wfree[2] = vsetup
                wb = (hp % 2) * 3072
                rs = wrun_next()
                slots = [wstage[:, 0:D], wstage[:, D:2 * D], FU[:, 1024:2048]]
                wait("sync", "t_wout", cnt["t_wout"])
                for j, wh in enumerate((wq_h, wk_h, wv_h)):
                    wait("sync", "vself", wfree[j])
                    dma("sync", lambda e, wh=wh, hp=hp, st=slots[j]:
                        e.dma_start(st.rearrange("p (i o) -> p i o", o=128),
                                    wh.ap()[hp]), rs)
                vwait(rs, cnt[rs])
                vwait("t_qkv", tq_need)
                wcast = []
                for j, (wof, scale) in enumerate(((0, sq), (1024, sk),
                                                  (2048, sv))):
                    vstg_l[0] = vop(lambda e, c=wb + wof, sc=scale,
                                    st=slots[j]:
                                    e.tensor_scalar(WB[:, c:c + 1024], st,
                                                    sc, None, ALU.mult))
                    wfree[j] = vstg_l[0]
                    wcast.append(vstg_l[0])
                return wcast

            def make_qkv_fillers(hp, wcast, xnm):
                par = (hp % 2) * PAR
                wb = (hp % 2) * 3072
                groups = []
                vev = [0]
                for j, (wof, bvec, dcol) in enumerate(
                        ((0, bqs, QB), (1024, bks, KB))):
                    for q4 in range(4):
                        def emitA(tb, tbb, j=j, q4=q4, wof=wof, wb=wb,
                                  wc=wcast[j], xnm=xnm):
                            wait("tensor", "vself", wc)
                            for w_ in xnm:
                                wait("tensor", *w_)
                            for ic in range(4):
                                op("tensor",
                                   lambda e, ic=ic, q4=q4, tb=tb, c=wb + wof:
                                   e.matmul(tb[:, 0:512],
                                            WB[:, c + ic * 128:
                                               c + ic * 128 + 128],
                                            fbA[:, ic * S + q4 * 512:
                                                ic * S + q4 * 512 + 512],
                                            start=(ic == 0), stop=False))
                            return None
                        def emitB(tb, tbb, j=j, q4=q4, wof=wof, bvec=bvec,
                                  dcol=dcol, hp=hp, par=par, wb=wb):
                            for ic in range(4, NFC):
                                op("tensor",
                                   lambda e, ic=ic, q4=q4, tb=tb, c=wb + wof:
                                   e.matmul(tb[:, 0:512],
                                            WB[:, c + ic * 128:
                                               c + ic * 128 + 128],
                                            fbA[:, ic * S + q4 * 512:
                                                ic * S + q4 * 512 + 512],
                                            start=False, stop=(ic == NFC - 1)),
                                   "t_qkv" if ic == NFC - 1 else None)
                            vwait("t_qkv", cnt["t_qkv"])
                            return vop(
                                lambda e, q4=q4, tb=tb, bv=bvec, hp=hp,
                                dc=par + dcol:
                                e.tensor_scalar(
                                    UA[:, dc + q4 * 512:dc + q4 * 512 + 512],
                                    tb[:, 0:512], 1.0,
                                    bv[:, hp:hp + 1], ALU.mult, ALU.add))
                        groups.append((emitA, True))
                        groups.append((emitB, False))
                for q4 in range(4):
                    def emitA(tb, tbb, q4=q4, wb=wb, wc=wcast[2], xnm=xnm):
                        wait("tensor", "vself", wc)
                        for w_ in xnm:
                            wait("tensor", *w_)
                        for ic in range(4):
                            op("tensor", lambda e, ic=ic, q4=q4, tb=tb,
                               c=wb + 2048:
                               e.matmul(tb[:, 0:512],
                                        WB[:, c + ic * 128:c + ic * 128 + 128],
                                        fbA[:, ic * S + q4 * 512:
                                            ic * S + q4 * 512 + 512],
                                        start=(ic == 0), stop=False))
                        return None
                    def emitB(tb, tbb, q4=q4, hp=hp, wb=wb, vev=vev):
                        for ic in range(4, NFC):
                            op("tensor", lambda e, ic=ic, q4=q4, tb=tb,
                               c=wb + 2048:
                               e.matmul(tb[:, 0:512],
                                        WB[:, c + ic * 128:c + ic * 128 + 128],
                                        fbA[:, ic * S + q4 * 512:
                                            ic * S + q4 * 512 + 512],
                                        start=False, stop=(ic == NFC - 1)),
                               "t_qkv" if ic == NFC - 1 else None)
                        vwait("t_qkv", cnt["t_qkv"])
                        m = vop(lambda e, q4=q4, tb=tb, hp=hp, par=par:
                                e.tensor_scalar(
                                    UA[:, par + VHbase + q4 * 512:
                                       par + VHbase + q4 * 512 + 512],
                                    tb[:, 0:512], 1.0,
                                    bvs[:, hp:hp + 1], ALU.mult, ALU.add))
                        vev.append(m)
                        return m
                    groups.append((emitA, True))
                    groups.append((emitB, False))
                for th in (1, 0):
                    def emitT(tb, tbb, th=th, par=par, vev=vev):
                        wait("tensor", "vself", vev[-1])
                        for tl in range(8):
                            tc = th * 8 + tl
                            op("tensor", lambda e, tc=tc, tl=tl, tbb=tbb,
                               par=par:
                               e.transpose(tbb[:, tl * 128:tl * 128 + 128],
                                           UA[:, par + VHbase + tc * 128:
                                              par + VHbase + tc * 128 + 128],
                                           identb[:]),
                               "t_qkv" if tl == 7 else None)
                        vwait("t_qkv", cnt["t_qkv"])
                        m = vop(lambda e, th=th, tbb=tbb, par=par:
                                e.tensor_copy(
                                    strided(UA, par + VHbase + th * 8 * 130,
                                            [[130, 8], [65, 2], [1, 64]]),
                                    tbb[:, 0:1024]))
                        if th == 0:
                            m = vop(lambda e, par=par: e.memset(
                                strided(UA, par + VHbase + 64,
                                        [[130, 16], [65, 2], [1, 1]]), 1.0))
                        return m
                    groups.append((emitT, True))
                return groups

            def emit_filler(pend, fpair):
                if not pend:
                    return
                emitf, newb = pend.pop(0)
                if newb:
                    k = fstate["k"]
                    fstate["k"] += 1
                    if k >= 2 and (k - 2) in fstate["marks"]:
                        wait("tensor", "vself", fstate["marks"][k - 2])
                    b = 2 * fpair + (k % 2)
                    fstate["bank"][k] = b
                    wait("tensor", "aself", ctxa_pair[fpair])
                    wait("tensor", "vself", ctxe_pair[fpair])
                k = fstate["k"] - 1
                tb, tbb = tb_view(fstate["bank"][k])
                m = emitf(tb, tbb)
                if m is not None:
                    fstate["marks"][k] = m
                    fstate["last"] = m
                    lastuse[fstate["bank"][k]] = m

            vnorm_l = [0]
            vnorms = []
            ctxe_l = [0]
            ctxa_l = [0]
            fstate["last"] = 0
            eas = {}
            tq_hist = {}
            wcast0 = stream_qkv_weights(0, 0)
            pend0 = make_qkv_fillers(0, wcast0, fbe_mark)
            while pend0:
                emit_filler(pend0, 1)
            tq_hist[0] = cnt["t_qkv"]
            qkv_done = fstate["last"]
            wcast = stream_qkv_weights(1, 0)
            pend = make_qkv_fillers(1, wcast, fbe_mark)

            for hp in range(NHP):
                par = (hp % 2) * PAR

                wait("tensor", "vself", qkv_done)
                pendq = []

                def emit_pv(payload, par=par):
                    gq, k0, klen, pt, cp = payload
                    wait("tensor", "aself", eas[gq])
                    if k0 == 0:
                        wait("tensor", "aself", ctxa_pair[cp])
                        wait("tensor", "vself", ctxe_pair[cp])
                        wait("tensor", "vself", lastuse[2 * cp])
                        wait("tensor", "vself", lastuse[2 * cp + 1])
                    kt = k0
                    for h in range(2):
                        dst = ctx_dst(cp, h)
                        ph = pt + h * 512
                        vh0 = par + VHbase + kt * 130 + h * 65
                        op("tensor", lambda e, d=dst, ph=ph,
                           v0=vh0, kt=kt:
                           e.matmul(d,
                                    UA[:, v0:v0 + 65],
                                    UA[:, ph:ph + 512],
                                    start=(kt == 0), stop=(kt == NKT - 1)),
                           "t_pvd" if h == 1 else None)

                pend_evac = [None]

                def emit_evac():
                    ehp, eqb, first_of_hp = pend_evac[0]
                    pend_evac[0] = None
                    eqt = eqb // QTW
                    cp = eqt % 2
                    awaits("t_pvd", cnt["t_pvd"])
                    srcA = (psA[0:64, 2048:2560] if cp == 0
                            else psA[0:64, 3072:3584])
                    ctxa_pair[cp] = ctxa_l[0] = aop(
                        lambda e, hp=ehp, qb=eqb, s=srcA: e.activation(
                            fbB[0:64, hp * S + qb:hp * S + qb + QTW],
                            s, AF.Identity))
                    vwait("t_pvd", cnt["t_pvd"])
                    if first_of_hp:
                        vwait("d_cb", cnt["d_cb"])
                    srcB = (psA[0:64, 2560:3072] if cp == 0
                            else psT[0:64, 0:512])
                    vop(lambda e, qb=eqb, s=srcB: e.tensor_copy(
                        CU[0:64, qb:qb + QTW], s))
                    vwait("d_sum", cnt["d_sum"])
                    if cp == 0:
                        ctxe_l[0] = vop(lambda e: e.tensor_copy(
                            sstage[64:65, 0:1024], psA[64:65, 2048:3072]))
                    else:
                        vop(lambda e: e.tensor_copy(
                            sstage[64:65, 0:512], psA[64:65, 3072:3584]))
                        ctxe_l[0] = vop(lambda e: e.tensor_copy(
                            sstage[64:65, 512:1024], psT[64:65, 0:512]))
                    ctxe_pair[cp] = ctxe_l[0]
                    wait("gpsimd", "vself", ctxe_l[0])
                    for half in range(2):
                        pa = 2 * eqt + half
                        pb = 8 + 2 * eqt + half
                        dma("gpsimd", lambda e, p=pa, o=half * 256:
                            e.dma_start(FU[p:p + 1, 0:256],
                                        sstage[64:65, o:o + 256]), "d_sum")
                        dma("gpsimd", lambda e, p=pb, o=512 + half * 256:
                            e.dma_start(FU[p:p + 1, 0:256],
                                        sstage[64:65, o:o + 256]), "d_sum")

                for qt in range(NQT):
                    qb = qt * QTW
                    for (k0, klen) in GRPS:
                        g = Gg[0]
                        Gg[0] += 1
                        p = g % 2
                        wait("tensor", "aself", eas.get(g - 2, 0))
                        kt = k0
                        for h in range(2):
                            base = h * 64
                            sc = p * 1024 + h * 512
                            op("tensor", lambda e, kt=kt, b=base, sc=sc,
                               qb=qb, par=par:
                               e.matmul(psA[:, sc:sc + 512],
                                        UA[b:b + 64, par + KB + kt * 128:
                                           par + KB + kt * 128 + 128],
                                        UA[b:b + 64, par + QB + qb:
                                           par + QB + qb + QTW],
                                        start=True, stop=True),
                               "t_sc" if h == 1 else None)
                        if k0 == 2 and pend_evac[0] is not None:
                            emit_evac()
                        if len(pendq) >= 2:
                            emit_pv(pendq.pop(0))
                        if (qt * len(GRPS) + k0 >= 12
                                and (qt == 0 or k0 >= 3)):
                            emit_filler(pend, 1 - qt % 2)
                        wait("scalar", "t_sc", cnt["t_sc"])
                        if g >= 4:
                            wait("scalar", "t_pvd", g - 3)
                        pt = PTbase + (g % 4) * 1024
                        eas[g] = aop(lambda e, pt=pt, p=p: e.activation(
                            UA[:, pt:pt + 1024], psA[:, p * 1024:p * 1024 + 1024],
                            AF.Exp, scale=0.125))
                        pendq.append((g, k0, klen, pt, qt % 2))
                    if pend and not pend[0][1]:
                        emit_filler(pend, 1 - qt % 2)
                    if qt < NQT - 1:
                        pend_evac[0] = (hp, qb, hp > 0 and qt == 0)
                    else:
                        wait("tensor", "aself", cnt["aself"])
                        while pendq:
                            emit_pv(pendq.pop(0))
                        pend_evac[0] = (hp, qb, False)
                        emit_evac()
                while pend:
                    emit_filler(pend, 1)
                if hp + 1 < NHP:
                    tq_hist[hp + 1] = cnt["t_qkv"]
                    qkv_done = fstate["last"]
                if hp + 2 < NHP:
                    wcast = stream_qkv_weights(hp + 2, tq_hist[hp])
                    pend = make_qkv_fillers(hp + 2, wcast, fbe_mark)
                else:
                    pend = []
                if hp == 6:
                    tq_all = cnt["t_qkv"]
                    for icp in range(4):
                        rs = wrun_next()
                        for sl in range(2):
                            ic = icp * 2 + sl
                            wait("sync", "vself", wfree[sl])
                            dma("sync", lambda e, ic=ic, sl=sl: e.dma_start(
                                wstage[:, sl * D:sl * D + D],
                                wo_h.ap()[ic * 128:ic * 128 + 128, :]), rs)
                        vwait(rs, cnt[rs])
                        vwait("t_qkv", tq_all)
                        for sl in range(2):
                            ic = icp * 2 + sl
                            vstg_l[0] = vop(
                                lambda e, ic=ic, sl=sl: e.tensor_scalar(
                                    WB[:, ic * 1024:ic * 1024 + 1024],
                                    wstage[:, sl * D:sl * D + D], so, None,
                                    ALU.mult))
                            wfree[sl] = vstg_l[0]


                wait("gpsimd", "vself", ctxe_l[0])
                dma("gpsimd", lambda e, hp=hp: e.dma_start(
                    fbB[64:128, hp * S:hp * S + S], CU[0:64, :]), "d_cb")

                vwait("d_sum", cnt["d_sum"])

                def _recip(e):
                    with nc.allow_low_precision(
                            reason="softmax 1/sum consumed as bf16 scale"):
                        return e.reciprocal(recb[0:16, 0:256], FU[0:16, 0:256])
                vrec = vop(_recip)
                wait("gpsimd", "vself", vrec)
                dma("gpsimd", lambda e, hp=hp: e.dma_start(
                    recd_h.ap()[16 * hp:16 * hp + 16, :],
                    recb[0:16, 0:256]), "d_rc")
                wait("gpsimd", "d_rc", cnt["d_rc"])
                if hp > 0:
                    wait("gpsimd", "vself", vnorm_l[0])
                for h in range(2):
                    rsrc = bass.AP(recd_h.ap().tensor,
                                   (16 * hp + 8 * h) * 256,
                                   [[0, 64], [256, 8], [1, 256]])
                    dma("gpsimd", lambda e, h=h, rsrc=rsrc: e.dma_start(
                        xntm[h * 64:h * 64 + 64, :]
                        .rearrange("p (b j) -> p b j", j=256), rsrc), "d_re")
                vwait("d_re", cnt["d_re"])
                vwait("d_cb", cnt["d_cb"])
                vwait("aself", ctxa_l[0])
                vnorm_l[0] = vop(lambda e, hp=hp: e.tensor_tensor(
                    fbB[:, hp * S:hp * S + S], fbB[:, hp * S:hp * S + S],
                    xntm[:, :], ALU.mult))
                vnorms.append(vnorm_l[0])
            vnorm = vnorm_l[0]

            wait("tensor", "vself", vstg_l[0])
            vwait("d_rc", cnt["d_rc"])
            woe, adds, rm3 = [], [], {}
            for tc in range(NTC):
                bank = (tc % 3) * 1024
                if tc >= 3:
                    wait("tensor", *woe[tc - 3])
                for half in range(2):
                    for ic in range(NFC):
                        if tc == 0 and half == 0:
                            wait("tensor", "vself", vnorms[ic])
                        op("tensor", lambda e, tc=tc, ic=ic, h=half, b=bank:
                           e.matmul(psA[:, b + h * 512:b + h * 512 + 512],
                                    fbB[:, ic * S + tc * 128:
                                        ic * S + tc * 128 + 128],
                                    WB[:, ic * 1024 + h * 512:
                                       ic * 1024 + h * 512 + 512],
                                    start=(ic == 0), stop=False))
                    op("tensor", lambda e, h=half, b=bank:
                       e.matmul(psA[:, b + h * 512:b + h * 512 + 512],
                                ones1b[:], bor[0:1, h * 512:h * 512 + 512],
                                start=False, stop=True),
                       "t_wo" if half == 1 else None)
                wait("scalar" if tc % 2 == 0 else "vector",
                     "t_wo", cnt["t_wo"])
                woe.append(xcopy(tc, fbA[:, tc * 1024:tc * 1024 + 1024],
                                 psA[:, bank:bank + 1024]))
                gen_stats_bn(fbA, tc, woe[tc])
                if tc % 4 != 3:
                    continue
                rmb = gen_rstd_batch(tc - 3)
                for dc in range(tc - 3, tc + 1):
                    sl = (dc % 2) * D
                    warm(2, 3072)
                    if dc >= 2:
                        eng = "scalar" if dc % 2 == 0 else "vector"
                        wait(eng, "vself", adds[dc - 2])
                    if dc % 2 == 0:
                        awaits("vself", rmb)
                    mks, mk = xapply(dc, xntm[:, sl:sl + D],
                                     fbA[:, dc * D:dc * D + D],
                                     agt[:, 64 + dc:65 + dc],
                                     agt[:, 80 + dc:81 + dc])
                    vwait(mks, mk)
                    adds.append(vop(lambda e, dc=dc, sl=sl: e.tensor_tensor(
                        X[:, dc * D:dc * D + D], X[:, dc * D:dc * D + D],
                        xntm[:, sl:sl + D], ALU.add)))
                    gen_stats_bn(X, dc)
                rm3[tc - 3] = gen_rstd_batch(tc - 3)
            wait("sync", "t_wo", cnt["t_wo"])
            rs_e = wrun_next()
            dma("sync", lambda e: e.dma_start(WB[0:64, 1024:2048], e_h.ap()),
                rs_e)
            tw_all = cnt["t_wo"]
            ffw = [(win_h, slice(0, 512), 64, 512, None),
                   (w2bd_h, slice(512, 1024), 64, 512, None),
                   (wout_h, None, None, None, "wout"),
                   (bout_h, None, None, None, "bout")]
            for srct, cols, oo, ww, special in ffw:
                sl = wslot[0] % 2
                wslot[0] += 1
                rs = wrun_next()
                wait("sync", "vself", wfree[sl])
                if special is None:
                    dma("sync", lambda e, srct=srct, sl=sl: e.dma_start(
                        wstage[:, sl * D:sl * D + 512]
                        .rearrange("p (i o) -> p i o", o=64),
                        srct.ap()), rs)
                elif special == "wout":
                    dma("sync", lambda e, sl=sl: e.dma_start(
                        wstage[0:64, sl * D:sl * D + D], wout_h.ap()), rs)
                else:
                    dma("sync", lambda e, sl=sl: e.dma_start(
                        wstage[0:1, sl * D:sl * D + D], bout_h.ap()), rs)
                vwait(rs, cnt[rs])
                vwait("t_wo", tw_all)
                if special is None:
                    vstg_l[0] = vop(lambda e, c=cols, sl=sl: e.tensor_copy(
                        WB[:, c], wstage[:, sl * D:sl * D + 512]))
                elif special == "wout":
                    vstg_l[0] = vop(lambda e, sl=sl: e.tensor_copy(
                        WB[0:64, 2048:3072], wstage[0:64, sl * D:sl * D + D]))
                else:
                    vstg_l[0] = vop(lambda e, sl=sl: e.tensor_copy(
                        WB[64:65, 2048:3072], wstage[0:1, sl * D:sl * D + D]))
                wfree[sl] = vstg_l[0]

            fbe_mark = gen_ln_fb(fbB, rm3=rm3)

            wait("sync", "t_wo", cnt["t_wo"])
            wait("tensor", "vself", vstg_l[0])
            for w_ in fbe_mark:
                wait("tensor", *w_)
            wait("tensor", rs_e, cnt[rs_e])
            sfbe = []
            for q4 in range(4):
                if q4 >= 1:
                    wait("tensor", "vself", sfbe[-1])
                for ic in range(NFC):
                    op("tensor", lambda e, ic=ic, q4=q4:
                       e.matmul(psT[0:64, 0:512],
                                WB[:, ic * 64:ic * 64 + 64],
                                fbB[:, ic * S + q4 * 512:
                                    ic * S + q4 * 512 + 512],
                                start=(ic == 0), stop=(ic == NFC - 1)),
                       "t_ffn" if ic == NFC - 1 else None)
                vwait("t_ffn", cnt["t_ffn"])
                sfbe.append(vop(lambda e, q4=q4: e.tensor_scalar(
                    FUb[0:64, q4 * 512:q4 * 512 + 512], psT[0:64, 0:512],
                    1.0, bins[0:64, 0:1], ALU.mult, ALU.add)))
            wait("tensor", "vself", sfbe[-1])
            for fc in range(NFC):
                if fc >= 1:
                    wait("tensor", "aself", cnt["aself"])
                for q4 in range(4):
                    op("tensor", lambda e, fc=fc, q4=q4:
                       e.matmul(psA[:, q4 * 512:q4 * 512 + 512],
                                WB[0:64, 1024 + fc * 128:1024 + fc * 128 + 128],
                                FUb[0:64, q4 * 512:q4 * 512 + 512],
                                start=True, stop=True),
                       "t_ffn" if q4 == 3 else None)
                wait("scalar", "t_ffn", cnt["t_ffn"])
                aop(lambda e, fc=fc: e.activation(
                    fbA[:, fc * S:fc * S + S], psA[:, 0:2048], AF.Relu,
                    bias=b1s[:, fc:fc + 1], scale=w1s[:, fc:fc + 1]))
            wait("tensor", "aself", cnt["aself"])
            gfbe = []
            for q4 in range(4):
                if q4 >= 1:
                    wait("tensor", "vself", gfbe[-1])
                for fc in range(NFC):
                    op("tensor", lambda e, fc=fc, q4=q4:
                       e.matmul(psT[0:64, 0:512],
                                WB[:, 512 + fc * 64:512 + fc * 64 + 64],
                                fbA[:, fc * S + q4 * 512:
                                    fc * S + q4 * 512 + 512],
                                start=(fc == 0), stop=(fc == NFC - 1)),
                       "t_g" if fc == NFC - 1 else None)
                vwait("t_g", cnt["t_g"])
                gfbe.append(vop(lambda e, q4=q4: e.tensor_scalar(
                    FUb[0:64, S + q4 * 512:S + q4 * 512 + 512],
                    psT[0:64, 0:512],
                    1.0, b2s[0:64, 0:1], ALU.mult, ALU.add)))
            gfbe.append(vop(lambda e: e.memset(FUb[64:65, S:2 * S], 1.0)))
            wait("tensor", "vself", gfbe[-1])
            woe, resid = [], []
            for tc in range(NTC):
                bank = (tc % 3) * 1024
                if tc >= 3:
                    wait("tensor", *woe[tc - 3])
                for half in range(2):
                    op("tensor", lambda e, tc=tc, h=half, b=bank:
                       e.matmul(psA[:, b + h * 512:b + h * 512 + 512],
                                FUb[0:65, S + tc * 128:S + tc * 128 + 128],
                                WB[0:65, 2048 + h * 512:2048 + h * 512 + 512],
                                start=True, stop=True),
                       "t_wout" if half == 1 else None)
                wait("scalar" if tc % 2 == 0 else "vector",
                     "t_wout", cnt["t_wout"])
                woe.append(xcopy(tc, UA[:, tc * 1024:tc * 1024 + 1024],
                                 psA[:, bank:bank + 1024]))
                gen_stats_bn(UA, tc, woe[tc])
                if tc % 4 != 3:
                    continue
                rmb = gen_rstd_batch(tc - 3)
                for dc in range(tc - 3, tc + 1):
                    sl = (dc % 2) * D
                    warm(2, 3072)
                    if dc >= 2:
                        eng = "scalar" if dc % 2 == 0 else "vector"
                        wait(eng, "vself", resid[dc - 2])
                    if dc % 2 == 0:
                        awaits("vself", rmb)
                    mks, mk = xapply(dc, xntm[:, sl:sl + D],
                                     UA[:, dc * D:dc * D + D],
                                     agt[:, 64 + dc:65 + dc],
                                     agt[:, 80 + dc:81 + dc])
                    vwait(mks, mk)
                    resid.append(vop(lambda e, dc=dc, sl=sl: e.tensor_tensor(
                        X[:, dc * D:dc * D + D], X[:, dc * D:dc * D + D],
                        xntm[:, sl:sl + D], ALU.add)))

            for tc in range(NTC):
                wait("sync", "vself", resid[tc])
                dma("sync", lambda e, s=s, tc=tc: e.dma_start(
                    y_h.ap()[s, tc * 128:tc * 128 + 128, :],
                    X[:, tc * D:tc * D + D]), "y%d" % (tc // 4))
            ysent += NTC

        for q in range(4):
            wait("sync", "y%d" % q, cnt["y%d" % q])

        blk = st.enter_context(nc.Block())

        @blk.tensor
        def _(e):
            for f in prog["tensor"]:
                f(e)

        @blk.vector
        def _(e):
            for f in prog["vector"]:
                f(e)

        @blk.scalar
        def _(e):
            for f in prog["scalar"]:
                f(e)

        @blk.sync
        def _(e):
            for f in prog["sync"]:
                f(e)

        @blk.gpsimd
        def _(e):
            for f in prog["gpsimd"]:
                f(e)

    return nc


def kernel(**inputs):
    g = lambda k: np.asarray(inputs[k])
    x = g("L0_attn_x")
    sq = float(np.asarray(inputs["L0_attn_s_q"]))
    sk = float(np.asarray(inputs["L0_attn_s_k"]))
    sv = float(np.asarray(inputs["L0_attn_s_v"]))
    so = float(np.asarray(inputs["L0_attn_s_o"]))

    def wlayout(w):
        wT = np.ascontiguousarray(w.T)
        return np.ascontiguousarray(
            wT.reshape(NFC, 128, NHP, 128).transpose(2, 1, 0, 3))

    wq = wlayout(g("L0_attn_w_q"))
    wk = wlayout(g("L0_attn_w_k"))
    wv = wlayout(g("L0_attn_w_v"))
    wo = np.ascontiguousarray(g("L0_attn_w_o").T)
    bq = np.ascontiguousarray(g("L0_attn_b_q").reshape(NHP, 128).T)
    bk = np.ascontiguousarray(g("L0_attn_b_k").reshape(NHP, 128).T)
    bv = np.ascontiguousarray(g("L0_attn_b_v").reshape(NHP, 128).T)
    bo = g("L0_attn_b_o").reshape(1, D)
    Win = g("L0_ffn_Win")
    win = np.ascontiguousarray(Win.reshape(NFC, 128, 64).transpose(1, 0, 2))
    W2 = g("L0_ffn_W2")
    w2bd_full = np.zeros((D, 64), np.float32)
    for m in range(64):
        w2bd_full[m * 16:(m + 1) * 16, m] = W2[m]
    w2bd = np.ascontiguousarray(
        w2bd_full.reshape(NFC, 128, 64).transpose(1, 0, 2))
    em = np.zeros((64, D), np.float32)
    for m in range(64):
        em[m, m * 16:(m + 1) * 16] = 1.0
    em = em.astype(BF)
    wout = np.ascontiguousarray(g("L0_ffn_Wout"))
    bout = g("L0_ffn_bout").reshape(1, D)
    binv = g("L0_ffn_bin_").reshape(64, 1)
    b2v = g("L0_ffn_b2").reshape(64, 1)
    w1il = np.ascontiguousarray(g("L0_ffn_W1").reshape(NFC, 128).T)
    b1il = np.ascontiguousarray(g("L0_ffn_b1").reshape(NFC, 128).T)
    ident = np.eye(128, dtype=np.float32).astype(BF)
    ones1 = np.ones((1, 128), np.float32).astype(BF)

    nc = build_program(sq, sk, sv, so)
    shared = dict(wq=wq, wk=wk, wv=wv, wo=wo, bq=bq, bk=bk, bv=bv, bo=bo,
                  win=win, w2bd=w2bd, em=em, wout=wout, bout=bout, binv=binv,
                  b2v=b2v, w1il=w1il, b1il=b1il, ident=ident, ones1=ones1)
    in_maps = []
    for c in range(8):
        m = dict(shared)
        m["x"] = np.ascontiguousarray(x[2 * c:2 * c + 2])
        in_maps.append(m)
    res = run_bass_kernel_spmd(nc, in_maps, core_ids=list(range(8)))
    out = np.empty((16, S, D), np.float32)
    for c in range(8):
        out[2 * c:2 * c + 2] = res.results[c]["y"]
    return out


# version 30
# speedup vs baseline: 1.0857x; 1.0140x over previous
import numpy as np
import ml_dtypes
from contextlib import ExitStack

import concourse.bass as bass
import concourse.mybir as mybir
from concourse.bass_utils import run_bass_kernel_spmd

F32 = mybir.dt.float32
BF16 = mybir.dt.bfloat16
AF = mybir.ActivationFunctionType
ALU = mybir.AluOpType
BF = ml_dtypes.bfloat16

S = 2048
D = 1024
NTC = 16
NFC = 8
NHP = 8
NQT = 4
QTW = 512
NKT = 16
GRPS = [(i, 1) for i in range(16)]
NSEQ = 2
EPS = 1e-5

ENGINES = ("tensor", "vector", "scalar", "sync", "gpsimd")


def build_program(sq, sk, sv, so):
    nc = bass.Bass("TRN2", target_bir_lowering=False)

    x_h = nc.dram_tensor("x", [NSEQ, S, D], F32, kind="ExternalInput")
    y_h = nc.dram_tensor("y", [NSEQ, S, D], F32, kind="ExternalOutput")
    wq_h = nc.dram_tensor("wq", [NHP, 128, NFC, 128], F32, kind="ExternalInput")
    wk_h = nc.dram_tensor("wk", [NHP, 128, NFC, 128], F32, kind="ExternalInput")
    wv_h = nc.dram_tensor("wv", [NHP, 128, NFC, 128], F32, kind="ExternalInput")
    wo_h = nc.dram_tensor("wo", [D, D], F32, kind="ExternalInput")
    bq_h = nc.dram_tensor("bq", [128, NHP], F32, kind="ExternalInput")
    bk_h = nc.dram_tensor("bk", [128, NHP], F32, kind="ExternalInput")
    bv_h = nc.dram_tensor("bv", [128, NHP], F32, kind="ExternalInput")
    bo_h = nc.dram_tensor("bo", [1, D], F32, kind="ExternalInput")
    win_h = nc.dram_tensor("win", [128, NFC, 64], F32, kind="ExternalInput")
    w2bd_h = nc.dram_tensor("w2bd", [128, NFC, 64], F32, kind="ExternalInput")
    e_h = nc.dram_tensor("em", [64, D], BF16, kind="ExternalInput")
    wout_h = nc.dram_tensor("wout", [64, D], F32, kind="ExternalInput")
    bout_h = nc.dram_tensor("bout", [1, D], F32, kind="ExternalInput")
    bin_h = nc.dram_tensor("binv", [64, 1], F32, kind="ExternalInput")
    b2_h = nc.dram_tensor("b2v", [64, 1], F32, kind="ExternalInput")
    w1il_h = nc.dram_tensor("w1il", [128, NFC], F32, kind="ExternalInput")
    b1il_h = nc.dram_tensor("b1il", [128, NFC], F32, kind="ExternalInput")
    ident_h = nc.dram_tensor("ident", [128, 128], BF16, kind="ExternalInput")
    ones1_h = nc.dram_tensor("ones1", [1, 128], BF16, kind="ExternalInput")
    recd_h = nc.dram_tensor("recd", [128, 256], BF16, kind="Internal")

    st = ExitStack()
    with st:
        sb = lambda n, sh, dt: st.enter_context(nc.sbuf_tensor(n, sh, dt))
        X = sb("X", [128, NTC * D], F32)
        fbA = sb("fbA", [128, NFC * S], BF16)
        fbB = sb("fbB", [128, NFC * S], BF16)
        UA = sb("UA", [128, 16448], BF16)
        xntm = sb("xntm", [128, 2 * D], BF16)
        wstage = sb("wstage", [128, 2 * D], F32)
        WB = sb("WB", [128, 8192], BF16)
        FU = sb("FU", [128, S], F32)
        FUb = FU.bitcast(BF16)
        CU = sb("CU", [128, S], BF16)
        sstage = CU.bitcast(F32)
        bnr = sb("bnr", [128, 192], F32)
        recb = bnr.bitcast(BF16)
        agt = sb("agt", [128, 96], F32)
        bqs = sb("bqs", [128, NHP], F32)
        bks = sb("bks", [128, NHP], F32)
        w1s = sb("w1s", [128, NFC], F32)
        b1s = sb("b1s", [128, NFC], F32)
        bins = sb("bins", [64, 1], F32)
        b2s = sb("b2s", [64, 1], F32)
        bvs = sb("bvs", [128, NHP], F32)
        bor = sb("bor", [1, D], BF16)
        identb = sb("identb", [128, 128], BF16)
        ones1b = sb("ones1b", [1, 128], BF16)

        psA = st.enter_context(nc.psum_tensor("psA", [128, 3584], F32))
        psT = st.enter_context(nc.psum_tensor("psT", [128, 512], F32))
        psT_bf = psT.bitcast(BF16)
        psA_bf = psA.bitcast(BF16)

        PAR = 6176
        QB = 0
        KB = 2048
        VHbase = 4096
        PTbase = 12352

        sem_names = ["vself", "aself", "gself",
                     "d_x", "d_w", "d_b", "d_y", "d_sum", "d_cb", "d_rc", "d_re",
                     "w0", "w1", "w2", "w3",
                     "x0", "x1", "x2", "x3", "y0", "y1", "y2", "y3",
                     "t_xnT", "t_qkv", "t_sc", "t_pvd", "t_wo", "t_ffn", "t_g",
                     "t_wout"]
        sems = {n: st.enter_context(nc.semaphore(n)) for n in sem_names}
        cnt = {n: 0 for n in sem_names}
        prog = {k: [] for k in ENGINES}

        def emit(eng, fn):
            prog[eng].append(fn)

        waited = {e: {} for e in ENGINES}

        def wait(eng, s, v):
            if v <= 0 or waited[eng].get(s, 0) >= v:
                return
            waited[eng][s] = v
            emit(eng, lambda e, _s=sems[s], _v=v: e.wait_ge(_s, _v))

        def op(eng, fn, inc=None):
            if inc is not None:
                cnt[inc] += 1

            def run(e, _fn=fn, _inc=inc):
                ins = _fn(e)
                if _inc is not None:
                    ins.then_inc(sems[_inc], 1)
            emit(eng, run)

        def dma(eng, fn, dsem):
            cnt[dsem] += 16

            def run(e, _fn=fn, _s=sems[dsem]):
                _fn(e).then_inc(_s, 16)
            emit(eng, run)

        def vop(fn):
            wait("vector", "vself", cnt["vself"])
            op("vector", fn, "vself")
            return cnt["vself"]

        def aop(fn):
            wait("scalar", "aself", cnt["aself"])
            op("scalar", fn, "aself")
            return cnt["aself"]

        wrun = [0]

        def wrun_next():
            wrun[0] += 1
            return "w%d" % (wrun[0] % 4)

        def gop(fn):
            wait("gpsimd", "gself", cnt["gself"])
            op("gpsimd", fn, "gself")
            return cnt["gself"]

        def vwait(s, v):
            wait("vector", s, v)

        def awaits(s, v):
            wait("scalar", s, v)

        def xcopy(tc, dst, srcp):
            if tc % 2 == 0:
                return ("aself", aop(lambda e: e.activation(dst, srcp,
                                                            AF.Identity)))
            return ("vself", vop(lambda e: e.tensor_copy(dst, srcp)))

        def xapply(tc, dst, srct, rcol, ncol):
            if tc % 2 == 0:
                return ("aself", aop(lambda e: e.activation(
                    dst, srct, AF.Identity, bias=ncol, scale=rcol)))
            return ("vself", vop(lambda e: e.tensor_scalar(
                dst, srct, rcol, ncol, ALU.mult, ALU.add)))

        def warm(n):
            for _ in range(n):
                op("tensor", lambda e: e.matmul(
                    psA[:, 2048:2176], identb[:], identb[:],
                    start=True, stop=True))

        def strided(t, off, dims):
            return bass.AP(t.ap().tensor, off, [t.ap().ap[0]] + dims)

        for dstt, srct in ((bqs, bq_h), (bks, bk_h), (bvs, bv_h),
                           (w1s, w1il_h),
                           (b1s, b1il_h), (bins, bin_h), (b2s, b2_h),
                           (identb, ident_h), (ones1b, ones1_h)):
            dma("sync", lambda e, a=dstt, b=srct: e.dma_start(a[:], b.ap()),
                "d_b")
        vsetup = 0
        for rsrc, rdst in ((bo_h, bor),):
            wait("sync", "vself", vsetup)
            dma("sync", lambda e, b=rsrc: e.dma_start(wstage[0:1, 0:D],
                                                      b.ap()), "d_b")
            vwait("d_b", cnt["d_b"])
            vsetup = vop(lambda e, d=rdst: e.tensor_copy(d[:],
                                                         wstage[0:1, 0:D]))
        vwait("d_b", cnt["d_b"])
        wait("tensor", "d_b", cnt["d_b"])

        Gg = [0]
        ctxa_pair = [0, 0]
        ctxe_pair = [0, 0]
        lastuse = [0, 0, 0, 0]

        def tb_view(b):
            if b == 3:
                return psT[:, 0:512], psT_bf[:, 0:1024]
            lo = 2048 + 512 * b
            return psA[:, lo:lo + 512], psA_bf[:, 2 * lo:2 * lo + 1024]

        def ctx_dst(cp, h):
            if cp == 0:
                c = 2048 + h * 512
                return psA[0:65, c:c + 512]
            return psA[0:65, 3072:3584] if h == 0 else psT[0:65, 0:512]

        def gen_stats_bn(src_tile, c, w=None):
            if w is not None:
                vwait(*w)
            vop(lambda e, c=c, t=src_tile: e.bn_stats(
                bnr[:, 12 * c:12 * c + 6], t[:, c * D:c * D + 512]))
            vop(lambda e, c=c, t=src_tile: e.bn_stats(
                bnr[:, 12 * c + 6:12 * c + 12],
                t[:, c * D + 512:c * D + 1024]))
            vop(lambda e, c=c: e.bn_aggr(
                strided(agt, c, [[16, 2]]), bnr[:, 12 * c:12 * c + 12]))

        def gen_stats_agg():
            n = NTC
            MEAN, VARP = agt[:, 0:n], agt[:, 16:16 + n]
            mv = vop(lambda e: e.tensor_scalar(VARP, VARP, EPS, None,
                                               ALU.add))
            wait("scalar", "vself", mv)
            aop(lambda e: e.activation(agt[:, 64:64 + n], VARP, AF.Ln))
            ma = aop(lambda e: e.activation(agt[:, 64:64 + n],
                                            agt[:, 64:64 + n],
                                            AF.Exp, scale=-0.5))
            vwait("aself", ma)
            vop(lambda e: e.tensor_scalar(agt[:, 80:80 + n],
                                          agt[:, 64:64 + n],
                                          -1.0, None, ALU.mult))
            return vop(lambda e: e.tensor_tensor(agt[:, 80:80 + n], MEAN,
                                                 agt[:, 80:80 + n],
                                                 ALU.mult))

        def gen_stats(src_tile, chunk_waits=None):
            for c in range(NTC):
                gen_stats_bn(src_tile, c,
                             chunk_waits[c] if chunk_waits is not None
                             else None)
            return gen_stats_agg()

        def gen_ln_fb(fb, chunk_waits=None, pre=False):
            stm = gen_stats_agg() if pre else gen_stats(X, chunk_waits)
            awaits("vself", stm)
            vwait("vself", stm)
            wait("tensor", "aself", ctxa_pair[1])
            wait("tensor", "vself", ctxe_pair[1])
            tposted = []
            for tc in range(NTC):
                slot = (tc % 2) * D
                if chunk_waits is not None:
                    wait("scalar" if tc % 2 == 0 else "vector",
                         *chunk_waits[tc])
                if tc >= 2:
                    eng = "scalar" if tc % 2 == 0 else "vector"
                    wait(eng, "t_xnT", cnt["t_xnT"] - 1)
                mks, mk = xapply(tc, xntm[:, slot:slot + D],
                                 X[:, tc * D:tc * D + D],
                                 agt[:, 64 + tc:65 + tc],
                                 agt[:, 80 + tc:81 + tc])
                warm(3)
                wait("tensor", mks, mk)
                if tc >= 2:
                    wait("tensor", *tposted[tc - 2])
                pdst = psT_bf[:, 0:1024] if tc % 2 == 0 else psA_bf[:, 6144:7168]
                for fc in range(NFC):
                    op("tensor", lambda e, fc=fc, sl=slot, pd=pdst:
                       e.transpose(pd[:, fc * 128:fc * 128 + 128],
                                   xntm[:, sl + fc * 128:sl + fc * 128 + 128],
                                   identb[:]),
                       "t_xnT" if fc == NFC - 1 else None)
                wait("scalar" if tc % 2 == 0 else "vector",
                     "t_xnT", cnt["t_xnT"])
                tposted.append(xcopy(
                    tc, strided(fb, tc * 128, [[S, NFC], [1, 128]]),
                    pdst[:, 0:1024]))
            return (tposted[-2], tposted[-1])

        ysent = 0
        vstg_l = [("vself", 0)]
        wslot = [0]
        wfree = [None, None, None]
        for s in range(NSEQ):
            xr = x_h.rearrange("s (tc p) d -> s p tc d", p=128)
            for q in range(4):
                if s > 0:
                    wait("sync", "y%d" % q, 64 * s)
                dma("sync", lambda e, s=s, q=q: e.dma_start(
                    X[:, q * 4096:(q + 1) * 4096]
                    .rearrange("p (t d) -> p t d", d=D),
                    xr[s, :, q * 4:(q + 1) * 4, :]), "x%d" % q)

            xw = [("x%d" % (c // 4), 16 * (s + 1)) for c in range(NTC)]
            fbe_mark = gen_ln_fb(fbA, xw)

            fstate = {"k": 0, "marks": {}, "bank": {}}

            def stream_qkv_weights(hp, tq_need, eng="vector"):
                if wfree[0] is None:
                    wfree[0] = wfree[1] = wfree[2] = ("vself", vsetup)
                wb = (hp % 2) * 3072
                rs = wrun_next()
                slots = [wstage[:, 0:D], wstage[:, D:2 * D], FU[:, 1024:2048]]
                wait("sync", "t_wout", cnt["t_wout"])
                for j, wh in enumerate((wq_h, wk_h, wv_h)):
                    wait("sync", *wfree[j])
                    dma("sync", lambda e, wh=wh, hp=hp, st=slots[j]:
                        e.dma_start(st.rearrange("p (i o) -> p i o", o=128),
                                    wh.ap()[hp]), rs)
                w8 = vwait if eng == "vector" else awaits
                w8(rs, cnt[rs])
                w8("t_qkv", tq_need)
                wcast = []
                for j, (wof, scale) in enumerate(((0, sq), (1024, sk),
                                                  (2048, sv))):
                    if eng == "vector":
                        m = ("vself", vop(
                            lambda e, c=wb + wof, sc=scale, st=slots[j]:
                            e.tensor_scalar(WB[:, c:c + 1024], st,
                                            sc, None, ALU.mult)))
                    else:
                        m = ("aself", aop(
                            lambda e, c=wb + wof, sc=scale, st=slots[j]:
                            e.activation(WB[:, c:c + 1024], st,
                                         AF.Identity, scale=sc)))
                    vstg_l[0] = wfree[j] = m
                    wcast.append(m)
                return wcast

            def make_qkv_fillers(hp, wcast, xnm):
                par = (hp % 2) * PAR
                wb = (hp % 2) * 3072
                groups = []
                vev = [0]
                for j, (wof, bvec, dcol) in enumerate(
                        ((0, bqs, QB), (1024, bks, KB))):
                    for q4 in range(4):
                        def emitA(tb, tbb, j=j, q4=q4, wof=wof, wb=wb,
                                  wc=wcast[j], xnm=xnm):
                            wait("tensor", *wc)
                            for w_ in xnm:
                                wait("tensor", *w_)
                            for ic in range(4):
                                op("tensor",
                                   lambda e, ic=ic, q4=q4, tb=tb, c=wb + wof:
                                   e.matmul(tb[:, 0:512],
                                            WB[:, c + ic * 128:
                                               c + ic * 128 + 128],
                                            fbA[:, ic * S + q4 * 512:
                                                ic * S + q4 * 512 + 512],
                                            start=(ic == 0), stop=False))
                            return None
                        def emitB(tb, tbb, j=j, q4=q4, wof=wof, bvec=bvec,
                                  dcol=dcol, hp=hp, par=par, wb=wb):
                            for ic in range(4, NFC):
                                op("tensor",
                                   lambda e, ic=ic, q4=q4, tb=tb, c=wb + wof:
                                   e.matmul(tb[:, 0:512],
                                            WB[:, c + ic * 128:
                                               c + ic * 128 + 128],
                                            fbA[:, ic * S + q4 * 512:
                                                ic * S + q4 * 512 + 512],
                                            start=False, stop=(ic == NFC - 1)),
                                   "t_qkv" if ic == NFC - 1 else None)
                            vwait("t_qkv", cnt["t_qkv"])
                            return vop(
                                lambda e, q4=q4, tb=tb, bv=bvec, hp=hp,
                                dc=par + dcol:
                                e.tensor_scalar(
                                    UA[:, dc + q4 * 512:dc + q4 * 512 + 512],
                                    tb[:, 0:512], 1.0,
                                    bv[:, hp:hp + 1], ALU.mult, ALU.add))
                        groups.append((emitA, True))
                        groups.append((emitB, False))
                for q4 in range(4):
                    def emitA(tb, tbb, q4=q4, wb=wb, wc=wcast[2], xnm=xnm):
                        wait("tensor", *wc)
                        for w_ in xnm:
                            wait("tensor", *w_)
                        for ic in range(4):
                            op("tensor", lambda e, ic=ic, q4=q4, tb=tb,
                               c=wb + 2048:
                               e.matmul(tb[:, 0:512],
                                        WB[:, c + ic * 128:c + ic * 128 + 128],
                                        fbA[:, ic * S + q4 * 512:
                                            ic * S + q4 * 512 + 512],
                                        start=(ic == 0), stop=False))
                        return None
                    def emitB(tb, tbb, q4=q4, hp=hp, wb=wb, vev=vev):
                        for ic in range(4, NFC):
                            op("tensor", lambda e, ic=ic, q4=q4, tb=tb,
                               c=wb + 2048:
                               e.matmul(tb[:, 0:512],
                                        WB[:, c + ic * 128:c + ic * 128 + 128],
                                        fbA[:, ic * S + q4 * 512:
                                            ic * S + q4 * 512 + 512],
                                        start=False, stop=(ic == NFC - 1)),
                               "t_qkv" if ic == NFC - 1 else None)
                        vwait("t_qkv", cnt["t_qkv"])
                        m = vop(lambda e, q4=q4, tb=tb, hp=hp, par=par:
                                e.tensor_scalar(
                                    UA[:, par + VHbase + q4 * 512:
                                       par + VHbase + q4 * 512 + 512],
                                    tb[:, 0:512], 1.0,
                                    bvs[:, hp:hp + 1], ALU.mult, ALU.add))
                        vev.append(m)
                        return m
                    groups.append((emitA, True))
                    groups.append((emitB, False))
                for th in (1, 0):
                    def emitT(tb, tbb, th=th, par=par, vev=vev):
                        wait("tensor", "vself", vev[-1])
                        for tl in range(8):
                            tc = th * 8 + tl
                            op("tensor", lambda e, tc=tc, tl=tl, tbb=tbb,
                               par=par:
                               e.transpose(tbb[:, tl * 128:tl * 128 + 128],
                                           UA[:, par + VHbase + tc * 128:
                                              par + VHbase + tc * 128 + 128],
                                           identb[:]),
                               "t_qkv" if tl == 7 else None)
                        vwait("t_qkv", cnt["t_qkv"])
                        m = vop(lambda e, th=th, tbb=tbb, par=par:
                                e.tensor_copy(
                                    strided(UA, par + VHbase + th * 8 * 130,
                                            [[130, 8], [65, 2], [1, 64]]),
                                    tbb[:, 0:1024]))
                        if th == 0:
                            m = vop(lambda e, par=par: e.memset(
                                strided(UA, par + VHbase + 64,
                                        [[130, 16], [65, 2], [1, 1]]), 1.0))
                        return m
                    groups.append((emitT, True))
                return groups

            def emit_filler(pend, fpair):
                if not pend:
                    return
                emitf, newb = pend.pop(0)
                if newb:
                    k = fstate["k"]
                    fstate["k"] += 1
                    if k >= 2 and (k - 2) in fstate["marks"]:
                        wait("tensor", "vself", fstate["marks"][k - 2])
                    b = 2 * fpair + (k % 2)
                    fstate["bank"][k] = b
                    wait("tensor", "aself", ctxa_pair[fpair])
                    wait("tensor", "vself", ctxe_pair[fpair])
                k = fstate["k"] - 1
                tb, tbb = tb_view(fstate["bank"][k])
                m = emitf(tb, tbb)
                if m is not None:
                    fstate["marks"][k] = m
                    fstate["last"] = m
                    lastuse[fstate["bank"][k]] = m

            vnorm_l = [0]
            vnorms = []
            ctxe_l = [0]
            ctxa_l = [0]
            fstate["last"] = 0
            eas = {}
            tq_hist = {}
            wcast0 = stream_qkv_weights(0, 0, eng="scalar")
            pend0 = make_qkv_fillers(0, wcast0, fbe_mark)
            while pend0:
                emit_filler(pend0, 1)
            tq_hist[0] = cnt["t_qkv"]
            qkv_done = fstate["last"]
            wcast = stream_qkv_weights(1, 0, eng="scalar")
            pend = make_qkv_fillers(1, wcast, fbe_mark)

            for hp in range(NHP):
                par = (hp % 2) * PAR

                wait("tensor", "vself", qkv_done)
                pendq = []

                def emit_pv(payload, par=par):
                    gq, k0, klen, pt, cp = payload
                    wait("tensor", "aself", eas[gq])
                    if k0 == 0:
                        wait("tensor", "aself", ctxa_pair[cp])
                        wait("tensor", "vself", ctxe_pair[cp])
                        wait("tensor", "vself", lastuse[2 * cp])
                        wait("tensor", "vself", lastuse[2 * cp + 1])
                    kt = k0
                    for h in range(2):
                        dst = ctx_dst(cp, h)
                        ph = pt + h * 512
                        vh0 = par + VHbase + kt * 130 + h * 65
                        op("tensor", lambda e, d=dst, ph=ph,
                           v0=vh0, kt=kt:
                           e.matmul(d,
                                    UA[:, v0:v0 + 65],
                                    UA[:, ph:ph + 512],
                                    start=(kt == 0), stop=(kt == NKT - 1)),
                           "t_pvd" if h == 1 else None)

                pend_evac = [None]

                def emit_evac():
                    ehp, eqb, first_of_hp = pend_evac[0]
                    pend_evac[0] = None
                    eqt = eqb // QTW
                    cp = eqt % 2
                    awaits("t_pvd", cnt["t_pvd"])
                    srcA = (psA[0:64, 2048:2560] if cp == 0
                            else psA[0:64, 3072:3584])
                    ctxa_pair[cp] = ctxa_l[0] = aop(
                        lambda e, hp=ehp, qb=eqb, s=srcA: e.activation(
                            fbB[0:64, hp * S + qb:hp * S + qb + QTW],
                            s, AF.Identity))
                    vwait("t_pvd", cnt["t_pvd"])
                    if first_of_hp:
                        vwait("d_cb", cnt["d_cb"])
                    srcB = (psA[0:64, 2560:3072] if cp == 0
                            else psT[0:64, 0:512])
                    vop(lambda e, qb=eqb, s=srcB: e.tensor_copy(
                        CU[0:64, qb:qb + QTW], s))
                    vwait("d_sum", cnt["d_sum"])
                    if cp == 0:
                        ctxe_l[0] = vop(lambda e: e.tensor_copy(
                            sstage[64:65, 0:1024], psA[64:65, 2048:3072]))
                    else:
                        vop(lambda e: e.tensor_copy(
                            sstage[64:65, 0:512], psA[64:65, 3072:3584]))
                        ctxe_l[0] = vop(lambda e: e.tensor_copy(
                            sstage[64:65, 512:1024], psT[64:65, 0:512]))
                    ctxe_pair[cp] = ctxe_l[0]
                    wait("gpsimd", "vself", ctxe_l[0])
                    for half in range(2):
                        pa = 2 * eqt + half
                        pb = 8 + 2 * eqt + half
                        dma("gpsimd", lambda e, p=pa, o=half * 256:
                            e.dma_start(FU[p:p + 1, 0:256],
                                        sstage[64:65, o:o + 256]), "d_sum")
                        dma("gpsimd", lambda e, p=pb, o=512 + half * 256:
                            e.dma_start(FU[p:p + 1, 0:256],
                                        sstage[64:65, o:o + 256]), "d_sum")

                for qt in range(NQT):
                    qb = qt * QTW
                    for (k0, klen) in GRPS:
                        g = Gg[0]
                        Gg[0] += 1
                        p = g % 2
                        wait("tensor", "aself", eas.get(g - 2, 0))
                        kt = k0
                        for h in range(2):
                            base = h * 64
                            sc = p * 1024 + h * 512
                            op("tensor", lambda e, kt=kt, b=base, sc=sc,
                               qb=qb, par=par:
                               e.matmul(psA[:, sc:sc + 512],
                                        UA[b:b + 64, par + KB + kt * 128:
                                           par + KB + kt * 128 + 128],
                                        UA[b:b + 64, par + QB + qb:
                                           par + QB + qb + QTW],
                                        start=True, stop=True),
                               "t_sc" if h == 1 else None)
                        if k0 == 2 and pend_evac[0] is not None:
                            emit_evac()
                        if len(pendq) >= 2:
                            emit_pv(pendq.pop(0))
                        if (qt * len(GRPS) + k0 >= 12
                                and (qt == 0 or k0 >= 3)):
                            emit_filler(pend, 1 - qt % 2)
                        wait("scalar", "t_sc", cnt["t_sc"])
                        if g >= 4:
                            wait("scalar", "t_pvd", g - 3)
                        pt = PTbase + (g % 4) * 1024
                        eas[g] = aop(lambda e, pt=pt, p=p: e.activation(
                            UA[:, pt:pt + 1024], psA[:, p * 1024:p * 1024 + 1024],
                            AF.Exp, scale=0.125))
                        pendq.append((g, k0, klen, pt, qt % 2))
                    if pend and not pend[0][1]:
                        emit_filler(pend, 1 - qt % 2)
                    if qt < NQT - 1:
                        pend_evac[0] = (hp, qb, hp > 0 and qt == 0)
                    else:
                        wait("tensor", "aself", cnt["aself"])
                        while pendq:
                            emit_pv(pendq.pop(0))
                        pend_evac[0] = (hp, qb, False)
                        emit_evac()
                while pend:
                    emit_filler(pend, 1)
                if hp + 1 < NHP:
                    tq_hist[hp + 1] = cnt["t_qkv"]
                    qkv_done = fstate["last"]
                if hp + 2 < NHP:
                    wcast = stream_qkv_weights(hp + 2, tq_hist[hp])
                    pend = make_qkv_fillers(hp + 2, wcast, fbe_mark)
                else:
                    pend = []
                if hp == 6:
                    tq_all = cnt["t_qkv"]
                    for icp in range(4):
                        rs = wrun_next()
                        for sl in range(2):
                            ic = icp * 2 + sl
                            wait("sync", *wfree[sl])
                            dma("sync", lambda e, ic=ic, sl=sl: e.dma_start(
                                wstage[:, sl * D:sl * D + D],
                                wo_h.ap()[ic * 128:ic * 128 + 128, :]), rs)
                        vwait(rs, cnt[rs])
                        vwait("t_qkv", tq_all)
                        for sl in range(2):
                            ic = icp * 2 + sl
                            vstg_l[0] = ("vself", vop(
                                lambda e, ic=ic, sl=sl: e.tensor_scalar(
                                    WB[:, ic * 1024:ic * 1024 + 1024],
                                    wstage[:, sl * D:sl * D + D], so, None,
                                    ALU.mult)))
                            wfree[sl] = vstg_l[0]


                wait("gpsimd", "vself", ctxe_l[0])
                dma("gpsimd", lambda e, hp=hp: e.dma_start(
                    fbB[64:128, hp * S:hp * S + S], CU[0:64, :]), "d_cb")

                vwait("d_sum", cnt["d_sum"])

                def _recip(e):
                    with nc.allow_low_precision(
                            reason="softmax 1/sum consumed as bf16 scale"):
                        return e.reciprocal(recb[0:16, 0:256], FU[0:16, 0:256])
                vrec = vop(_recip)
                wait("gpsimd", "vself", vrec)
                dma("gpsimd", lambda e, hp=hp: e.dma_start(
                    recd_h.ap()[16 * hp:16 * hp + 16, :],
                    recb[0:16, 0:256]), "d_rc")
                wait("gpsimd", "d_rc", cnt["d_rc"])
                if hp > 0:
                    wait("gpsimd", "vself", vnorm_l[0])
                for h in range(2):
                    rsrc = bass.AP(recd_h.ap().tensor,
                                   (16 * hp + 8 * h) * 256,
                                   [[0, 64], [256, 8], [1, 256]])
                    dma("gpsimd", lambda e, h=h, rsrc=rsrc: e.dma_start(
                        xntm[h * 64:h * 64 + 64, :]
                        .rearrange("p (b j) -> p b j", j=256), rsrc), "d_re")
                vwait("d_re", cnt["d_re"])
                vwait("d_cb", cnt["d_cb"])
                vwait("aself", ctxa_l[0])
                vnorm_l[0] = vop(lambda e, hp=hp: e.tensor_tensor(
                    fbB[:, hp * S:hp * S + S], fbB[:, hp * S:hp * S + S],
                    xntm[:, :], ALU.mult))
                vnorms.append(vnorm_l[0])
            vnorm = vnorm_l[0]

            wait("tensor", *vstg_l[0])
            vwait("d_rc", cnt["d_rc"])
            woe = []
            for tc in range(NTC):
                bank = (tc % 2) * 1024
                if tc >= 2:
                    wait("tensor", *woe[tc - 2])
                for half in range(2):
                    for ic in range(NFC):
                        if tc == 0 and half == 0:
                            wait("tensor", "vself", vnorms[ic])
                        op("tensor", lambda e, tc=tc, ic=ic, h=half, b=bank:
                           e.matmul(psA[:, b + h * 512:b + h * 512 + 512],
                                    fbB[:, ic * S + tc * 128:
                                        ic * S + tc * 128 + 128],
                                    WB[:, ic * 1024 + h * 512:
                                       ic * 1024 + h * 512 + 512],
                                    start=(ic == 0), stop=False))
                    op("tensor", lambda e, h=half, b=bank:
                       e.matmul(psA[:, b + h * 512:b + h * 512 + 512],
                                ones1b[:], bor[0:1, h * 512:h * 512 + 512],
                                start=False, stop=True),
                       "t_wo" if half == 1 else None)
                wait("scalar" if tc % 2 == 0 else "vector",
                     "t_wo", cnt["t_wo"])
                woe.append(xcopy(tc, fbA[:, tc * 1024:tc * 1024 + 1024],
                                 psA[:, bank:bank + 1024]))
                gen_stats_bn(fbA, tc, woe[tc])
            wait("sync", "t_wo", cnt["t_wo"])
            rs_e = wrun_next()
            dma("sync", lambda e: e.dma_start(WB[0:64, 1024:2048], e_h.ap()),
                rs_e)
            tw_all = cnt["t_wo"]
            ffw = [(win_h, slice(0, 512), 64, 512, None),
                   (w2bd_h, slice(512, 1024), 64, 512, None),
                   (wout_h, None, None, None, "wout"),
                   (bout_h, None, None, None, "bout")]
            for srct, cols, oo, ww, special in ffw:
                sl = wslot[0] % 2
                wslot[0] += 1
                rs = wrun_next()
                wait("sync", *wfree[sl])
                if special is None:
                    dma("sync", lambda e, srct=srct, sl=sl: e.dma_start(
                        wstage[:, sl * D:sl * D + 512]
                        .rearrange("p (i o) -> p i o", o=64),
                        srct.ap()), rs)
                elif special == "wout":
                    dma("sync", lambda e, sl=sl: e.dma_start(
                        wstage[0:64, sl * D:sl * D + D], wout_h.ap()), rs)
                else:
                    dma("sync", lambda e, sl=sl: e.dma_start(
                        wstage[0:1, sl * D:sl * D + D], bout_h.ap()), rs)
                awaits(rs, cnt[rs])
                awaits("t_wo", tw_all)
                if special is None:
                    vstg_l[0] = ("aself", aop(
                        lambda e, c=cols, sl=sl: e.activation(
                            WB[:, c], wstage[:, sl * D:sl * D + 512],
                            AF.Identity)))
                elif special == "wout":
                    vstg_l[0] = ("aself", aop(lambda e, sl=sl: e.activation(
                        WB[0:64, 2048:3072], wstage[0:64, sl * D:sl * D + D],
                        AF.Identity)))
                else:
                    vstg_l[0] = ("aself", aop(lambda e, sl=sl: e.activation(
                        WB[64:65, 2048:3072], wstage[0:1, sl * D:sl * D + D],
                        AF.Identity)))
                wfree[sl] = vstg_l[0]

            stm = gen_stats_agg()
            awaits("vself", stm)
            adds = []
            for tc in range(NTC):
                sl = (tc % 2) * D
                warm(3)
                if tc >= 2:
                    eng = "scalar" if tc % 2 == 0 else "vector"
                    wait(eng, "vself", adds[tc - 2])
                mks, mk = xapply(tc, xntm[:, sl:sl + D],
                                 fbA[:, tc * D:tc * D + D],
                                 agt[:, 64 + tc:65 + tc],
                                 agt[:, 80 + tc:81 + tc])
                vwait(mks, mk)
                adds.append(vop(lambda e, tc=tc, sl=sl: e.tensor_tensor(
                    X[:, tc * D:tc * D + D], X[:, tc * D:tc * D + D],
                    xntm[:, sl:sl + D], ALU.add)))
                gen_stats_bn(X, tc)

            fbe_mark = gen_ln_fb(fbB, pre=True)

            wait("sync", "t_wo", cnt["t_wo"])
            wait("tensor", *vstg_l[0])
            for w_ in fbe_mark:
                wait("tensor", *w_)
            wait("tensor", rs_e, cnt[rs_e])
            sfbe = []
            for q4 in range(4):
                if q4 >= 1:
                    wait("tensor", "vself", sfbe[-1])
                for ic in range(NFC):
                    op("tensor", lambda e, ic=ic, q4=q4:
                       e.matmul(psT[0:64, 0:512],
                                WB[:, ic * 64:ic * 64 + 64],
                                fbB[:, ic * S + q4 * 512:
                                    ic * S + q4 * 512 + 512],
                                start=(ic == 0), stop=(ic == NFC - 1)),
                       "t_ffn" if ic == NFC - 1 else None)
                vwait("t_ffn", cnt["t_ffn"])
                sfbe.append(vop(lambda e, q4=q4: e.tensor_scalar(
                    FUb[0:64, q4 * 512:q4 * 512 + 512], psT[0:64, 0:512],
                    1.0, bins[0:64, 0:1], ALU.mult, ALU.add)))
            wait("tensor", "vself", sfbe[-1])
            for fc in range(NFC):
                if fc >= 1:
                    wait("tensor", "aself", cnt["aself"])
                for q4 in range(4):
                    op("tensor", lambda e, fc=fc, q4=q4:
                       e.matmul(psA[:, q4 * 512:q4 * 512 + 512],
                                WB[0:64, 1024 + fc * 128:1024 + fc * 128 + 128],
                                FUb[0:64, q4 * 512:q4 * 512 + 512],
                                start=True, stop=True),
                       "t_ffn" if q4 == 3 else None)
                wait("scalar", "t_ffn", cnt["t_ffn"])
                aop(lambda e, fc=fc: e.activation(
                    fbA[:, fc * S:fc * S + S], psA[:, 0:2048], AF.Relu,
                    bias=b1s[:, fc:fc + 1], scale=w1s[:, fc:fc + 1]))
            wait("tensor", "aself", cnt["aself"])
            gfbe = []
            for q4 in range(4):
                if q4 >= 1:
                    wait("tensor", "vself", gfbe[-1])
                for fc in range(NFC):
                    op("tensor", lambda e, fc=fc, q4=q4:
                       e.matmul(psT[0:64, 0:512],
                                WB[:, 512 + fc * 64:512 + fc * 64 + 64],
                                fbA[:, fc * S + q4 * 512:
                                    fc * S + q4 * 512 + 512],
                                start=(fc == 0), stop=(fc == NFC - 1)),
                       "t_g" if fc == NFC - 1 else None)
                vwait("t_g", cnt["t_g"])
                gfbe.append(vop(lambda e, q4=q4: e.tensor_scalar(
                    FUb[0:64, S + q4 * 512:S + q4 * 512 + 512],
                    psT[0:64, 0:512],
                    1.0, b2s[0:64, 0:1], ALU.mult, ALU.add)))
            gfbe.append(vop(lambda e: e.memset(FUb[64:65, S:2 * S], 1.0)))
            wait("tensor", "vself", gfbe[-1])
            woe = []
            for tc in range(NTC):
                bank = (tc % 2) * 1024
                if tc >= 2:
                    wait("tensor", *woe[tc - 2])
                for half in range(2):
                    op("tensor", lambda e, tc=tc, h=half, b=bank:
                       e.matmul(psA[:, b + h * 512:b + h * 512 + 512],
                                FUb[0:65, S + tc * 128:S + tc * 128 + 128],
                                WB[0:65, 2048 + h * 512:2048 + h * 512 + 512],
                                start=True, stop=True),
                       "t_wout" if half == 1 else None)
                wait("scalar" if tc % 2 == 0 else "vector",
                     "t_wout", cnt["t_wout"])
                woe.append(xcopy(tc, UA[:, tc * 1024:tc * 1024 + 1024],
                                 psA[:, bank:bank + 1024]))
                gen_stats_bn(UA, tc, woe[tc])
            stm = gen_stats_agg()
            awaits("vself", stm)
            resid = []
            for tc in range(NTC):
                sl = (tc % 2) * D
                warm(3)
                if tc >= 2:
                    eng = "scalar" if tc % 2 == 0 else "vector"
                    wait(eng, "vself", resid[tc - 2])
                mks, mk = xapply(tc, xntm[:, sl:sl + D],
                                 UA[:, tc * D:tc * D + D],
                                 agt[:, 64 + tc:65 + tc],
                                 agt[:, 80 + tc:81 + tc])
                vwait(mks, mk)
                resid.append(vop(lambda e, tc=tc, sl=sl: e.tensor_tensor(
                    X[:, tc * D:tc * D + D], X[:, tc * D:tc * D + D],
                    xntm[:, sl:sl + D], ALU.add)))

            for tc in range(NTC):
                wait("sync", "vself", resid[tc])
                dma("sync", lambda e, s=s, tc=tc: e.dma_start(
                    y_h.ap()[s, tc * 128:tc * 128 + 128, :],
                    X[:, tc * D:tc * D + D]), "y%d" % (tc // 4))
            ysent += NTC

        for q in range(4):
            wait("sync", "y%d" % q, cnt["y%d" % q])

        blk = st.enter_context(nc.Block())

        @blk.tensor
        def _(e):
            for f in prog["tensor"]:
                f(e)

        @blk.vector
        def _(e):
            for f in prog["vector"]:
                f(e)

        @blk.scalar
        def _(e):
            for f in prog["scalar"]:
                f(e)

        @blk.sync
        def _(e):
            for f in prog["sync"]:
                f(e)

        @blk.gpsimd
        def _(e):
            for f in prog["gpsimd"]:
                f(e)

    return nc


def kernel(**inputs):
    g = lambda k: np.asarray(inputs[k])
    x = g("L0_attn_x")
    sq = float(np.asarray(inputs["L0_attn_s_q"]))
    sk = float(np.asarray(inputs["L0_attn_s_k"]))
    sv = float(np.asarray(inputs["L0_attn_s_v"]))
    so = float(np.asarray(inputs["L0_attn_s_o"]))

    def wlayout(w):
        wT = np.ascontiguousarray(w.T)
        return np.ascontiguousarray(
            wT.reshape(NFC, 128, NHP, 128).transpose(2, 1, 0, 3))

    wq = wlayout(g("L0_attn_w_q"))
    wk = wlayout(g("L0_attn_w_k"))
    wv = wlayout(g("L0_attn_w_v"))
    wo = np.ascontiguousarray(g("L0_attn_w_o").T)
    bq = np.ascontiguousarray(g("L0_attn_b_q").reshape(NHP, 128).T)
    bk = np.ascontiguousarray(g("L0_attn_b_k").reshape(NHP, 128).T)
    bv = np.ascontiguousarray(g("L0_attn_b_v").reshape(NHP, 128).T)
    bo = g("L0_attn_b_o").reshape(1, D)
    Win = g("L0_ffn_Win")
    win = np.ascontiguousarray(Win.reshape(NFC, 128, 64).transpose(1, 0, 2))
    W2 = g("L0_ffn_W2")
    w2bd_full = np.zeros((D, 64), np.float32)
    for m in range(64):
        w2bd_full[m * 16:(m + 1) * 16, m] = W2[m]
    w2bd = np.ascontiguousarray(
        w2bd_full.reshape(NFC, 128, 64).transpose(1, 0, 2))
    em = np.zeros((64, D), np.float32)
    for m in range(64):
        em[m, m * 16:(m + 1) * 16] = 1.0
    em = em.astype(BF)
    wout = np.ascontiguousarray(g("L0_ffn_Wout"))
    bout = g("L0_ffn_bout").reshape(1, D)
    binv = g("L0_ffn_bin_").reshape(64, 1)
    b2v = g("L0_ffn_b2").reshape(64, 1)
    w1il = np.ascontiguousarray(g("L0_ffn_W1").reshape(NFC, 128).T)
    b1il = np.ascontiguousarray(g("L0_ffn_b1").reshape(NFC, 128).T)
    ident = np.eye(128, dtype=np.float32).astype(BF)
    ones1 = np.ones((1, 128), np.float32).astype(BF)

    nc = build_program(sq, sk, sv, so)
    shared = dict(wq=wq, wk=wk, wv=wv, wo=wo, bq=bq, bk=bk, bv=bv, bo=bo,
                  win=win, w2bd=w2bd, em=em, wout=wout, bout=bout, binv=binv,
                  b2v=b2v, w1il=w1il, b1il=b1il, ident=ident, ones1=ones1)
    in_maps = []
    for c in range(8):
        m = dict(shared)
        m["x"] = np.ascontiguousarray(x[2 * c:2 * c + 2])
        in_maps.append(m)
    res = run_bass_kernel_spmd(nc, in_maps, core_ids=list(range(8)))
    out = np.empty((16, S, D), np.float32)
    for c in range(8):
        out[2 * c:2 * c + 2] = res.results[c]["y"]
    return out


# version 31 (final)
# speedup vs baseline: 1.1057x; 1.0184x over previous
import numpy as np
import ml_dtypes
from contextlib import ExitStack

import concourse.bass as bass
import concourse.mybir as mybir
from concourse.bass_utils import run_bass_kernel_spmd

F32 = mybir.dt.float32
BF16 = mybir.dt.bfloat16
AF = mybir.ActivationFunctionType
ALU = mybir.AluOpType
BF = ml_dtypes.bfloat16

S = 2048
D = 1024
NTC = 16
NFC = 8
NHP = 8
NQT = 4
QTW = 512
NKT = 16
GRPS = [(i, 1) for i in range(16)]
NSEQ = 2
EPS = 1e-5

ENGINES = ("tensor", "vector", "scalar", "sync", "gpsimd")


def build_program(sq, sk, sv, so):
    nc = bass.Bass("TRN2", target_bir_lowering=False)

    x_h = nc.dram_tensor("x", [NSEQ, S, D], F32, kind="ExternalInput")
    y_h = nc.dram_tensor("y", [NSEQ, S, D], F32, kind="ExternalOutput")
    wq_h = nc.dram_tensor("wq", [NHP, 128, NFC, 128], F32, kind="ExternalInput")
    wk_h = nc.dram_tensor("wk", [NHP, 128, NFC, 128], F32, kind="ExternalInput")
    wv_h = nc.dram_tensor("wv", [NHP, 128, NFC, 128], F32, kind="ExternalInput")
    wo_h = nc.dram_tensor("wo", [D, D], F32, kind="ExternalInput")
    bq_h = nc.dram_tensor("bq", [128, NHP], F32, kind="ExternalInput")
    bk_h = nc.dram_tensor("bk", [128, NHP], F32, kind="ExternalInput")
    bv_h = nc.dram_tensor("bv", [128, NHP], F32, kind="ExternalInput")
    bo_h = nc.dram_tensor("bo", [1, D], F32, kind="ExternalInput")
    win_h = nc.dram_tensor("win", [128, NFC, 64], F32, kind="ExternalInput")
    w2bd_h = nc.dram_tensor("w2bd", [128, NFC, 64], F32, kind="ExternalInput")
    e_h = nc.dram_tensor("em", [64, D], BF16, kind="ExternalInput")
    wout_h = nc.dram_tensor("wout", [64, D], F32, kind="ExternalInput")
    bout_h = nc.dram_tensor("bout", [1, D], F32, kind="ExternalInput")
    bin_h = nc.dram_tensor("binv", [64, 1], F32, kind="ExternalInput")
    b2_h = nc.dram_tensor("b2v", [64, 1], F32, kind="ExternalInput")
    w1il_h = nc.dram_tensor("w1il", [128, NFC], F32, kind="ExternalInput")
    b1il_h = nc.dram_tensor("b1il", [128, NFC], F32, kind="ExternalInput")
    ident_h = nc.dram_tensor("ident", [128, 128], BF16, kind="ExternalInput")
    ones1_h = nc.dram_tensor("ones1", [1, 128], BF16, kind="ExternalInput")
    recd_h = nc.dram_tensor("recd", [128, 256], BF16, kind="Internal")

    st = ExitStack()
    with st:
        sb = lambda n, sh, dt: st.enter_context(nc.sbuf_tensor(n, sh, dt))
        X = sb("X", [128, NTC * D], F32)
        fbA = sb("fbA", [128, NFC * S], BF16)
        fbB = sb("fbB", [128, NFC * S], BF16)
        UA = sb("UA", [128, 16448], BF16)
        xntm = sb("xntm", [128, 2 * D], BF16)
        wstage = sb("wstage", [128, 2 * D], F32)
        WB = sb("WB", [128, 8192], BF16)
        FU = sb("FU", [128, S], F32)
        FUb = FU.bitcast(BF16)
        CU = sb("CU", [128, S], BF16)
        sstage = CU.bitcast(F32)
        bnr = sb("bnr", [128, 192], F32)
        recb = bnr.bitcast(BF16)
        agt = sb("agt", [128, 96], F32)
        bqs = sb("bqs", [128, NHP], F32)
        bks = sb("bks", [128, NHP], F32)
        w1s = sb("w1s", [128, NFC], F32)
        b1s = sb("b1s", [128, NFC], F32)
        bins = sb("bins", [64, 1], F32)
        b2s = sb("b2s", [64, 1], F32)
        bvs = sb("bvs", [128, NHP], F32)
        bor = sb("bor", [1, D], BF16)
        identb = sb("identb", [128, 128], BF16)
        ones1b = sb("ones1b", [1, 128], BF16)

        psA = st.enter_context(nc.psum_tensor("psA", [128, 3584], F32))
        psT = st.enter_context(nc.psum_tensor("psT", [128, 512], F32))
        psT_bf = psT.bitcast(BF16)
        psA_bf = psA.bitcast(BF16)

        PAR = 6176
        QB = 0
        KB = 2048
        VHbase = 4096
        PTbase = 12352

        sem_names = ["vself", "aself", "gself",
                     "d_x", "d_w", "d_b", "d_y", "d_sum", "d_cb", "d_rc", "d_re",
                     "w0", "w1", "w2", "w3",
                     "x0", "x1", "x2", "x3", "y0", "y1", "y2", "y3",
                     "t_xnT", "t_qkv", "t_sc", "t_pvd", "t_wo", "t_ffn", "t_g",
                     "t_wout"]
        sems = {n: st.enter_context(nc.semaphore(n)) for n in sem_names}
        cnt = {n: 0 for n in sem_names}
        prog = {k: [] for k in ENGINES}

        def emit(eng, fn):
            prog[eng].append(fn)

        waited = {e: {} for e in ENGINES}

        def wait(eng, s, v):
            if v <= 0 or waited[eng].get(s, 0) >= v:
                return
            waited[eng][s] = v
            emit(eng, lambda e, _s=sems[s], _v=v: e.wait_ge(_s, _v))

        def op(eng, fn, inc=None):
            if inc is not None:
                cnt[inc] += 1

            def run(e, _fn=fn, _inc=inc):
                ins = _fn(e)
                if _inc is not None:
                    ins.then_inc(sems[_inc], 1)
            emit(eng, run)

        def dma(eng, fn, dsem):
            cnt[dsem] += 16

            def run(e, _fn=fn, _s=sems[dsem]):
                _fn(e).then_inc(_s, 16)
            emit(eng, run)

        def vop(fn):
            wait("vector", "vself", cnt["vself"])
            op("vector", fn, "vself")
            return cnt["vself"]

        def aop(fn):
            wait("scalar", "aself", cnt["aself"])
            op("scalar", fn, "aself")
            return cnt["aself"]

        wrun = [0]

        def wrun_next():
            wrun[0] += 1
            return "w%d" % (wrun[0] % 4)

        def gop(fn):
            wait("gpsimd", "gself", cnt["gself"])
            op("gpsimd", fn, "gself")
            return cnt["gself"]

        def vwait(s, v):
            wait("vector", s, v)

        def awaits(s, v):
            wait("scalar", s, v)

        def xcopy(tc, dst, srcp):
            if tc % 2 == 0:
                return ("aself", aop(lambda e: e.activation(dst, srcp,
                                                            AF.Identity)))
            return ("vself", vop(lambda e: e.tensor_copy(dst, srcp)))

        def xapply(tc, dst, srct, rcol, ncol):
            if tc % 2 == 0:
                return ("aself", aop(lambda e: e.activation(
                    dst, srct, AF.Identity, bias=ncol, scale=rcol)))
            return ("vself", vop(lambda e: e.tensor_scalar(
                dst, srct, rcol, ncol, ALU.mult, ALU.add)))

        def warm(n):
            for _ in range(n):
                op("tensor", lambda e: e.matmul(
                    psA[:, 2048:2176], identb[:], identb[:],
                    start=True, stop=True))

        def strided(t, off, dims):
            return bass.AP(t.ap().tensor, off, [t.ap().ap[0]] + dims)

        for dstt, srct in ((bqs, bq_h), (bks, bk_h), (bvs, bv_h),
                           (w1s, w1il_h),
                           (b1s, b1il_h), (bins, bin_h), (b2s, b2_h),
                           (identb, ident_h), (ones1b, ones1_h)):
            dma("sync", lambda e, a=dstt, b=srct: e.dma_start(a[:], b.ap()),
                "d_b")
        vsetup = 0
        for rsrc, rdst in ((bo_h, bor),):
            wait("sync", "vself", vsetup)
            dma("sync", lambda e, b=rsrc: e.dma_start(wstage[0:1, 0:D],
                                                      b.ap()), "d_b")
            vwait("d_b", cnt["d_b"])
            vsetup = vop(lambda e, d=rdst: e.tensor_copy(d[:],
                                                         wstage[0:1, 0:D]))
        vwait("d_b", cnt["d_b"])
        wait("tensor", "d_b", cnt["d_b"])

        Gg = [0]
        ctxa_pair = [0, 0]
        ctxe_pair = [0, 0]
        lastuse = [0, 0, 0, 0]

        def tb_view(b):
            if b == 3:
                return psT[:, 0:512], psT_bf[:, 0:1024]
            lo = 2048 + 512 * b
            return psA[:, lo:lo + 512], psA_bf[:, 2 * lo:2 * lo + 1024]

        def ctx_dst(cp, h):
            if cp == 0:
                c = 2048 + h * 512
                return psA[0:65, c:c + 512]
            return psA[0:65, 3072:3584] if h == 0 else psT[0:65, 0:512]

        def gen_stats_bn(src_tile, c, w=None):
            if w is not None:
                vwait(*w)
            vop(lambda e, c=c, t=src_tile: e.bn_stats(
                bnr[:, 12 * c:12 * c + 6], t[:, c * D:c * D + 512]))
            vop(lambda e, c=c, t=src_tile: e.bn_stats(
                bnr[:, 12 * c + 6:12 * c + 12],
                t[:, c * D + 512:c * D + 1024]))
            vop(lambda e, c=c: e.bn_aggr(
                strided(agt, c, [[16, 2]]), bnr[:, 12 * c:12 * c + 12]))

        def gen_stats_agg():
            n = NTC
            MEAN, VARP = agt[:, 0:n], agt[:, 16:16 + n]
            mv = vop(lambda e: e.tensor_scalar(VARP, VARP, EPS, None,
                                               ALU.add))
            wait("scalar", "vself", mv)
            aop(lambda e: e.activation(agt[:, 64:64 + n], VARP, AF.Ln))
            ma = aop(lambda e: e.activation(agt[:, 64:64 + n],
                                            agt[:, 64:64 + n],
                                            AF.Exp, scale=-0.5))
            vwait("aself", ma)
            vop(lambda e: e.tensor_scalar(agt[:, 80:80 + n],
                                          agt[:, 64:64 + n],
                                          -1.0, None, ALU.mult))
            return vop(lambda e: e.tensor_tensor(agt[:, 80:80 + n], MEAN,
                                                 agt[:, 80:80 + n],
                                                 ALU.mult))

        def gen_stats(src_tile, chunk_waits=None):
            for c in range(NTC):
                gen_stats_bn(src_tile, c,
                             chunk_waits[c] if chunk_waits is not None
                             else None)
            return gen_stats_agg()

        def gen_ln_fb(fb, chunk_waits=None, pre=False):
            stm = gen_stats_agg() if pre else gen_stats(X, chunk_waits)
            awaits("vself", stm)
            vwait("vself", stm)
            wait("tensor", "aself", ctxa_pair[1])
            wait("tensor", "vself", ctxe_pair[1])
            tposted = []
            for tc in range(NTC):
                slot = (tc % 2) * D
                if chunk_waits is not None:
                    wait("scalar" if tc % 2 == 0 else "vector",
                         *chunk_waits[tc])
                if tc >= 2:
                    eng = "scalar" if tc % 2 == 0 else "vector"
                    wait(eng, "t_xnT", cnt["t_xnT"] - 1)
                mks, mk = xapply(tc, xntm[:, slot:slot + D],
                                 X[:, tc * D:tc * D + D],
                                 agt[:, 64 + tc:65 + tc],
                                 agt[:, 80 + tc:81 + tc])
                warm(3)
                wait("tensor", mks, mk)
                if tc >= 2:
                    wait("tensor", *tposted[tc - 2])
                pdst = psT_bf[:, 0:1024] if tc % 2 == 0 else psA_bf[:, 6144:7168]
                for fc in range(NFC):
                    op("tensor", lambda e, fc=fc, sl=slot, pd=pdst:
                       e.transpose(pd[:, fc * 128:fc * 128 + 128],
                                   xntm[:, sl + fc * 128:sl + fc * 128 + 128],
                                   identb[:]),
                       "t_xnT" if fc == NFC - 1 else None)
                wait("scalar" if tc % 2 == 0 else "vector",
                     "t_xnT", cnt["t_xnT"])
                tposted.append(xcopy(
                    tc, strided(fb, tc * 128, [[S, NFC], [1, 128]]),
                    pdst[:, 0:1024]))
            return (tposted[-2], tposted[-1])

        ysent = 0
        vstg_l = [("vself", 0)]
        wslot = [0]
        wfree = [None, None, None]
        for s in range(NSEQ):
            xr = x_h.rearrange("s (tc p) d -> s p tc d", p=128)
            for q in range(4):
                if s > 0:
                    wait("sync", "y%d" % q, 64 * s)
                dma("sync", lambda e, s=s, q=q: e.dma_start(
                    X[:, q * 4096:(q + 1) * 4096]
                    .rearrange("p (t d) -> p t d", d=D),
                    xr[s, :, q * 4:(q + 1) * 4, :]), "x%d" % q)

            xw = [("x%d" % (c // 4), 16 * (s + 1)) for c in range(NTC)]
            fbe_mark = gen_ln_fb(fbA, xw)

            fstate = {"k": 0, "marks": {}, "bank": {}}

            def stream_qkv_weights(hp, tq_need, eng="vector"):
                if wfree[0] is None:
                    wfree[0] = wfree[1] = wfree[2] = ("vself", vsetup)
                wb = (hp % 2) * 3072
                rs = wrun_next()
                slots = [wstage[:, 0:D], wstage[:, D:2 * D], FU[:, 1024:2048]]
                wait("sync", "t_wout", cnt["t_wout"])
                for j, wh in enumerate((wq_h, wk_h, wv_h)):
                    wait("sync", *wfree[j])
                    dma("sync", lambda e, wh=wh, hp=hp, st=slots[j]:
                        e.dma_start(st.rearrange("p (i o) -> p i o", o=128),
                                    wh.ap()[hp]), rs)
                w8 = vwait if eng == "vector" else awaits
                w8(rs, cnt[rs])
                w8("t_qkv", tq_need)
                wcast = []
                for j, (wof, scale) in enumerate(((0, sq), (1024, sk),
                                                  (2048, sv))):
                    if eng == "vector":
                        m = ("vself", vop(
                            lambda e, c=wb + wof, sc=scale, st=slots[j]:
                            e.tensor_scalar(WB[:, c:c + 1024], st,
                                            sc, None, ALU.mult)))
                    else:
                        m = ("aself", aop(
                            lambda e, c=wb + wof, sc=scale, st=slots[j]:
                            e.activation(WB[:, c:c + 1024], st,
                                         AF.Identity, scale=sc)))
                    vstg_l[0] = wfree[j] = m
                    wcast.append(m)
                return wcast

            def make_qkv_fillers(hp, wcast, xnm):
                par = (hp % 2) * PAR
                wb = (hp % 2) * 3072
                groups = []
                vev = [0]
                for j, (wof, bvec, dcol) in enumerate(
                        ((0, bqs, QB), (1024, bks, KB))):
                    for q4 in range(4):
                        def emitA(tb, tbb, j=j, q4=q4, wof=wof, wb=wb,
                                  wc=wcast[j], xnm=xnm):
                            wait("tensor", *wc)
                            for w_ in xnm:
                                wait("tensor", *w_)
                            for ic in range(4):
                                op("tensor",
                                   lambda e, ic=ic, q4=q4, tb=tb, c=wb + wof:
                                   e.matmul(tb[:, 0:512],
                                            WB[:, c + ic * 128:
                                               c + ic * 128 + 128],
                                            fbA[:, ic * S + q4 * 512:
                                                ic * S + q4 * 512 + 512],
                                            start=(ic == 0), stop=False))
                            return None
                        def emitB(tb, tbb, j=j, q4=q4, wof=wof, bvec=bvec,
                                  dcol=dcol, hp=hp, par=par, wb=wb):
                            for ic in range(4, NFC):
                                op("tensor",
                                   lambda e, ic=ic, q4=q4, tb=tb, c=wb + wof:
                                   e.matmul(tb[:, 0:512],
                                            WB[:, c + ic * 128:
                                               c + ic * 128 + 128],
                                            fbA[:, ic * S + q4 * 512:
                                                ic * S + q4 * 512 + 512],
                                            start=False, stop=(ic == NFC - 1)),
                                   "t_qkv" if ic == NFC - 1 else None)
                            vwait("t_qkv", cnt["t_qkv"])
                            return vop(
                                lambda e, q4=q4, tb=tb, bv=bvec, hp=hp,
                                dc=par + dcol:
                                e.tensor_scalar(
                                    UA[:, dc + q4 * 512:dc + q4 * 512 + 512],
                                    tb[:, 0:512], 1.0,
                                    bv[:, hp:hp + 1], ALU.mult, ALU.add))
                        groups.append((emitA, True))
                        groups.append((emitB, False))
                for q4 in range(4):
                    def emitA(tb, tbb, q4=q4, wb=wb, wc=wcast[2], xnm=xnm):
                        wait("tensor", *wc)
                        for w_ in xnm:
                            wait("tensor", *w_)
                        for ic in range(4):
                            op("tensor", lambda e, ic=ic, q4=q4, tb=tb,
                               c=wb + 2048:
                               e.matmul(tb[:, 0:512],
                                        WB[:, c + ic * 128:c + ic * 128 + 128],
                                        fbA[:, ic * S + q4 * 512:
                                            ic * S + q4 * 512 + 512],
                                        start=(ic == 0), stop=False))
                        return None
                    def emitB(tb, tbb, q4=q4, hp=hp, wb=wb, vev=vev):
                        for ic in range(4, NFC):
                            op("tensor", lambda e, ic=ic, q4=q4, tb=tb,
                               c=wb + 2048:
                               e.matmul(tb[:, 0:512],
                                        WB[:, c + ic * 128:c + ic * 128 + 128],
                                        fbA[:, ic * S + q4 * 512:
                                            ic * S + q4 * 512 + 512],
                                        start=False, stop=(ic == NFC - 1)),
                               "t_qkv" if ic == NFC - 1 else None)
                        vwait("t_qkv", cnt["t_qkv"])
                        m = vop(lambda e, q4=q4, tb=tb, hp=hp, par=par:
                                e.tensor_scalar(
                                    UA[:, par + VHbase + q4 * 512:
                                       par + VHbase + q4 * 512 + 512],
                                    tb[:, 0:512], 1.0,
                                    bvs[:, hp:hp + 1], ALU.mult, ALU.add))
                        vev.append(m)
                        return m
                    groups.append((emitA, True))
                    groups.append((emitB, False))
                for th in (1, 0):
                    def emitT(tb, tbb, th=th, par=par, vev=vev):
                        wait("tensor", "vself", vev[-1])
                        for tl in range(8):
                            tc = th * 8 + tl
                            op("tensor", lambda e, tc=tc, tl=tl, tbb=tbb,
                               par=par:
                               e.transpose(tbb[:, tl * 128:tl * 128 + 128],
                                           UA[:, par + VHbase + tc * 128:
                                              par + VHbase + tc * 128 + 128],
                                           identb[:]),
                               "t_qkv" if tl == 7 else None)
                        vwait("t_qkv", cnt["t_qkv"])
                        m = vop(lambda e, th=th, tbb=tbb, par=par:
                                e.tensor_copy(
                                    strided(UA, par + VHbase + th * 8 * 130,
                                            [[130, 8], [65, 2], [1, 64]]),
                                    tbb[:, 0:1024]))
                        if th == 0:
                            m = vop(lambda e, par=par: e.memset(
                                strided(UA, par + VHbase + 64,
                                        [[130, 16], [65, 2], [1, 1]]), 1.0))
                        return m
                    groups.append((emitT, True))
                return groups

            def emit_filler(pend, fpair):
                if not pend:
                    return
                emitf, newb = pend.pop(0)
                if newb:
                    k = fstate["k"]
                    fstate["k"] += 1
                    if k >= 2 and (k - 2) in fstate["marks"]:
                        wait("tensor", "vself", fstate["marks"][k - 2])
                    b = 2 * fpair + (k % 2)
                    fstate["bank"][k] = b
                    wait("tensor", "aself", ctxa_pair[fpair])
                    wait("tensor", "vself", ctxe_pair[fpair])
                k = fstate["k"] - 1
                tb, tbb = tb_view(fstate["bank"][k])
                m = emitf(tb, tbb)
                if m is not None:
                    fstate["marks"][k] = m
                    fstate["last"] = m
                    lastuse[fstate["bank"][k]] = m

            vnorm_l = [0]
            vnorms = []
            ctxe_l = [0]
            ctxa_l = [0]
            fstate["last"] = 0
            eas = {}
            tq_hist = {}
            wcast0 = stream_qkv_weights(0, 0, eng="scalar")
            pend0 = make_qkv_fillers(0, wcast0, fbe_mark)
            while pend0:
                emit_filler(pend0, 1)
            tq_hist[0] = cnt["t_qkv"]
            qkv_done = fstate["last"]
            wcast = stream_qkv_weights(1, 0, eng="scalar")
            pend = make_qkv_fillers(1, wcast, fbe_mark)

            pendq = []
            pend_evac = [None]

            def emit_pv(payload):
                gq, k0, klen, pt, cp, par = payload
                wait("tensor", "aself", eas[gq])
                if k0 == 0:
                    wait("tensor", "aself", ctxa_pair[cp])
                    wait("tensor", "vself", ctxe_pair[cp])
                    wait("tensor", "vself", lastuse[2 * cp])
                    wait("tensor", "vself", lastuse[2 * cp + 1])
                kt = k0
                for h in range(2):
                    dst = ctx_dst(cp, h)
                    ph = pt + h * 512
                    vh0 = par + VHbase + kt * 130 + h * 65
                    op("tensor", lambda e, d=dst, ph=ph,
                       v0=vh0, kt=kt:
                       e.matmul(d,
                                UA[:, v0:v0 + 65],
                                UA[:, ph:ph + 512],
                                start=(kt == 0), stop=(kt == NKT - 1)),
                       "t_pvd" if h == 1 else None)

            def emit_normalize(nhp):
                wait("gpsimd", "vself", ctxe_l[0])
                dma("gpsimd", lambda e, hp=nhp: e.dma_start(
                    fbB[64:128, hp * S:hp * S + S], CU[0:64, :]), "d_cb")
                vwait("d_sum", cnt["d_sum"])

                def _recip(e):
                    with nc.allow_low_precision(
                            reason="softmax 1/sum consumed as bf16 scale"):
                        return e.reciprocal(recb[0:16, 0:256],
                                            FU[0:16, 0:256])
                vrec = vop(_recip)
                wait("gpsimd", "vself", vrec)
                dma("gpsimd", lambda e, hp=nhp: e.dma_start(
                    recd_h.ap()[16 * hp:16 * hp + 16, :],
                    recb[0:16, 0:256]), "d_rc")
                wait("gpsimd", "d_rc", cnt["d_rc"])
                if nhp > 0:
                    wait("gpsimd", "vself", vnorm_l[0])
                for h in range(2):
                    rsrc = bass.AP(recd_h.ap().tensor,
                                   (16 * nhp + 8 * h) * 256,
                                   [[0, 64], [256, 8], [1, 256]])
                    dma("gpsimd", lambda e, h=h, rsrc=rsrc: e.dma_start(
                        xntm[h * 64:h * 64 + 64, :]
                        .rearrange("p (b j) -> p b j", j=256), rsrc), "d_re")
                vwait("d_re", cnt["d_re"])
                vwait("d_cb", cnt["d_cb"])
                vwait("aself", ctxa_l[0])
                vnorm_l[0] = vop(lambda e, hp=nhp: e.tensor_tensor(
                    fbB[:, hp * S:hp * S + S], fbB[:, hp * S:hp * S + S],
                    xntm[:, :], ALU.mult))
                vnorms.append(vnorm_l[0])

            def emit_evac():
                ehp, eqb, first_of_hp, then_norm = pend_evac[0]
                pend_evac[0] = None
                eqt = eqb // QTW
                cp = eqt % 2
                awaits("t_pvd", cnt["t_pvd"])
                srcA = (psA[0:64, 2048:2560] if cp == 0
                        else psA[0:64, 3072:3584])
                ctxa_pair[cp] = ctxa_l[0] = aop(
                    lambda e, hp=ehp, qb=eqb, s=srcA: e.activation(
                        fbB[0:64, hp * S + qb:hp * S + qb + QTW],
                        s, AF.Identity))
                vwait("t_pvd", cnt["t_pvd"])
                if first_of_hp:
                    vwait("d_cb", cnt["d_cb"])
                srcB = (psA[0:64, 2560:3072] if cp == 0
                        else psT[0:64, 0:512])
                vop(lambda e, qb=eqb, s=srcB: e.tensor_copy(
                    CU[0:64, qb:qb + QTW], s))
                vwait("d_sum", cnt["d_sum"])
                if cp == 0:
                    ctxe_l[0] = vop(lambda e: e.tensor_copy(
                        sstage[64:65, 0:1024], psA[64:65, 2048:3072]))
                else:
                    vop(lambda e: e.tensor_copy(
                        sstage[64:65, 0:512], psA[64:65, 3072:3584]))
                    ctxe_l[0] = vop(lambda e: e.tensor_copy(
                        sstage[64:65, 512:1024], psT[64:65, 0:512]))
                ctxe_pair[cp] = ctxe_l[0]
                wait("gpsimd", "vself", ctxe_l[0])
                for half in range(2):
                    pa = 2 * eqt + half
                    pb = 8 + 2 * eqt + half
                    dma("gpsimd", lambda e, p=pa, o=half * 256:
                        e.dma_start(FU[p:p + 1, 0:256],
                                    sstage[64:65, o:o + 256]), "d_sum")
                    dma("gpsimd", lambda e, p=pb, o=512 + half * 256:
                        e.dma_start(FU[p:p + 1, 0:256],
                                    sstage[64:65, o:o + 256]), "d_sum")
                if then_norm:
                    emit_normalize(ehp)

            for hp in range(NHP):
                par = (hp % 2) * PAR

                wait("tensor", "vself", qkv_done)

                for qt in range(NQT):
                    qb = qt * QTW
                    for (k0, klen) in GRPS:
                        g = Gg[0]
                        Gg[0] += 1
                        p = g % 2
                        wait("tensor", "aself", eas.get(g - 2, 0))
                        kt = k0
                        for h in range(2):
                            base = h * 64
                            sc = p * 1024 + h * 512
                            op("tensor", lambda e, kt=kt, b=base, sc=sc,
                               qb=qb, par=par:
                               e.matmul(psA[:, sc:sc + 512],
                                        UA[b:b + 64, par + KB + kt * 128:
                                           par + KB + kt * 128 + 128],
                                        UA[b:b + 64, par + QB + qb:
                                           par + QB + qb + QTW],
                                        start=True, stop=True),
                               "t_sc" if h == 1 else None)
                        if k0 == 2 and pend_evac[0] is not None:
                            emit_evac()
                        if len(pendq) >= 2:
                            emit_pv(pendq.pop(0))
                        if (qt * len(GRPS) + k0 >= 12
                                and (qt == 0 or k0 >= 3)):
                            emit_filler(pend, 1 - qt % 2)
                        wait("scalar", "t_sc", cnt["t_sc"])
                        if g >= 4:
                            wait("scalar", "t_pvd", g - 3)
                        pt = PTbase + (g % 4) * 1024
                        eas[g] = aop(lambda e, pt=pt, p=p: e.activation(
                            UA[:, pt:pt + 1024], psA[:, p * 1024:p * 1024 + 1024],
                            AF.Exp, scale=0.125))
                        pendq.append((g, k0, klen, pt, qt % 2, par))
                    if pend and not pend[0][1]:
                        emit_filler(pend, 1 - qt % 2)
                    if hp == NHP - 1 and qt == NQT - 1:
                        wait("tensor", "aself", cnt["aself"])
                        while pendq:
                            emit_pv(pendq.pop(0))
                        pend_evac[0] = (hp, qb, False, False)
                        emit_evac()
                    else:
                        pend_evac[0] = (hp, qb, hp > 0 and qt == 0,
                                        qt == NQT - 1)
                if hp == NHP - 1:
                    while pend:
                        emit_filler(pend, 1)
                else:
                    assert not pend, "fillers overran the head pair"
                if hp + 1 < NHP:
                    tq_hist[hp + 1] = cnt["t_qkv"]
                    qkv_done = fstate["last"]
                if hp + 2 < NHP:
                    wcast = stream_qkv_weights(hp + 2, tq_hist[hp])
                    pend = make_qkv_fillers(hp + 2, wcast, fbe_mark)
                else:
                    pend = []
                if hp == 6:
                    tq_all = cnt["t_qkv"]
                    for icp in range(4):
                        rs = wrun_next()
                        for sl in range(2):
                            ic = icp * 2 + sl
                            wait("sync", *wfree[sl])
                            dma("sync", lambda e, ic=ic, sl=sl: e.dma_start(
                                wstage[:, sl * D:sl * D + D],
                                wo_h.ap()[ic * 128:ic * 128 + 128, :]), rs)
                        vwait(rs, cnt[rs])
                        vwait("t_qkv", tq_all)
                        for sl in range(2):
                            ic = icp * 2 + sl
                            vstg_l[0] = ("vself", vop(
                                lambda e, ic=ic, sl=sl: e.tensor_scalar(
                                    WB[:, ic * 1024:ic * 1024 + 1024],
                                    wstage[:, sl * D:sl * D + D], so, None,
                                    ALU.mult)))
                            wfree[sl] = vstg_l[0]


                if hp == NHP - 1:
                    emit_normalize(hp)
            vnorm = vnorm_l[0]

            wait("tensor", *vstg_l[0])
            vwait("d_rc", cnt["d_rc"])
            woe = []
            for tc in range(NTC):
                bank = (tc % 2) * 1024
                if tc >= 2:
                    wait("tensor", *woe[tc - 2])
                for half in range(2):
                    for ic in range(NFC):
                        if tc == 0 and half == 0:
                            wait("tensor", "vself", vnorms[ic])
                        op("tensor", lambda e, tc=tc, ic=ic, h=half, b=bank:
                           e.matmul(psA[:, b + h * 512:b + h * 512 + 512],
                                    fbB[:, ic * S + tc * 128:
                                        ic * S + tc * 128 + 128],
                                    WB[:, ic * 1024 + h * 512:
                                       ic * 1024 + h * 512 + 512],
                                    start=(ic == 0), stop=False))
                    op("tensor", lambda e, h=half, b=bank:
                       e.matmul(psA[:, b + h * 512:b + h * 512 + 512],
                                ones1b[:], bor[0:1, h * 512:h * 512 + 512],
                                start=False, stop=True),
                       "t_wo" if half == 1 else None)
                wait("scalar" if tc % 2 == 0 else "vector",
                     "t_wo", cnt["t_wo"])
                woe.append(xcopy(tc, fbA[:, tc * 1024:tc * 1024 + 1024],
                                 psA[:, bank:bank + 1024]))
                gen_stats_bn(fbA, tc, woe[tc])
            wait("sync", "t_wo", cnt["t_wo"])
            rs_e = wrun_next()
            dma("sync", lambda e: e.dma_start(WB[0:64, 1024:2048], e_h.ap()),
                rs_e)
            tw_all = cnt["t_wo"]
            ffw = [(win_h, slice(0, 512), 64, 512, None),
                   (w2bd_h, slice(512, 1024), 64, 512, None),
                   (wout_h, None, None, None, "wout"),
                   (bout_h, None, None, None, "bout")]
            for srct, cols, oo, ww, special in ffw:
                sl = wslot[0] % 2
                wslot[0] += 1
                rs = wrun_next()
                wait("sync", *wfree[sl])
                if special is None:
                    dma("sync", lambda e, srct=srct, sl=sl: e.dma_start(
                        wstage[:, sl * D:sl * D + 512]
                        .rearrange("p (i o) -> p i o", o=64),
                        srct.ap()), rs)
                elif special == "wout":
                    dma("sync", lambda e, sl=sl: e.dma_start(
                        wstage[0:64, sl * D:sl * D + D], wout_h.ap()), rs)
                else:
                    dma("sync", lambda e, sl=sl: e.dma_start(
                        wstage[0:1, sl * D:sl * D + D], bout_h.ap()), rs)
                awaits(rs, cnt[rs])
                awaits("t_wo", tw_all)
                if special is None:
                    vstg_l[0] = ("aself", aop(
                        lambda e, c=cols, sl=sl: e.activation(
                            WB[:, c], wstage[:, sl * D:sl * D + 512],
                            AF.Identity)))
                elif special == "wout":
                    vstg_l[0] = ("aself", aop(lambda e, sl=sl: e.activation(
                        WB[0:64, 2048:3072], wstage[0:64, sl * D:sl * D + D],
                        AF.Identity)))
                else:
                    vstg_l[0] = ("aself", aop(lambda e, sl=sl: e.activation(
                        WB[64:65, 2048:3072], wstage[0:1, sl * D:sl * D + D],
                        AF.Identity)))
                wfree[sl] = vstg_l[0]

            stm = gen_stats_agg()
            awaits("vself", stm)
            adds = []
            for tc in range(NTC):
                sl = (tc % 2) * D
                warm(3)
                if tc >= 2:
                    eng = "scalar" if tc % 2 == 0 else "vector"
                    wait(eng, "vself", adds[tc - 2])
                mks, mk = xapply(tc, xntm[:, sl:sl + D],
                                 fbA[:, tc * D:tc * D + D],
                                 agt[:, 64 + tc:65 + tc],
                                 agt[:, 80 + tc:81 + tc])
                vwait(mks, mk)
                adds.append(vop(lambda e, tc=tc, sl=sl: e.tensor_tensor(
                    X[:, tc * D:tc * D + D], X[:, tc * D:tc * D + D],
                    xntm[:, sl:sl + D], ALU.add)))
                gen_stats_bn(X, tc)

            fbe_mark = gen_ln_fb(fbB, pre=True)

            wait("sync", "t_wo", cnt["t_wo"])
            wait("tensor", *vstg_l[0])
            for w_ in fbe_mark:
                wait("tensor", *w_)
            wait("tensor", rs_e, cnt[rs_e])
            sfbe = []
            for q4 in range(4):
                if q4 >= 1:
                    wait("tensor", "vself", sfbe[-1])
                for ic in range(NFC):
                    op("tensor", lambda e, ic=ic, q4=q4:
                       e.matmul(psT[0:64, 0:512],
                                WB[:, ic * 64:ic * 64 + 64],
                                fbB[:, ic * S + q4 * 512:
                                    ic * S + q4 * 512 + 512],
                                start=(ic == 0), stop=(ic == NFC - 1)),
                       "t_ffn" if ic == NFC - 1 else None)
                vwait("t_ffn", cnt["t_ffn"])
                sfbe.append(vop(lambda e, q4=q4: e.tensor_scalar(
                    FUb[0:64, q4 * 512:q4 * 512 + 512], psT[0:64, 0:512],
                    1.0, bins[0:64, 0:1], ALU.mult, ALU.add)))
            wait("tensor", "vself", sfbe[-1])
            for fc in range(NFC):
                if fc >= 1:
                    wait("tensor", "aself", cnt["aself"])
                for q4 in range(4):
                    op("tensor", lambda e, fc=fc, q4=q4:
                       e.matmul(psA[:, q4 * 512:q4 * 512 + 512],
                                WB[0:64, 1024 + fc * 128:1024 + fc * 128 + 128],
                                FUb[0:64, q4 * 512:q4 * 512 + 512],
                                start=True, stop=True),
                       "t_ffn" if q4 == 3 else None)
                wait("scalar", "t_ffn", cnt["t_ffn"])
                aop(lambda e, fc=fc: e.activation(
                    fbA[:, fc * S:fc * S + S], psA[:, 0:2048], AF.Relu,
                    bias=b1s[:, fc:fc + 1], scale=w1s[:, fc:fc + 1]))
            wait("tensor", "aself", cnt["aself"])
            gfbe = []
            for q4 in range(4):
                if q4 >= 1:
                    wait("tensor", "vself", gfbe[-1])
                for fc in range(NFC):
                    op("tensor", lambda e, fc=fc, q4=q4:
                       e.matmul(psT[0:64, 0:512],
                                WB[:, 512 + fc * 64:512 + fc * 64 + 64],
                                fbA[:, fc * S + q4 * 512:
                                    fc * S + q4 * 512 + 512],
                                start=(fc == 0), stop=(fc == NFC - 1)),
                       "t_g" if fc == NFC - 1 else None)
                vwait("t_g", cnt["t_g"])
                gfbe.append(vop(lambda e, q4=q4: e.tensor_scalar(
                    FUb[0:64, S + q4 * 512:S + q4 * 512 + 512],
                    psT[0:64, 0:512],
                    1.0, b2s[0:64, 0:1], ALU.mult, ALU.add)))
            gfbe.append(vop(lambda e: e.memset(FUb[64:65, S:2 * S], 1.0)))
            wait("tensor", "vself", gfbe[-1])
            woe = []
            for tc in range(NTC):
                bank = (tc % 2) * 1024
                if tc >= 2:
                    wait("tensor", *woe[tc - 2])
                for half in range(2):
                    op("tensor", lambda e, tc=tc, h=half, b=bank:
                       e.matmul(psA[:, b + h * 512:b + h * 512 + 512],
                                FUb[0:65, S + tc * 128:S + tc * 128 + 128],
                                WB[0:65, 2048 + h * 512:2048 + h * 512 + 512],
                                start=True, stop=True),
                       "t_wout" if half == 1 else None)
                wait("scalar" if tc % 2 == 0 else "vector",
                     "t_wout", cnt["t_wout"])
                woe.append(xcopy(tc, UA[:, tc * 1024:tc * 1024 + 1024],
                                 psA[:, bank:bank + 1024]))
                gen_stats_bn(UA, tc, woe[tc])
            stm = gen_stats_agg()
            awaits("vself", stm)
            resid = []
            for tc in range(NTC):
                sl = (tc % 2) * D
                warm(3)
                if tc >= 2:
                    eng = "scalar" if tc % 2 == 0 else "vector"
                    wait(eng, "vself", resid[tc - 2])
                mks, mk = xapply(tc, xntm[:, sl:sl + D],
                                 UA[:, tc * D:tc * D + D],
                                 agt[:, 64 + tc:65 + tc],
                                 agt[:, 80 + tc:81 + tc])
                vwait(mks, mk)
                resid.append(vop(lambda e, tc=tc, sl=sl: e.tensor_tensor(
                    X[:, tc * D:tc * D + D], X[:, tc * D:tc * D + D],
                    xntm[:, sl:sl + D], ALU.add)))

            for tc in range(NTC):
                wait("sync", "vself", resid[tc])
                dma("sync", lambda e, s=s, tc=tc: e.dma_start(
                    y_h.ap()[s, tc * 128:tc * 128 + 128, :],
                    X[:, tc * D:tc * D + D]), "y%d" % (tc // 4))
            ysent += NTC

        for q in range(4):
            wait("sync", "y%d" % q, cnt["y%d" % q])

        blk = st.enter_context(nc.Block())

        @blk.tensor
        def _(e):
            for f in prog["tensor"]:
                f(e)

        @blk.vector
        def _(e):
            for f in prog["vector"]:
                f(e)

        @blk.scalar
        def _(e):
            for f in prog["scalar"]:
                f(e)

        @blk.sync
        def _(e):
            for f in prog["sync"]:
                f(e)

        @blk.gpsimd
        def _(e):
            for f in prog["gpsimd"]:
                f(e)

    return nc


def kernel(**inputs):
    g = lambda k: np.asarray(inputs[k])
    x = g("L0_attn_x")
    sq = float(np.asarray(inputs["L0_attn_s_q"]))
    sk = float(np.asarray(inputs["L0_attn_s_k"]))
    sv = float(np.asarray(inputs["L0_attn_s_v"]))
    so = float(np.asarray(inputs["L0_attn_s_o"]))

    def wlayout(w):
        wT = np.ascontiguousarray(w.T)
        return np.ascontiguousarray(
            wT.reshape(NFC, 128, NHP, 128).transpose(2, 1, 0, 3))

    wq = wlayout(g("L0_attn_w_q"))
    wk = wlayout(g("L0_attn_w_k"))
    wv = wlayout(g("L0_attn_w_v"))
    wo = np.ascontiguousarray(g("L0_attn_w_o").T)
    bq = np.ascontiguousarray(g("L0_attn_b_q").reshape(NHP, 128).T)
    bk = np.ascontiguousarray(g("L0_attn_b_k").reshape(NHP, 128).T)
    bv = np.ascontiguousarray(g("L0_attn_b_v").reshape(NHP, 128).T)
    bo = g("L0_attn_b_o").reshape(1, D)
    Win = g("L0_ffn_Win")
    win = np.ascontiguousarray(Win.reshape(NFC, 128, 64).transpose(1, 0, 2))
    W2 = g("L0_ffn_W2")
    w2bd_full = np.zeros((D, 64), np.float32)
    for m in range(64):
        w2bd_full[m * 16:(m + 1) * 16, m] = W2[m]
    w2bd = np.ascontiguousarray(
        w2bd_full.reshape(NFC, 128, 64).transpose(1, 0, 2))
    em = np.zeros((64, D), np.float32)
    for m in range(64):
        em[m, m * 16:(m + 1) * 16] = 1.0
    em = em.astype(BF)
    wout = np.ascontiguousarray(g("L0_ffn_Wout"))
    bout = g("L0_ffn_bout").reshape(1, D)
    binv = g("L0_ffn_bin_").reshape(64, 1)
    b2v = g("L0_ffn_b2").reshape(64, 1)
    w1il = np.ascontiguousarray(g("L0_ffn_W1").reshape(NFC, 128).T)
    b1il = np.ascontiguousarray(g("L0_ffn_b1").reshape(NFC, 128).T)
    ident = np.eye(128, dtype=np.float32).astype(BF)
    ones1 = np.ones((1, 128), np.float32).astype(BF)

    nc = build_program(sq, sk, sv, so)
    shared = dict(wq=wq, wk=wk, wv=wv, wo=wo, bq=bq, bk=bk, bv=bv, bo=bo,
                  win=win, w2bd=w2bd, em=em, wout=wout, bout=bout, binv=binv,
                  b2v=b2v, w1il=w1il, b1il=b1il, ident=ident, ones1=ones1)
    in_maps = []
    for c in range(8):
        m = dict(shared)
        m["x"] = np.ascontiguousarray(x[2 * c:2 * c + 2])
        in_maps.append(m)
    res = run_bass_kernel_spmd(nc, in_maps, core_ids=list(range(8)))
    out = np.empty((16, S, D), np.float32)
    for c in range(8):
        out[2 * c:2 * c + 2] = res.results[c]["y"]
    return out
